# version 11
# baseline (speedup 1.0000x reference)
import numpy as np
import ml_dtypes
from contextlib import ExitStack

import concourse.bass as bass
import concourse.mybir as mybir
from concourse.bass_utils import run_bass_kernel_spmd

F32 = mybir.dt.float32
BF16 = mybir.dt.bfloat16
AF = mybir.ActivationFunctionType
ALU = mybir.AluOpType
bfloat16 = ml_dtypes.bfloat16

D_MODEL = 2048
HEADDIM = 64
CH = 128
D_INNER = 4096
NHEADS = 64
NPAIR = 32
CONV_DIM = 4352
D_IN_PROJ = 8512
NT_Z = 32
NT_XBC = 34
NT_IN = 67
EPS = 1e-5
NEG_BIG = -1e9
HHALF = 16

ENGS = ("tensor", "vector", "scalar", "gpsimd", "sync")


def rep_mid(ap2d, reps):
    return ap2d.unsqueeze(1).to_broadcast((ap2d.shape[0], reps, ap2d.shape[1]))


def rep_last(ap2d, reps):
    return ap2d.to_broadcast((ap2d.shape[0], ap2d.shape[1], reps))


def slices(total, step):
    out = []
    o = 0
    while o < total:
        out.append((o, min(total, o + step)))
        o += step
    return out


class Prog:
    def __init__(self, nc):
        self.nc = nc
        self.es = ExitStack()
        self.sems = {}
        self.count = {}
        self.watermark = {}
        self.last_write = {}
        self.readers = {}
        self.ops = []
        for e in ENGS:
            self._new_sem(e)

    def _new_sem(self, name):
        self.sems[name] = self.es.enter_context(self.nc.semaphore("sem_" + name))
        self.count[name] = 0

    def dma_group(self, name):
        if name not in self.sems:
            self._new_sem(name)
        return name

    def _need_waits(self, eng, reads, writes):
        need = {}

        def add(sem, val):
            if val > need.get(sem, 0):
                need[sem] = val

        for r in reads:
            lw = self.last_write.get(r)
            if lw:
                add(*lw)
        for w in writes:
            lw = self.last_write.get(w)
            if lw:
                add(*lw)
            for sem, val in self.readers.get(w, {}).items():
                add(sem, val)
        out = []
        for sem, val in need.items():
            if sem == eng == "tensor":
                continue
            if sem not in ENGS:
                val = self.count[sem]
            wm = self.watermark.get((eng, sem), 0)
            if val > wm:
                out.append((sem, val))
                self.watermark[(eng, sem)] = val
        return out

    def op(self, eng, emit_fn, reads=(), writes=(), tag=None):
        waits = self._need_waits(eng, reads, writes)
        self.count[eng] += 1
        v = self.count[eng]
        self.ops.append((eng, waits, emit_fn, eng, 1, tag))
        for w in writes:
            self.last_write[w] = (eng, v)
            self.readers[w] = {}
        for r in reads:
            self.readers.setdefault(r, {})[eng] = v

    def _dma_common(self, q, group, emit, reads, writes, tag):
        g = self.dma_group(group)
        waits = self._need_waits(q, reads, writes)
        self.count[g] += 16
        v = self.count[g]
        self.ops.append((q, waits, emit, g, 16, tag))
        for w in writes:
            self.last_write[w] = (g, v)
            self.readers[w] = {}
        for r in reads:
            self.readers.setdefault(r, {})[g] = v

    def dma(self, q, group, out, in_, reads=(), writes=(), tag=None, **kw):
        def emit(e, out=out, in_=in_, kw=kw):
            return e.dma_start(out=out, in_=in_, **kw)
        self._dma_common(q, group, emit, reads, writes, tag)

    def dma_tr(self, q, group, out, in_, reads=(), writes=(), tag=None):
        def emit(e, out=out, in_=in_):
            return e.dma_start_transpose(out=out, in_=in_)
        self._dma_common(q, group, emit, reads, writes, tag)

    def collective(self, emit_fn, reads=(), writes=(), tag="cc"):
        self._dma_common("gpsimd", "cc", emit_fn, reads, writes, tag)

    def nop_marker(self, eng, reads, writes):
        self.op(eng, lambda e: e.engine_nop(), reads=reads, writes=writes)

    def barrier(self):
        snap = dict(self.count)
        for e in ENGS:
            waits = []
            for sem, val in snap.items():
                if sem == e or val <= 0:
                    continue
                wm = self.watermark.get((e, sem), 0)
                if val > wm:
                    waits.append((sem, val))
                    self.watermark[(e, sem)] = val
            if waits:
                self.ops.append((e, waits, None, None, 0, "barrier"))

    def fence(self, eng, keys):
        waits = self._need_waits(eng, keys, ())
        if waits:
            self.ops.append((eng, waits, None, None, 0, "fence"))

    def build(self):
        per_eng = {}
        for rec in self.ops:
            per_eng.setdefault(rec[0], []).append(rec)
        block = self.es.enter_context(self.nc.Block())
        for eng_name in ENGS:
            recs = per_eng.get(eng_name, [])
            if not recs:
                continue
            section = getattr(block, eng_name)

            def body(engine, recs=recs):
                for (_e, waits, emit_fn, inc_sem, inc_val, _tag) in recs:
                    for sem, val in waits:
                        engine.wait_ge(self.sems[sem], val)
                    if emit_fn is not None:
                        inst = emit_fn(engine)
                        if inc_sem is not None:
                            inst.then_inc(self.sems[inc_sem], inc_val)

            section(body)



def host_layouts(inputs, T=2048):
    f32 = np.float32
    x = np.ascontiguousarray(np.asarray(inputs["L0_attn_x"], f32))
    w_in = np.asarray(inputs["L0_attn_w_in"], f32)
    conv_w = np.asarray(inputs["L0_attn_conv_w"], f32)
    conv_b = np.asarray(inputs["L0_attn_conv_b"], f32)
    dt_bias = np.asarray(inputs["L0_attn_dt_bias"], f32)
    A_log = np.asarray(inputs["L0_attn_A_log"], f32)
    Dv = np.asarray(inputs["L0_attn_D"], f32)
    norm_w = np.asarray(inputs["L0_attn_norm_w"], f32)
    w_out = np.asarray(inputs["L0_attn_w_out"], f32)
    gk = np.asarray(inputs["L0_ffn_gate_kernel"], f32)
    wi0 = np.asarray(inputs["L0_ffn_wi_0"], f32)
    wi1 = np.asarray(inputs["L0_ffn_wi_1"], f32)
    wo = np.asarray(inputs["L0_ffn_wo"], f32)

    w_in_p = np.zeros((2048, NT_IN * 128), f32)
    w_in_p[:, :D_IN_PROJ] = w_in
    w_in_r = np.ascontiguousarray(
        w_in_p.reshape(16, 128, NT_IN, 128).transpose(2, 1, 0, 3))
    w_out_r = np.ascontiguousarray(
        w_out.reshape(32, 128, 16, 128).transpose(2, 1, 0, 3))
    wi0_r = np.ascontiguousarray(
        wi0.reshape(8, 16, 128, 16, 128).transpose(0, 3, 2, 1, 4))
    wi1_r = np.ascontiguousarray(
        wi1.reshape(8, 16, 128, 16, 128).transpose(0, 3, 2, 1, 4))
    wo_r = np.ascontiguousarray(
        wo.reshape(8, 16, 128, 16, 128).transpose(0, 3, 2, 1, 4))
    gk_r = np.ascontiguousarray(gk.reshape(16, 128, 8).transpose(1, 0, 2))
    conv_w_r = np.ascontiguousarray(conv_w.reshape(NT_XBC, 128, 4).transpose(1, 0, 2))
    conv_b_r = np.ascontiguousarray(conv_b.reshape(NT_XBC, 128).transpose(1, 0))
    D_cols = np.ascontiguousarray(
        Dv.reshape(NPAIR, 2)[:, :, None].repeat(HEADDIM, 2)
        .transpose(1, 2, 0).reshape(128, NPAIR))
    normw_cols = np.ascontiguousarray(norm_w.reshape(NT_Z, 128).transpose(1, 0))
    dtb = np.ascontiguousarray(dt_bias.reshape(NHEADS, 1))
    alog = np.ascontiguousarray(A_log.reshape(NHEADS, 1))

    li = np.arange(CH)
    shared = dict(
        w_in_r=w_in_r, w_out_r=w_out_r, wi0_r=wi0_r, wi1_r=wi1_r, wo_r=wo_r,
        gk_r=gk_r, conv_w_r=conv_w_r, conv_b_r=conv_b_r, D_cols=D_cols,
        normw_cols=normw_cols, dt_bias_c=dtb, A_log_c=alog,
        ident_bf=np.eye(128, dtype=bfloat16),
        ident_f32=np.eye(128, dtype=f32),
        ones_row_f32=np.ones((1, 128), f32),
        ones_col_f32=np.ones((128, 1), f32),
        masktri=np.where(li[None, :] >= li[:, None], 0.0, NEG_BIG).astype(f32),
        misc_cols=np.tile(np.array([[EPS, -1.0, 1.0 / D_INNER, -0.5]], f32), (128, 1)),
    )

    in_maps = []
    for core in range(8):
        b, half = core // 2, core % 2
        x_tm = np.ascontiguousarray(x[b, half * T:(half + 1) * T])
        x_fm = np.ascontiguousarray(x_tm.T)
        x_ctx = np.zeros((128, 2048), f32)
        x_ptm = np.zeros((T, 2048), f32)
        if half == 1:
            x_ctx[0:3] = x[b, T - 3:T]
            x_ptm[:] = x[b, 0:T]
        m = dict(shared)
        m.update(x_tm=x_tm, x_fm=x_fm, x_ctx=x_ctx, x_ptm=x_ptm,
                 mask_first=np.full((128, 1), 0.0 if half else 1.0, f32),
                 mask_second=np.full((128, 1), 1.0 if half else 0.0, f32))
        in_maps.append(m)
    return in_maps



def build_program(T=2048, debug_taps=()):
    nc = bass.Bass("TRN2", target_bir_lowering=False)
    NCH = T // CH
    NST = T // 128
    LG = min(1024, T)
    NLG = T // LG
    HALF_T = T // 2
    p = Prog(nc)
    es = p.es

    def din(name, shape, dtype=F32):
        return nc.dram_tensor(name, list(shape), dtype, kind="ExternalInput").ap()

    def dint(name, shape, dtype):
        kind = "ExternalOutput" if name in debug_taps else "Internal"
        return nc.dram_tensor(name, list(shape), dtype, kind=kind).ap()

    x_tm = din("x_tm", (T, 2048))
    x_fm = din("x_fm", (2048, T))
    x_ctx = din("x_ctx", (128, 2048))
    x_ptm = din("x_ptm", (T, 2048))
    w_in_r = din("w_in_r", (NT_IN, 128, 16, 128))
    w_out_r = din("w_out_r", (16, 128, 32, 128))
    wi0_r = din("wi0_r", (8, 16, 128, 16, 128))
    wi1_r = din("wi1_r", (8, 16, 128, 16, 128))
    wo_r = din("wo_r", (8, 16, 128, 16, 128))
    gk_r = din("gk_r", (128, 16, 8))
    conv_w_d = din("conv_w_r", (128, NT_XBC, 4))
    conv_b_d = din("conv_b_r", (128, NT_XBC))
    D_cols_d = din("D_cols", (128, NPAIR))
    normw_d = din("normw_cols", (128, NT_Z))
    dtb_d = din("dt_bias_c", (NHEADS, 1))
    alog_d = din("A_log_c", (NHEADS, 1))
    ident_bf_d = din("ident_bf", (128, 128), BF16)
    ident_f32_d = din("ident_f32", (128, 128))
    ones_row_f32_d = din("ones_row_f32", (1, 128))
    ones_col_f32_d = din("ones_col_f32", (128, 1))
    masktri_d = din("masktri", (CH, CH))
    mask1_d = din("mask_first", (128, 1))
    mask2_d = din("mask_second", (128, 1))
    misc_d = din("misc_cols", (128, 4))

    out_d = nc.dram_tensor("out_fm", [16, 128, T], F32, kind="ExternalOutput").ap()

    ln1_d = dint("ln1_d", (T + 128, 2048), BF16)
    z_d = dint("z_d", (D_INNER, T), BF16)
    xbc_d = dint("xbc_d", (CONV_DIM, T), BF16)
    ln1p_d = dint("ln1p_d", (T, 2048), BF16)
    xbcp_d = dint("xbcp_d", (CONV_DIM, T), BF16)
    xh_d = dint("xh_d", (D_INNER, CH), BF16)
    acs_d = dint("acs_d", (NHEADS, CH), F32)
    ea_d = dint("ea_d", (NCH, NHEADS, CH), BF16)
    sc_d = dint("sc_d", (NHEADS, CH), BF16)
    y_d = dint("y_d", (NCH, 128, NT_Z * CH), BF16)
    x1_d = dint("x1_d", (16, 128, T), F32)
    cw_d = dint("cw_d", (8, T), F32)

    def sbuf(stack, name, shape, dtype, side=None):
        return stack.enter_context(nc.sbuf_tensor(name, list(shape), dtype, side=side))

    conv_w_sb = sbuf(es, "conv_w_sb", (128, NT_XBC * 4), F32)
    conv_b_sb = sbuf(es, "conv_b_sb", (128, NT_XBC), F32)
    D_cols_sb = sbuf(es, "D_cols_sb", (128, NPAIR), F32)
    normw_sb = sbuf(es, "normw_sb", (128, NT_Z), F32)
    dtb_sb = sbuf(es, "dtb_sb", (NHEADS, 1), F32)
    alog_sb = sbuf(es, "alog_sb", (NHEADS, 1), F32)
    expalog_sb = sbuf(es, "expalog_sb", (NHEADS, 1), F32)
    ident_sb = sbuf(es, "ident_sb", (128, 128), BF16)
    identf_sb = sbuf(es, "identf_sb", (128, 128), F32)
    ones_row_sb = sbuf(es, "ones_row_sb", (1, 128), F32)
    ones_col_sb = sbuf(es, "ones_col_sb", (128, 1), F32)
    masktri_sb = sbuf(es, "masktri_sb", (CH, CH), F32)
    mask1_sb = sbuf(es, "mask1_sb", (128, 1), F32)
    mask2_sb = sbuf(es, "mask2_sb", (128, 1), F32)
    misc_sb = sbuf(es, "misc_sb", (128, 4), F32)
    ms1 = ExitStack()
    dt_sb = sbuf(ms1, "dt_sb", (NHEADS, T), F32, side="right")
    ctx3_sb = sbuf(ms1, "ctx3_sb", (128, NT_XBC * 3), F32, side="right")
    xc = [sbuf(ms1, f"xc{i}", (128, NT_XBC * (CH + 3)), BF16, side="right") for i in range(2)]
    msp = ExitStack()
    dtp_sb = sbuf(msp, "dtp_sb", (NHEADS, T), F32, side="right")
    ms2 = ExitStack()
    S0b_sb = sbuf(ms2, "S0b_sb", (128, NHEADS * HEADDIM), BF16)
    Cfull_sb = sbuf(ms2, "Cfull_sb", (128, NCH * CH), BF16)
    Alast_sb = sbuf(ms2, "Alast_sb", (NHEADS, NCH), F32)
    cumprev_sb = sbuf(ms2, "cumprev_sb", (NHEADS, NCH), F32)

    KC = ("consts",)
    for dst, src in [
        (conv_w_sb, conv_w_d.rearrange("p a b -> p (a b)")), (conv_b_sb, conv_b_d),
        (D_cols_sb, D_cols_d), (normw_sb, normw_d), (dtb_sb, dtb_d),
        (alog_sb, alog_d), (ident_sb, ident_bf_d), (identf_sb, ident_f32_d),
        (ones_row_sb, ones_row_f32_d), (ones_col_sb, ones_col_f32_d),
        (masktri_sb, masktri_d), (mask1_sb, mask1_d), (mask2_sb, mask2_d),
        (misc_sb, misc_d),
    ]:
        p.dma("sync", "consts", dst[:], src, writes=[KC])
    p.op("scalar", lambda e: e.activation(expalog_sb[:], alog_sb[:], AF.Exp),
         reads=[KC], writes=[KC])

    pha1 = ExitStack()
    _x1b = sbuf(pha1, "a1_x0", (128, 2048), F32, side="right")
    xst = [_x1b, _x1b]
    _l1b = sbuf(pha1, "a1_ln0", (128, 2048), BF16, side="right")
    lnb = [_l1b, _l1b]
    st6 = sbuf(pha1, "a1_st6", (128, 24), F32, side="right")
    mv_sb = sbuf(pha1, "a1_mv", (128, 2), F32, side="right")
    rstd_sb = sbuf(pha1, "a1_rstd", (128, 1), F32, side="right")
    mr_sb = sbuf(pha1, "a1_mr", (128, 1), F32, side="right")

    order = list(range(NST + 1, 2 * NST + 1)) + list(range(NST + 1))
    for st in order:
        par = st % 2
        xb, lb = xst[par], lnb[par]
        kx, kl = ("a1x",), ("a1l",)
        if st < NST:
            src = x_tm[st * 128:(st + 1) * 128, :]
        elif st == NST:
            src = x_ctx[:, :]
        else:
            sp = st - NST - 1
            src = x_ptm[sp * 128:(sp + 1) * 128, :]
        p.dma("scalar", "a1_in", xb[:], src, writes=[kx])
        for q in range(4):
            p.op("vector", lambda e, xb=xb, q=q: e.bn_stats(
                out=st6[:, q * 6:(q + 1) * 6], in_=xb[:, q * 512:(q + 1) * 512]),
                reads=[kx], writes=[("a1st6",)])
        p.op("vector", lambda e: e.bn_aggr(out=mv_sb[:], in_=st6[:]),
             reads=[("a1st6",)], writes=[("a1mv",)])
        p.op("scalar", lambda e: e.activation(rstd_sb[:], mv_sb[:, 1:2], AF.Sqrt, bias=misc_sb[:, 0:1]),
             reads=[("a1mv",), KC], writes=[("a1rstd",)])
        p.op("vector", lambda e: e.reciprocal(rstd_sb[:], rstd_sb[:]),
             reads=[("a1rstd",)], writes=[("a1rstd",)])
        p.op("vector", lambda e: e.tensor_tensor(mr_sb[:], mv_sb[:, 0:1], rstd_sb[:], op=ALU.mult),
             reads=[("a1mv",), ("a1rstd",)], writes=[("a1mr",)])
        p.op("vector", lambda e, xb=xb, lb=lb: e.tensor_scalar(
            lb[:], xb[:], rstd_sb[:], mr_sb[:], op0=ALU.mult, op1=ALU.subtract),
            reads=[kx, ("a1rstd",), ("a1mr",)], writes=[kl])
        if st <= NST:
            p.dma("scalar", "a1_out", ln1_d[st * 128:(st + 1) * 128, :], lb[:],
                  reads=[kl], writes=[("d_ln1",)])
        else:
            sp = st - NST - 1
            p.dma("scalar", "a1_out", ln1p_d[sp * 128:(sp + 1) * 128, :], lb[:],
                  reads=[kl], writes=[("d_ln1p",)])

    c1s = ExitStack()
    convo_s = sbuf(c1s, "convo", (128, NT_XBC * CH), F32)
    convo2 = [convo_s, convo_s]
    _xbcs = sbuf(c1s, "xbcs0", (128, NT_XBC * CH), BF16)
    xbcs2 = [_xbcs, _xbcs]
    dtc2 = [sbuf(c1s, f"dtc{i}", (NHEADS, CH), F32) for i in range(2)]
    acs2 = [sbuf(c1s, f"acs{i}", (NHEADS, CH), F32) for i in range(2)]
    nacs2 = [sbuf(c1s, f"nacs{i}", (NHEADS, CH), F32) for i in range(2)]
    _expa = sbuf(c1s, "expa0", (NHEADS, CH), BF16)
    expa2 = [_expa, _expa]
    decay2 = [sbuf(c1s, f"decay{i}", (NHEADS, CH), F32) for i in range(2)]
    ddec2 = [sbuf(c1s, f"ddec{i}", (NHEADS, CH), F32) for i in range(2)]
    _t64a = sbuf(c1s, "t64a0", (NHEADS, CH), F32)
    t64a2 = [_t64a, _t64a]
    _t64b = sbuf(c1s, "t64b0", (NHEADS, CH), F32)
    t64b2 = [_t64b, _t64b]
    _eacol = sbuf(c1s, "eacol0", (NHEADS, 1), F32)
    eacol2 = [_eacol, _eacol]
    dtT2 = [sbuf(c1s, f"dtT{i}", (128, NHEADS), F32) for i in range(2)]
    ddecT2 = [sbuf(c1s, f"ddecT{i}", (128, NHEADS), F32) for i in range(2)]
    nacsT2 = [sbuf(c1s, f"nacsT{i}", (128, NHEADS), F32) for i in range(2)]
    EAb2 = [sbuf(c1s, f"EAb{i}", (128, NHEADS), F32) for i in range(2)]
    _earow = sbuf(c1s, "earow0", (1, NHEADS), F32)
    earow2 = [_earow, _earow]
    Btm = sbuf(c1s, "Btm", (128, CH), BF16)
    Gsb = sbuf(c1s, "Gsb", (128, CH), BF16)
    t128 = sbuf(c1s, "t128", (128, CH), F32)
    Xdec = sbuf(c1s, "Xdec", (128, D_INNER), BF16)
    _xhg = sbuf(c1s, "xhg0", (128, 1024), BF16)
    xhg2 = [_xhg, _xhg]
    S_sb = sbuf(c1s, "S_sb", (128, NHEADS * HEADDIM), F32)
    psS = [c1s.enter_context(nc.psum_tensor(f"psS{i}", [128, 512], F32)) for i in range(2)]
    psM = c1s.enter_context(nc.psum_tensor("psM", [128, 512], F32))
    psMb = c1s.enter_context(nc.psum_tensor("psMb", [128, 1024], BF16))
    KM, KMB = ("psM",), ("psMb",)
    CHW = CH + 3

    a2s = ExitStack()
    TC = T + 128
    ln1f = sbuf(a2s, "ln1f", (128, 16 * T), BF16, side="right")
    lnctxf = sbuf(a2s, "lnctxf", (128, 16 * 128), BF16, side="right")
    w2 = [sbuf(a2s, f"a2_w{i}", (128, 16 * 128), BF16, side="right") for i in range(2)]
    _evb = sbuf(a2s, "a2_eb0", (128, T), BF16, side="right")
    ev_b = [_evb, _evb]
    ctx_ev = sbuf(a2s, "a2_ctxev", (128, 128), F32, side="right")
    ps_a2 = [a2s.enter_context(nc.psum_tensor(f"psA{i}", [128, 512], F32))
             for i in range(4)]
    TSL = slices(T, 512)
    KLN1F = [("ln1f", kt) for kt in range(16)]

    def emit_a2_tile(nt, need_ctx, xbc_dst, kxbc_dst, dt_dst, kdt_dst, z_out):
        if True:
            par = nt % 2
            wb = w2[par]
            kw = ("a2w", par)
            p.dma("gpsimd", f"a2_w{par}", wb[:], w_in_r[nt].rearrange("d k n -> d (k n)"),
                  writes=[kw], tag=f"w_in {nt}")
            pset = [ps_a2[i] for i in range(len(TSL))]
            pk = [("psA", i) for i in range(len(TSL))]
            for kt in range(16):
                for s, (n0, n1) in enumerate(TSL):
                    p.op("tensor",
                         lambda e, wb=wb, kt=kt, n0=n0, n1=n1, ps=pset[s]:
                         e.matmul(ps[:, 0:n1 - n0], wb[:, kt * 128:(kt + 1) * 128],
                                  ln1f[:, kt * T + n0:kt * T + n1],
                                  start=(kt == 0), stop=(kt == 15)),
                         reads=[kw, KLN1F[kt]], writes=[pk[s]])
            is_z = nt < NT_Z
            is_dt = nt >= NT_Z + NT_XBC
            eb = ev_b[par]
            keb = ("a2eb",)
            for s, (n0, n1) in enumerate(TSL):
                if is_dt:
                    p.op("vector", lambda e, ps=pset[s], n0=n0, n1=n1, dt_dst=dt_dst:
                         e.tensor_copy(dt_dst[:, n0:n1], ps[0:64, 0:n1 - n0]),
                         reads=[pk[s]], writes=[kdt_dst])
                elif s % 2 == 0:
                    p.op("vector", lambda e, ps=pset[s], n0=n0, n1=n1, eb=eb:
                         e.tensor_copy(eb[:, n0:n1], ps[:, 0:n1 - n0]),
                         reads=[pk[s]], writes=[keb])
                else:
                    p.op("scalar", lambda e, ps=pset[s], n0=n0, n1=n1, eb=eb:
                         e.copy(eb[:, n0:n1], ps[:, 0:n1 - n0]),
                         reads=[pk[s]], writes=[keb])
            if is_z:
                p.dma("scalar", "a2_zout", z_out[nt * 128:(nt + 1) * 128, :], eb[:, 0:T],
                      reads=[keb], writes=[("d_z",)])
            elif not is_dt:
                j = nt - NT_Z
                p.dma("scalar", "a2_xout", xbc_dst[j * 128:(j + 1) * 128, :], eb[:, 0:T],
                      reads=[keb], writes=[kxbc_dst])
                if need_ctx:
                    pc, kpc = pset[0], pk[0]
                    for kt in range(16):
                        p.op("tensor", lambda e, wb=wb, kt=kt, pc=pc:
                             e.matmul(pc[:, 0:128], wb[:, kt * 128:(kt + 1) * 128],
                                      lnctxf[:, kt * 128:(kt + 1) * 128],
                                      start=(kt == 0), stop=(kt == 15)),
                             reads=[kw, KLN1F[kt]], writes=[kpc])
                    p.op("vector", lambda e, pc=pc: e.tensor_copy(ctx_ev[:], pc[:, 0:128]),
                         reads=[kpc], writes=[("a2ctx",)])
                    p.op("vector", lambda e, j=j: e.tensor_copy(
                        ctx3_sb[:, j * 3:(j + 1) * 3], ctx_ev[:, 0:3]),
                        reads=[("a2ctx",)], writes=[("ctx3",)])

    def emit_chunk(c, full, xbcd, kxbcd, dtbuf, kdtbuf):
        par = c % 2
        xcb = xc[par]
        kxc = ("xc", par)
        convo, xbcs = convo2[par], xbcs2[par]
        dtc, acs, nacs, expa = dtc2[par], acs2[par], nacs2[par], expa2[par]
        decay, ddec, t64a, t64b = decay2[par], ddec2[par], t64a2[par], t64b2[par]
        eacol, dtT, ddecT, nacsT = eacol2[par], dtT2[par], ddecT2[par], nacsT2[par]
        EAb, earow = EAb2[par], earow2[par]
        p.dma("sync", f"c1_xbc{par}",
              xcb[:].rearrange("p (nt w) -> p nt w", w=CHW)[:, :, 3:3 + CH],
              xbcd.rearrange("(nt p) t -> p nt t", p=128)[:, :, c * CH:(c + 1) * CH],
              reads=[kxbcd], writes=[kxc])
        for nt in range(NT_XBC):
            o = nt * CHW
            co = nt * CH
            if nt % 2 == 0:
                p.op("vector", lambda e, o=o, co=co, nt=nt, xcb=xcb: e.tensor_scalar(
                    convo[:, co:co + CH], xcb[:, o:o + CH],
                    conv_w_sb[:, 4 * nt:4 * nt + 1], conv_b_sb[:, nt:nt + 1],
                    op0=ALU.mult, op1=ALU.add),
                    reads=[kxc, KC], writes=[("convo", nt)])
                for tap in range(1, 4):
                    p.op("vector", lambda e, o=o, co=co, nt=nt, tap=tap, xcb=xcb:
                         e.scalar_tensor_tensor(
                             convo[:, co:co + CH], xcb[:, o + tap:o + tap + CH],
                             conv_w_sb[:, 4 * nt + tap:4 * nt + tap + 1],
                             convo[:, co:co + CH], op0=ALU.mult, op1=ALU.add),
                         reads=[kxc, KC, ("convo", nt)], writes=[("convo", nt)])
            else:
                for tap in range(4):
                    wv = conv_w_sb[:, 4 * nt + tap:4 * nt + tap + 1].to_broadcast((128, 1, CH))
                    if tap == 0:
                        p.op("gpsimd", lambda e, o=o, co=co, wv=wv, xcb=xcb: e.tensor_tensor(
                            convo[:, co:co + CH], xcb[:, o:o + CH], wv, op=ALU.mult),
                            reads=[kxc, KC], writes=[("convo", nt)])
                    else:
                        p.op("gpsimd", lambda e, o=o, tap=tap, wv=wv, xcb=xcb:
                             e.tensor_tensor(t128[:, 0:CH], xcb[:, o + tap:o + tap + CH],
                                             wv, op=ALU.mult),
                             reads=[kxc, KC], writes=[("t128",)])
                        p.op("gpsimd", lambda e, co=co, nt=nt: e.tensor_tensor(
                            convo[:, co:co + CH], convo[:, co:co + CH], t128[:, 0:CH],
                            op=ALU.add),
                            reads=[("t128",), ("convo", nt)], writes=[("convo", nt)])
                p.op("gpsimd", lambda e, co=co, nt=nt: e.tensor_tensor(
                    convo[:, co:co + CH], convo[:, co:co + CH],
                    conv_b_sb[:, nt:nt + 1].to_broadcast((128, 1, CH)), op=ALU.add),
                    reads=[KC, ("convo", nt)], writes=[("convo", nt)])
            eng = "vector" if nt % 2 == 0 else "gpsimd"
            p.op(eng, lambda e, o=o, opar=1 - par, xcb=xcb: e.tensor_copy(
                xc[opar][:, o:o + 3], xcb[:, o + CH:o + CH + 3]),
                reads=[kxc], writes=[("xc", 1 - par)])
        KCV = [("convo", nt) for nt in range(NT_XBC)]
        if full:
            for nt in range(0, NT_XBC, 6):
                p.op("tensor", lambda e: e.matmul(
                    psM[:, 448:448 + 64], ones_row_sb[:], earow[:],
                    start=True, stop=True),
                    reads=[("convo", nt), ("earow",), KC], writes=[KM])
        p.op("scalar", lambda e: e.activation(xbcs[:], convo[:], AF.Silu),
             reads=KCV, writes=[("xbcs",)])
        B_nm = xbcs[:, NT_Z * CH:(NT_Z + 1) * CH]
        C_nm = xbcs[:, (NT_Z + 1) * CH:(NT_Z + 2) * CH]
        if full:
            p.op("vector", lambda e, c=c, C_nm=C_nm: e.tensor_copy(
                Cfull_sb[:, c * CH:(c + 1) * CH], C_nm),
                reads=[("xbcs",)], writes=[("Cfull",)])

        p.op("vector", lambda e, c=c, dtbuf=dtbuf: e.tensor_scalar(
            dtc[:], dtbuf[:, c * CH:(c + 1) * CH], dtb_sb[:], None, op0=ALU.add),
            reads=[kdtbuf, KC], writes=[("dtc", par)])
        p.op("scalar", lambda e: e.activation(t64a[:], dtc[:], AF.Abs),
             reads=[("dtc", par)], writes=[("t64a",)])
        p.op("scalar", lambda e: e.activation(t64a[:], t64a[:], AF.Exp, scale=misc_sb[0:NHEADS, 1:2]),
             reads=[("t64a",), KC], writes=[("t64a",)])
        p.op("scalar", lambda e: e.activation(t64b[:], t64a[:], AF.Ln, bias=1.0),
             reads=[("t64a",)], writes=[("t64b",)])
        p.op("vector", lambda e: e.tensor_scalar_max(dtc[:], dtc[:], 0.0),
             reads=[("dtc", par), ("t64a",)], writes=[("dtc", par)])
        p.op("vector", lambda e: e.tensor_tensor(dtc[:], dtc[:], t64b[:], op=ALU.add),
             reads=[("dtc", par), ("t64b",)], writes=[("dtc", par)])
        p.op("vector", lambda e: e.tensor_scalar(
            t64a[:], dtc[:], expalog_sb[:], None, op0=ALU.mult),
            reads=[("dtc", par), KC], writes=[("t64a",)])
        p.op("vector", lambda e: e.tensor_tensor_scan(
            nacs[:], t64a[:], t64a[:], 0.0, op0=ALU.add, op1=ALU.bypass),
            reads=[("t64a",)], writes=[("nacs", par)])
        p.op("vector", lambda e: e.tensor_scalar(acs[:], nacs[:], -1.0, None, op0=ALU.mult),
             reads=[("nacs", par)], writes=[("acs", par)])
        if full:
            p.dma("scalar", "c1_acs", acs_d[:, :], acs[:], reads=[("acs", par)], writes=[("d_acs",)])
            p.op("scalar", lambda e: e.activation(expa[:], acs[:], AF.Exp),
                 reads=[("acs", par)], writes=[("expa",)])
            p.dma("scalar", "c1_ea", ea_d[c], expa[:], reads=[("expa",)], writes=[("d_ea",)])
            p.op("vector", lambda e, c=c: e.tensor_copy(Alast_sb[:, c:c + 1], acs[:, CH - 1:CH]),
                 reads=[("acs", par)], writes=[("Alast",)])
        p.op("scalar", lambda e: e.activation(decay[:], nacs[:], AF.Exp,
                                              bias=acs[:, CH - 1:CH]),
             reads=[("nacs", par), ("acs", par)], writes=[("decay", par)])
        p.op("vector", lambda e: e.tensor_tensor(ddec[:], dtc[:], decay[:], op=ALU.mult),
             reads=[("dtc", par), ("decay", par)], writes=[("ddec", par)])
        p.op("scalar", lambda e: e.activation(eacol[:], acs[:, CH - 1:CH], AF.Exp),
             reads=[("acs", par)], writes=[("eacol",)])
        p.op("tensor", lambda e: e.transpose(psM[0:1, 384:384 + NHEADS], eacol[:],
                                             identf_sb[0:NHEADS, 0:NHEADS]),
             reads=[("eacol",), KC], writes=[KM])
        p.op("vector", lambda e: e.tensor_copy(earow[:], psM[0:1, 384:384 + NHEADS]),
             reads=[KM], writes=[("earow",)])
        p.op("tensor", lambda e: e.matmul(psM[:, 256:256 + NHEADS], ones_row_sb[:],
                                          earow[:], start=True, stop=True),
             reads=[("earow",), KC], writes=[KM])
        p.op("vector", lambda e: e.tensor_copy(EAb[:], psM[:, 256:256 + NHEADS]),
             reads=[KM], writes=[("EAb", par)])
        trs = [(ddec, ddecT, ("ddec", par), ("ddecT", par))]
        if full:
            trs += [(dtc, dtT, ("dtc", par), ("dtT", par)), (nacs, nacsT, ("nacs", par), ("nacsT", par))]
        for tsrc, tdst, rk, wk in trs:
            p.op("tensor", lambda e, tsrc=tsrc: e.transpose(
                psM[:, 128:128 + NHEADS], tsrc[:], identf_sb[0:NHEADS, 0:NHEADS]),
                reads=[rk, KC], writes=[KM])
            p.op("vector", lambda e, tdst=tdst: e.tensor_copy(tdst[:], psM[:, 128:128 + NHEADS]),
                 reads=[KM], writes=[wk])

        xhg = xhg2[par]
        for grp in range(4):
            for i8 in range(8):
                jx = grp * 8 + i8
                p.op("tensor", lambda e, i8=i8, jx=jx, xbcs=xbcs: e.transpose(
                    psMb[:, i8 * 128:(i8 + 1) * 128], xbcs[:, jx * CH:(jx + 1) * CH],
                    ident_sb[:]),
                    reads=[("xbcs",), KC], writes=[KMB])
            h16 = slice(grp * 16, (grp + 1) * 16)
            p.op("scalar", lambda e, xhg=xhg: e.copy(
                xhg[:, 0:1024], psMb[:, 0:1024]),
                reads=[KMB], writes=[("xhg",)])
            if full:
                p.op("vector", lambda e, grp=grp, h16=h16, dtT=dtT, xhg=xhg: e.tensor_tensor(
                    Xp[:, grp * 1024:(grp + 1) * 1024], xhg[:, 0:1024],
                    rep_last(dtT[:, h16], HEADDIM), op=ALU.mult),
                    reads=[("xhg",), ("dtT", par)], writes=[("Xp",)])
            p.op("vector", lambda e, grp=grp, h16=h16, ddecT=ddecT, xhg=xhg: e.tensor_tensor(
                Xdec[:, grp * 1024:(grp + 1) * 1024], xhg[:, 0:1024],
                rep_last(ddecT[:, h16], HEADDIM), op=ALU.mult),
                reads=[("xhg",), ("ddecT", par)], writes=[("Xdec",)])

        if full:
            p.op("tensor", lambda e, B_nm=B_nm, C_nm=C_nm: e.matmul(
                psM[:, 0:CH], B_nm, C_nm, start=True, stop=True),
                reads=[("xbcs",)], writes=[KM])
            p.op("vector", lambda e: e.tensor_copy(Gsb[:], psM[:, 0:CH]),
                 reads=[KM], writes=[("G",)])
        p.op("tensor", lambda e, B_nm=B_nm: e.transpose(psMb[:, 0:CH], B_nm, ident_sb[:]),
             reads=[("xbcs",), KC], writes=[KMB])
        p.op("vector", lambda e: e.tensor_copy(Btm[:], psMb[:, 0:CH]),
             reads=[KMB], writes=[("Btm",)])

        for hh in range(NHEADS // HHALF):
            h0 = hh * HHALF
            if full:
                Dm = Dm2[hh % 2]
                kdm = ("Dm", hh % 2)
                p.op("vector", lambda e, h0=h0, Dm=Dm: e.tensor_tensor(
                    Dm[:], rep_mid(masktri_sb[:], HHALF),
                    rep_last(nacsT[:, h0:h0 + HHALF], CH), op=ALU.add),
                    reads=[KC, ("nacsT", par)], writes=[kdm])
                p.dma("gpsimd", f"c1_abc{hh % 2}", Dm[:],
                      acs_d.rearrange("h l -> (h l)").unsqueeze(0)[:, h0 * CH:(h0 + HHALF) * CH]
                      .to_broadcast((128, HHALF * CH)),
                      reads=[("d_acs",), kdm], writes=[kdm],
                      accum_op=ALU.add)
                p.dma("sync", "c1_eabc", eabc[:],
                      ea_d[c].rearrange("h l -> (h l)").unsqueeze(0)[:, h0 * CH:(h0 + HHALF) * CH]
                      .to_broadcast((128, HHALF * CH)),
                      reads=[("d_ea",)], writes=[("eabc",)])
                p.op("scalar", lambda e, Dm=Dm: e.activation(Lh[:], Dm[:], AF.Exp),
                     reads=[kdm], writes=[("Lh",)])
                p.op("vector", lambda e: e.tensor_tensor(
                    Mth[:], Lh[:], rep_mid(Gsb[:], HHALF), op=ALU.mult),
                    reads=[("Lh",), ("G",)], writes=[("Mth",)])
                p.op("gpsimd", lambda e, C_nm=C_nm: e.tensor_tensor(
                    Cexp[:], eabc[:], rep_mid(C_nm, HHALF), op=ALU.mult),
                    reads=[("eabc",), ("xbcs",)], writes=[("Cexp",)])

                for jj in range(HHALF // 2):
                    j = hh * (HHALF // 2) + jj
                    bank, kb = psY[jj % 2], ("psY", jj % 2)
                    for u in range(2):
                        h = 2 * j + u
                        lh = h - h0
                        rows = slice(64 * u, 64 * u + 64)
                        p.op("tensor", lambda e, bank=bank, rows=rows, h=h, lh=lh:
                             e.matmul(bank[rows, 0:CH], Xp[:, h * 64:h * 64 + 64],
                                      Mth[:, lh * CH:(lh + 1) * CH], start=True, stop=False),
                             reads=[("Xp",), ("Mth",)], writes=[kb])
                        p.op("tensor", lambda e, bank=bank, rows=rows, h=h, lh=lh:
                             e.matmul(bank[rows, 0:CH], Sb_sb[:, h * 64:h * 64 + 64],
                                      Cexp[:, lh * CH:(lh + 1) * CH], start=False, stop=True),
                             reads=[("Sb",), ("Cexp",)], writes=[kb])
                    p.op("vector", lambda e, j=j, bank=bank: e.scalar_tensor_tensor(
                        ysb[:, j * CH:(j + 1) * CH], xbcs[:, j * CH:(j + 1) * CH],
                        D_cols_sb[:, j:j + 1], bank[:, 0:CH], op0=ALU.mult, op1=ALU.add),
                        reads=[("xbcs",), kb, KC], writes=[("ysb",)])

            for g in range(HHALF // 8):
                gg = hh * (HHALF // 8) + g
                bank, kb = psS[gg % 2], ("psS", gg % 2)
                hbase = h0 + g * 8
                for i in range(8):
                    h = hbase + i
                    p.op("tensor", lambda e, bank=bank, i=i, h=h: e.matmul(
                        bank[:, i * 64:(i + 1) * 64], Btm[:],
                        Xdec[:, h * 64:(h + 1) * 64], start=True, stop=True),
                        reads=[("Btm",), ("Xdec",)], writes=[kb])
                base = hbase * 64
                p.op("vector", lambda e, base=base, hbase=hbase: e.tensor_tensor(
                    S_sb[:, base:base + 512], S_sb[:, base:base + 512],
                    rep_last(EAb[:, hbase:hbase + 8], 64), op=ALU.mult),
                    reads=[("S",), ("EAb", par)], writes=[("S",)])
                p.op("vector", lambda e, base=base, bank=bank: e.tensor_tensor(
                    S_sb[:, base:base + 512], S_sb[:, base:base + 512],
                    bank[:, 0:512], op=ALU.add),
                    reads=[("S",), kb], writes=[("S",)])
        if full:
            p.op("vector", lambda e: e.tensor_copy(Sb_sb[:], S_sb[:]),
                 reads=[("S",)], writes=[("Sb",)])
            p.dma("scalar", "c1_y", y_d[c], ysb[:], reads=[("ysb",)], writes=[("d_y", c)])

    for kt in range(16):
        p.dma_tr("sync", "a2_lnf", ln1f[:, kt * T:kt * T + T],
                 ln1p_d[0:T, kt * 128:(kt + 1) * 128],
                 reads=[("d_ln1p",)], writes=[("ln1f", kt)])
    for nt in range(NT_Z, NT_IN):
        emit_a2_tile(nt, False, xbcp_d, ("d_xbcp",), dtp_sb, ("dtp_sb",), None)

    for kt in range(16):
        p.dma_tr("sync", "a2_lnf", ln1f[:, kt * T:kt * T + T],
                 ln1_d[0:T, kt * 128:(kt + 1) * 128],
                 reads=[("d_ln1",)], writes=[("ln1f", kt)])
        p.dma_tr("sync", "a2_lnf", lnctxf[:, kt * 128:(kt + 1) * 128],
                 ln1_d[T:T + 128, kt * 128:(kt + 1) * 128],
                 reads=[("d_ln1",)], writes=[("ln1f", kt)])

    p.op("vector", lambda e: e.memset(S_sb[:], 0.0), writes=[("S",)])
    p.op("vector", lambda e: e.memset(
        xc[0][:].rearrange("p (nt w) -> p nt w", w=CHW)[:, :, 0:3], 0.0),
        writes=[("xc", 0)])
    ti = 0
    for c in range(NCH):
        quota = (NT_IN * (c + 1) + NCH - 1) // NCH
        while ti < min(quota, NT_IN):
            emit_a2_tile(ti, True, xbc_d, ("d_xbc",), dt_sb, ("dt_sb",), z_d)
            ti += 1
        emit_chunk(c, False, xbcp_d, ("d_xbcp",), dtp_sb, ("dtp_sb",))
    while ti < NT_IN:
        emit_a2_tile(ti, True, xbc_d, ("d_xbc",), dt_sb, ("dt_sb",), z_d)
        ti += 1
    p.op("vector", lambda e: e.tensor_copy(S0b_sb[:], S_sb[:]),
         reads=[("S",)], writes=[("S0b",)])

    p.barrier()
    a2s.close()
    pha1.close()
    msp.close()

    c1x = ExitStack()
    Xp = sbuf(c1x, "Xp", (128, D_INNER), BF16)
    Dm2 = [sbuf(c1x, f"Dm{i}", (128, HHALF * CH), F32) for i in range(2)]
    Lh = sbuf(c1x, "Lh", (128, HHALF * CH), BF16)
    Mth = sbuf(c1x, "Mth", (128, HHALF * CH), BF16)
    eabc = sbuf(c1x, "eabc", (128, HHALF * CH), BF16)
    Cexp = sbuf(c1x, "Cexp", (128, HHALF * CH), BF16)
    ysb = sbuf(c1x, "ysb", (128, NT_Z * CH), BF16)
    Sb_sb = sbuf(c1x, "Sb_sb", (128, NHEADS * HEADDIM), BF16)
    psY = [c1x.enter_context(nc.psum_tensor(f"psY{i}", [128, 512], F32)) for i in range(2)]

    p.op("vector", lambda e: e.memset(S_sb[:], 0.0), reads=[("S0b",)], writes=[("S",)])
    p.op("vector", lambda e: e.memset(Sb_sb[:], 0.0), writes=[("Sb",)])
    p.op("vector", lambda e: e.tensor_copy(
        xc[0][:].rearrange("p (nt w) -> p nt w", w=CHW)[:, :, 0:3],
        ctx3_sb[:].rearrange("p (nt w) -> p nt w", w=3)),
        reads=[("ctx3",)], writes=[("xc", 0)])
    for c in range(NCH):
        emit_chunk(c, True, xbc_d, ("d_xbc",), dt_sb, ("dt_sb",))

    p.op("vector", lambda e: e.tensor_tensor_scan(
        cumprev_sb[:, 1:NCH], Alast_sb[:, 0:NCH - 1], Alast_sb[:, 0:NCH - 1], 0.0,
        op0=ALU.add, op1=ALU.bypass),
        reads=[("Alast",)], writes=[("cumprev",)])
    p.op("vector", lambda e: e.memset(cumprev_sb[:, 0:1], 0.0), writes=[("cumprev",)])

    p.barrier()
    c1x.close()
    c1s.close()
    ms1.close()

    ph = ExitStack()
    yn_sb = sbuf(ph, "yn_sb", (128, NT_Z * HALF_T), BF16)
    ez = sbuf(ph, "ez", (NHEADS, CH), BF16)
    edc = sbuf(ph, "edc", (NHEADS, 1), F32)
    scc = sbuf(ph, "scc", (NHEADS, CH), BF16)
    scbc = sbuf(ph, "scbc", (128, HHALF * CH), BF16)
    Cph = sbuf(ph, "Cph", (128, HHALF * CH), BF16)
    yc2 = sbuf(ph, "yc2", (128, NT_Z * CH), F32)
    zc2 = sbuf(ph, "zc2", (128, NT_Z * CH), BF16)
    slz = sbuf(ph, "slz", (128, NT_Z * CH), F32)
    sqz = sbuf(ph, "sqz", (128, NT_Z * CH), F32)
    rrow = sbuf(ph, "rrow", (1, CH), F32)
    psC = [ph.enter_context(nc.psum_tensor(f"psC{i}", [128, 512], F32)) for i in range(2)]
    psR = ph.enter_context(nc.psum_tensor("psR", [128, 1024], F32))
    wD = [sbuf(ph, f"wD{i}", (128, 32 * 128), BF16) for i in range(2)]
    x0D2 = [sbuf(ph, f"x0D{i}", (128, HALF_T), F32) for i in range(2)]
    lnD2 = [sbuf(ph, f"lnD{i}", (128, HALF_T), BF16) for i in range(2)]
    x1D2 = [sbuf(ph, f"x1D{i}", (128, HALF_T), F32) for i in range(2)]
    psD = [ph.enter_context(nc.psum_tensor(f"psD{i}", [128, 512], F32)) for i in range(4)]
    DSL = slices(HALF_T, 512)

    def emit_d_group(half, g):
        n0g, n1g = g * 512, min(HALF_T, (g + 1) * 512)
        W = n1g - n0g
        for dt in range(16):
            par = dt % 2
            wb = wD[par]
            kw = ("wD", par)
            p.dma("gpsimd", f"d_w{par}", wb[:], w_out_r[dt].rearrange("f k d -> f (k d)"),
                  writes=[kw])
            ps = psD[par * 2 + (g % 2)]
            pkk = ("psD", par * 2 + (g % 2))
            for kt in range(32):
                p.op("tensor", lambda e, wb=wb, kt=kt, ps=ps:
                     e.matmul(ps[:, 0:W], wb[:, kt * 128:(kt + 1) * 128],
                              yn_sb[:, kt * HALF_T + n0g:kt * HALF_T + n1g],
                              start=(kt == 0), stop=(kt == 31)),
                     reads=[kw, ("yn", g)], writes=[pkk])
            x0D, lnD, x1D = x0D2[par], lnD2[par], x1D2[par]
            t0a = half * HALF_T + n0g
            p.dma("sync", f"d_x0{par}", x0D[:, 0:W],
                  x_fm[dt * 128:(dt + 1) * 128, t0a:t0a + W],
                  writes=[("x0D", par)])
            p.dma_tr("sync", f"d_ln{par}", lnD[:, 0:W],
                     ln1_d[t0a:t0a + W, dt * 128:(dt + 1) * 128],
                     reads=[("d_ln1",)], writes=[("lnD", par)])
            p.op("vector", lambda e, x1D=x1D, x0D=x0D, lnD=lnD: e.tensor_tensor(
                x1D[:, 0:W], x0D[:, 0:W], lnD[:, 0:W], op=ALU.add),
                reads=[("x0D", par), ("lnD", par)], writes=[("x1D", par)])
            p.op("vector", lambda e, ps=ps, x1D=x1D: e.tensor_tensor(
                x1D[:, 0:W], x1D[:, 0:W], ps[:, 0:W], op=ALU.add),
                reads=[("x1D", par), pkk], writes=[("x1D", par)])
            p.dma("scalar", f"d_x1{par}",
                  x1_d[dt][:, t0a:t0a + W], x1D[:, 0:W],
                  reads=[("x1D", par)], writes=[("d_x1",)])

    for half in range(2):
        for c in range(half * (NCH // 2), (half + 1) * (NCH // 2)):
            zrel = c - half * (NCH // 2)
            p.dma("sync", "c2_ez", ez[:], ea_d[c], reads=[("d_ea",)], writes=[("ez",)])
            p.op("scalar", lambda e, c=c: e.activation(edc[:], cumprev_sb[:, c:c + 1], AF.Exp),
                 reads=[("cumprev",)], writes=[("edc",)])
            p.op("vector", lambda e: e.tensor_tensor(edc[:], edc[:], mask2_sb[0:NHEADS, :],
                                                     op=ALU.mult),
                 reads=[("edc",), KC], writes=[("edc",)])
            p.op("vector", lambda e: e.tensor_scalar(scc[:], ez[:], edc[:], None, op0=ALU.mult),
                 reads=[("ez",), ("edc",)], writes=[("scc",)])
            p.dma("scalar", "c2_sc", sc_d[:, :], scc[:], reads=[("scc",)], writes=[("d_sc",)])
            p.dma("gpsimd", "c2_y", yc2[:], y_d[c], reads=[("d_y", c)], writes=[("yc2",)])
            for hh in range(NHEADS // HHALF):
                h0 = hh * HHALF
                p.dma("sync", "c2_scbc", scbc[:],
                      sc_d.rearrange("h l -> (h l)").unsqueeze(0)[:, h0 * CH:(h0 + HHALF) * CH]
                      .to_broadcast((128, HHALF * CH)),
                      reads=[("d_sc",)], writes=[("scbc",)])
                p.op("vector", lambda e, c=c: e.tensor_tensor(
                    Cph[:], scbc[:], rep_mid(Cfull_sb[:, c * CH:(c + 1) * CH], HHALF),
                    op=ALU.mult), reads=[("scbc",), ("Cfull",)], writes=[("Cph",)])
                for jj in range(HHALF // 2):
                    j = hh * (HHALF // 2) + jj
                    bank, kb = psC[jj % 2], ("psC", jj % 2)
                    for u in range(2):
                        h = 2 * j + u
                        lh = h - h0
                        rows = slice(64 * u, 64 * u + 64)
                        p.op("tensor", lambda e, bank=bank, rows=rows, h=h, lh=lh:
                             e.matmul(bank[rows, 0:CH], S0b_sb[:, h * 64:h * 64 + 64],
                                      Cph[:, lh * CH:(lh + 1) * CH], start=True, stop=True),
                             reads=[("S0b",), ("Cph",)], writes=[kb])
                    p.op("vector", lambda e, j=j, bank=bank: e.tensor_tensor(
                        yc2[:, j * CH:(j + 1) * CH], yc2[:, j * CH:(j + 1) * CH],
                        bank[:, 0:CH], op=ALU.add),
                        reads=[("yc2",), kb], writes=[("yc2",)])
            p.dma("sync", "c2_z", zc2[:].rearrange("p (nt l) -> p nt l", l=CH),
                  z_d.rearrange("(nt p) t -> p nt t", p=128)[:, :, c * CH:(c + 1) * CH],
                  reads=[("d_z",)], writes=[("zc2",)])
            p.op("scalar", lambda e: e.activation(slz[:], zc2[:], AF.Silu),
                 reads=[("zc2",)], writes=[("slz",)])
            p.op("vector", lambda e: e.tensor_tensor(yc2[:], yc2[:], slz[:], op=ALU.mult),
                 reads=[("yc2",), ("slz",)], writes=[("yc2",)])
            p.op("scalar", lambda e: e.activation(sqz[:], yc2[:], AF.Square),
                 reads=[("yc2",)], writes=[("sqz",)])
            for nt in range(NT_Z):
                p.op("tensor", lambda e, nt=nt: e.matmul(
                    psR[0:1, 0:CH], ones_col_sb[:], sqz[:, nt * CH:(nt + 1) * CH],
                    start=(nt == 0), stop=(nt == NT_Z - 1)),
                    reads=[("sqz",), KC], writes=[("psRs",)])
            p.op("scalar", lambda e: e.activation(rrow[:], psR[0:1, 0:CH], AF.Ln,
                                                  bias=misc_sb[0:1, 0:1],
                                                  scale=misc_sb[0:1, 2:3]),
                 reads=[("psRs",), KC], writes=[("rrow",)])
            p.op("scalar", lambda e: e.activation(rrow[:], rrow[:], AF.Exp,
                                                  scale=misc_sb[0:1, 3:4]),
                 reads=[("rrow",), KC], writes=[("rrow",)])
            p.op("tensor", lambda e: e.matmul(psR[:, 512:512 + CH], ones_row_sb[:],
                                              rrow[:], start=True, stop=True),
                 reads=[("rrow",), KC], writes=[("psRb",)])
            for nt in range(NT_Z):
                p.op("vector", lambda e, nt=nt, zrel=zrel: e.scalar_tensor_tensor(
                    yn_sb[:, nt * HALF_T + zrel * CH:nt * HALF_T + (zrel + 1) * CH],
                    yc2[:, nt * CH:(nt + 1) * CH], normw_sb[:, nt:nt + 1],
                    psR[:, 512:512 + CH], op0=ALU.mult, op1=ALU.mult),
                    reads=[("yc2",), ("psRb",), KC], writes=[("yn", zrel // 4)])
            if ((zrel + 1) * CH) % 512 == 0 or zrel == NCH // 2 - 1:
                if ((zrel + 1) * CH) % 512 == 0 or ((zrel + 1) * CH) == HALF_T:
                    emit_d_group(half, zrel * CH // 512)


    p.barrier()
    ph.close()
    ms2.close()

    ph = ExitStack()
    x2acc = sbuf(ph, "x2acc", (128, 16 * LG), F32)
    ln2 = sbuf(ph, "ln2", (128, 16 * LG), BF16)
    hsb = sbuf(ph, "hsb", (128, 16 * LG), BF16)
    sqE = [sbuf(ph, f"sqE{i}", (128, LG), F32) for i in range(2)]
    gkE = sbuf(ph, "gkE", (128, 16 * 8), BF16)
    rowsE = sbuf(ph, "rowsE", (1, 3 * LG), F32)
    abE = sbuf(ph, "abE", (128, 2 * LG), F32)
    g8 = sbuf(ph, "g8", (128, 8), F32)
    m8 = sbuf(ph, "m8", (128, 8), F32)
    maskE = sbuf(ph, "maskE", (128, 8), F32)
    eallE = sbuf(ph, "eallE", (128, 8), F32)
    em2E = sbuf(ph, "em2E", (128, 1), F32)
    denE = sbuf(ph, "denE", (128, 1), F32)
    negm1E = sbuf(ph, "negm1E", (128, 1), F32)
    cwE = sbuf(ph, "cwE", (128, 8), F32)
    cwTE = sbuf(ph, "cwTE", (8, LG), F32)
    cwrow = sbuf(ph, "cwrow", (1, LG), F32)
    cwb = sbuf(ph, "cwb", (128, LG), F32)
    usb = sbuf(ph, "usb", (128, LG), F32)
    usb2 = sbuf(ph, "usb2", (128, LG), F32)
    ssb = sbuf(ph, "ssb", (128, LG), BF16)
    ssb2 = sbuf(ph, "ssb2", (128, LG), BF16)
    wiA = [sbuf(ph, f"wiA{i}", (128, 16 * 128), BF16) for i in range(2)]
    wiB = [sbuf(ph, f"wiB{i}", (128, 16 * 128), BF16) for i in range(2)]
    woE = [sbuf(ph, f"woE{i}", (128, 16 * 128), BF16) for i in range(2)]
    psH0 = ph.enter_context(nc.psum_tensor("psH0", [128, 1024], F32))
    psH1 = ph.enter_context(nc.psum_tensor("psH1", [128, 1024], F32))
    psW = ph.enter_context(nc.psum_tensor("psW", [128, 1024], F32))
    psE = ph.enter_context(nc.psum_tensor("psE", [128, 1024], F32))
    KH0, KH1, KW_, KE = ("psH0",), ("psH1",), ("psW",), ("psE",)
    LSL = slices(LG, 512)

    p.dma("gpsimd", "e_gk", gkE[:], gk_r.rearrange("p k e -> p (k e)"), writes=[("gkE",)])

    for lg in range(NLG):
        t0 = lg * LG
        for dt in range(16):
            p.dma("sync", "e_x1", x2acc[:, dt * LG:(dt + 1) * LG],
                  x1_d[dt][:, t0:t0 + LG], reads=[("d_x1",)], writes=[("x2a", dt)])
        for dt in range(16):
            par = dt % 2
            p.op("scalar", lambda e, dt=dt, par=par: e.activation(
                sqE[par][:], x2acc[:, dt * LG:(dt + 1) * LG], AF.Square),
                reads=[("x2a", dt)], writes=[("sqE", par)])
            for (n0, n1) in LSL:
                p.op("tensor", lambda e, dt=dt, n0=n0, n1=n1: e.matmul(
                    psE[0:1, n0:n1], ones_col_sb[:], x2acc[:, dt * LG + n0:dt * LG + n1],
                    start=(dt == 0), stop=(dt == 15)),
                    reads=[("x2a", dt), KC], writes=[KE])
                p.op("tensor", lambda e, par=par, n0=n0, n1=n1, dt=dt: e.matmul(
                    psW[0:1, n0:n1], ones_col_sb[:], sqE[par][:, n0:n1],
                    start=(dt == 0), stop=(dt == 15)),
                    reads=[("sqE", par), KC], writes=[KW_])
        p.op("vector", lambda e: e.tensor_scalar(
            rowsE[0:1, 0:LG], psE[0:1, 0:LG], 1.0 / D_MODEL, None, op0=ALU.mult),
            reads=[KE], writes=[("rowsE",)])
        p.op("vector", lambda e: e.tensor_scalar(
            rowsE[0:1, LG:2 * LG], psW[0:1, 0:LG], 1.0 / D_MODEL, None, op0=ALU.mult),
            reads=[KW_, ("rowsE",)], writes=[("rowsE",)])
        p.op("vector", lambda e: e.tensor_tensor(
            rowsE[0:1, 2 * LG:3 * LG], rowsE[0:1, 0:LG], rowsE[0:1, 0:LG], op=ALU.mult),
            reads=[("rowsE",)], writes=[("rowsE",)])
        p.op("vector", lambda e: e.tensor_tensor(
            rowsE[0:1, LG:2 * LG], rowsE[0:1, LG:2 * LG], rowsE[0:1, 2 * LG:3 * LG],
            op=ALU.subtract), reads=[("rowsE",)], writes=[("rowsE",)])
        p.op("scalar", lambda e: e.activation(rowsE[0:1, LG:2 * LG],
                                              rowsE[0:1, LG:2 * LG], AF.Sqrt,
                                              bias=misc_sb[0:1, 0:1]),
             reads=[("rowsE",), KC], writes=[("rowsE",)])
        p.op("vector", lambda e: e.reciprocal(rowsE[0:1, LG:2 * LG], rowsE[0:1, LG:2 * LG]),
             reads=[("rowsE",)], writes=[("rowsE",)])
        p.op("vector", lambda e: e.tensor_tensor(
            rowsE[0:1, 0:LG], rowsE[0:1, 0:LG], rowsE[0:1, LG:2 * LG], op=ALU.mult),
            reads=[("rowsE",)], writes=[("rowsE",)])
        for (n0, n1) in LSL:
            p.op("tensor", lambda e, n0=n0, n1=n1: e.matmul(
                psE[:, n0:n1], ones_row_sb[:], rowsE[0:1, LG + n0:LG + n1],
                start=True, stop=True), reads=[("rowsE",), KC], writes=[KE])
            p.op("vector", lambda e, n0=n0, n1=n1: e.tensor_copy(
                abE[:, n0:n1], psE[:, n0:n1]), reads=[KE], writes=[("abE",)])
            p.op("tensor", lambda e, n0=n0, n1=n1: e.matmul(
                psW[:, n0:n1], ones_row_sb[:], rowsE[0:1, n0:n1],
                start=True, stop=True), reads=[("rowsE",), KC], writes=[KW_])
            p.op("vector", lambda e, n0=n0, n1=n1: e.tensor_copy(
                abE[:, LG + n0:LG + n1], psW[:, n0:n1]), reads=[KW_], writes=[("abE",)])
        for dt in range(16):
            par = dt % 2
            p.op("vector", lambda e, dt=dt, par=par: e.tensor_tensor(
                sqE[par][:], x2acc[:, dt * LG:(dt + 1) * LG], abE[:, 0:LG], op=ALU.mult),
                reads=[("x2a", dt), ("abE",)], writes=[("sqE", par)])
            p.op("vector", lambda e, dt=dt, par=par: e.tensor_tensor(
                ln2[:, dt * LG:(dt + 1) * LG], sqE[par][:], abE[:, LG:2 * LG],
                op=ALU.subtract),
                reads=[("sqE", par), ("abE",)], writes=[("ln2", dt)])
        KLN2 = [("ln2", dt) for dt in range(16)]
        for st in range(LG // 128):
            for kt in range(16):
                p.op("tensor", lambda e, kt=kt, st=st: e.matmul(
                    psE[:, 0:8], ln2[:, kt * LG + st * 128:kt * LG + (st + 1) * 128],
                    gkE[:, kt * 8:(kt + 1) * 8], start=(kt == 0), stop=(kt == 15)),
                    reads=[KLN2[kt], ("gkE",)], writes=[KE])
            p.op("vector", lambda e: e.tensor_copy(g8[:], psE[:, 0:8]),
                 reads=[KE], writes=[("g8",)])
            p.op("vector", lambda e: e.max(out=m8[:], in_=g8[:]),
                 reads=[("g8",)], writes=[("m8",)])
            p.op("vector", lambda e: e.tensor_scalar(maskE[:], g8[:], m8[:, 1:2], None,
                                                     op0=ALU.is_ge),
                 reads=[("g8",), ("m8",)], writes=[("maskE",)])
            p.op("vector", lambda e: e.tensor_scalar(negm1E[:], m8[:, 0:1], -1.0, None,
                                                     op0=ALU.mult),
                 reads=[("m8",)], writes=[("negm1",)])
            p.op("scalar", lambda e: e.activation(eallE[:], g8[:], AF.Exp, bias=negm1E[:]),
                 reads=[("g8",), ("negm1",)], writes=[("eall",)])
            p.op("scalar", lambda e: e.activation(em2E[:], m8[:, 1:2], AF.Exp, bias=negm1E[:]),
                 reads=[("m8",), ("negm1",)], writes=[("em2",)])
            p.op("vector", lambda e: e.tensor_scalar(denE[:], em2E[:], 1.0, None, op0=ALU.add),
                 reads=[("em2",)], writes=[("den",)])
            p.op("vector", lambda e: e.reciprocal(denE[:], denE[:]),
                 reads=[("den",)], writes=[("den",)])
            p.op("vector", lambda e: e.scalar_tensor_tensor(
                cwE[:], eallE[:], denE[:], maskE[:], op0=ALU.mult, op1=ALU.mult),
                reads=[("eall",), ("den",), ("maskE",)], writes=[("cwE",)])
            p.op("tensor", lambda e: e.transpose(psE[0:8, 512:512 + 128], cwE[:],
                                                 identf_sb[:]),
                 reads=[("cwE",), KC], writes=[KE])
            p.op("vector", lambda e, st=st: e.tensor_copy(
                cwTE[:, st * 128:(st + 1) * 128], psE[0:8, 512:512 + 128]),
                reads=[KE], writes=[("cwT",)])
        p.dma("scalar", "e_cw", cw_d[:, t0:t0 + LG], cwTE[:],
              reads=[("cwT",)], writes=[("d_cw",)])

        for ex in range(8):
            p.dma("sync", "e_cwrow", cwrow[:], cw_d[ex:ex + 1, t0:t0 + LG],
                  reads=[("d_cw",)], writes=[("cwrow",)])
            for (n0, n1) in LSL:
                p.op("tensor", lambda e, n0=n0, n1=n1: e.matmul(
                    psE[:, n0:n1], ones_row_sb[:], cwrow[0:1, n0:n1],
                    start=True, stop=True), reads=[("cwrow",), KC], writes=[KE])
                p.op("vector", lambda e, n0=n0, n1=n1: e.tensor_copy(
                    cwb[:, n0:n1], psE[:, n0:n1]), reads=[KE], writes=[("cwb",)])
            for mt in range(16):
                par = mt % 2
                ph0, kh0 = (psH0, KH0) if par == 0 else (psW, KW_)
                ph1, kh1 = (psH1, KH1) if par == 0 else (psE, KE)
                p.dma("gpsimd", f"e_wi0_{par}", wiA[par][:],
                      wi0_r[ex, mt].rearrange("d k m -> d (k m)"), writes=[("wiA", par)])
                p.dma("gpsimd", f"e_wi1_{par}", wiB[par][:],
                      wi1_r[ex, mt].rearrange("d k m -> d (k m)"), writes=[("wiB", par)])
                for kt in range(16):
                    for (n0, n1) in LSL:
                        p.op("tensor", lambda e, par=par, kt=kt, n0=n0, n1=n1, ph0=ph0:
                             e.matmul(ph0[:, n0:n1], wiA[par][:, kt * 128:(kt + 1) * 128],
                                      ln2[:, kt * LG + n0:kt * LG + n1],
                                      start=(kt == 0), stop=(kt == 15)),
                             reads=[("wiA", par), KLN2[kt]], writes=[kh0])
                        p.op("tensor", lambda e, par=par, kt=kt, n0=n0, n1=n1, ph1=ph1:
                             e.matmul(ph1[:, n0:n1], wiB[par][:, kt * 128:(kt + 1) * 128],
                                      ln2[:, kt * LG + n0:kt * LG + n1],
                                      start=(kt == 0), stop=(kt == 15)),
                             reads=[("wiB", par), KLN2[kt]], writes=[kh1])
                usbp, kusb = (usb, ("usb", 0)) if par == 0 else (usb2, ("usb", 1))
                ssbp, kssb = (ssb, ("ssb", 0)) if par == 0 else (ssb2, ("ssb", 1))
                p.op("vector", lambda e, ph1=ph1, usbp=usbp: e.tensor_tensor(
                    usbp[:], ph1[:, 0:LG], cwb[:], op=ALU.mult),
                    reads=[kh1, ("cwb",)], writes=[kusb])
                p.op("scalar", lambda e, ph0=ph0, ssbp=ssbp: e.activation(
                    ssbp[:], ph0[:, 0:LG], AF.Silu),
                    reads=[kh0], writes=[kssb])
                p.op("vector", lambda e, mt=mt, usbp=usbp, ssbp=ssbp: e.tensor_tensor(
                    hsb[:, mt * LG:(mt + 1) * LG], usbp[:], ssbp[:], op=ALU.mult),
                    reads=[kusb, kssb], writes=[("hsb", mt)])
            for dt in range(16):
                par = dt % 2
                pw, kw2 = (psH0, KH0) if par == 0 else (psW, KW_)
                p.dma("gpsimd", f"e_wo_{par}", woE[par][:],
                      wo_r[ex, dt].rearrange("m t d -> m (t d)"), writes=[("woE", par)])
                for mt in range(16):
                    for (n0, n1) in LSL:
                        p.op("tensor", lambda e, par=par, mt=mt, n0=n0, n1=n1, pw=pw:
                             e.matmul(pw[:, n0:n1], woE[par][:, mt * 128:(mt + 1) * 128],
                                      hsb[:, mt * LG + n0:mt * LG + n1],
                                      start=(mt == 0), stop=(mt == 15)),
                             reads=[("woE", par), ("hsb", mt)], writes=[kw2])
                for (n0, n1) in LSL:
                    p.op("vector", lambda e, dt=dt, n0=n0, n1=n1, pw=pw: e.tensor_tensor(
                        x2acc[:, dt * LG + n0:dt * LG + n1],
                        x2acc[:, dt * LG + n0:dt * LG + n1], pw[:, n0:n1], op=ALU.add),
                        reads=[("x2a", dt), kw2], writes=[("x2a", dt)])
        for dt in range(16):
            p.dma("scalar", "e_out", out_d[dt][:, t0:t0 + LG],
                  x2acc[:, dt * LG:(dt + 1) * LG],
                  reads=[("x2a", dt)], writes=[("d_out",)])

    p.fence("gpsimd", [("d_out",)])
    p.build()
    return nc


def kernel(**inputs):
    T = 2048
    in_maps = host_layouts(inputs, T)
    nc = build_program(T)
    res = run_bass_kernel_spmd(nc, in_maps, core_ids=list(range(8)))
    out = np.zeros((4, 4096, 2048), np.float32)
    for core in range(8):
        b, half = core // 2, core % 2
        o = np.asarray(res.results[core]["out_fm"])
        out[b, half * T:(half + 1) * T, :] = o.reshape(2048, T).T
    return out


# version 12
# speedup vs baseline: 1.1318x; 1.1318x over previous
import numpy as np
import ml_dtypes
from contextlib import ExitStack

import concourse.bass as bass
import concourse.mybir as mybir
from concourse.bass_utils import run_bass_kernel_spmd

F32 = mybir.dt.float32
BF16 = mybir.dt.bfloat16
AF = mybir.ActivationFunctionType
ALU = mybir.AluOpType
bfloat16 = ml_dtypes.bfloat16

D_MODEL = 2048
HEADDIM = 64
CH = 128
D_INNER = 4096
NHEADS = 64
NPAIR = 32
CONV_DIM = 4352
D_IN_PROJ = 8512
NT_Z = 32
NT_XBC = 34
NT_IN = 67
EPS = 1e-5
NEG_BIG = -1e9
HHALF = 16

ENGS = ("tensor", "vector", "scalar", "gpsimd", "sync")


def rep_mid(ap2d, reps):
    return ap2d.unsqueeze(1).to_broadcast((ap2d.shape[0], reps, ap2d.shape[1]))


def rep_last(ap2d, reps):
    return ap2d.to_broadcast((ap2d.shape[0], ap2d.shape[1], reps))


def slices(total, step):
    out = []
    o = 0
    while o < total:
        out.append((o, min(total, o + step)))
        o += step
    return out


class Prog:
    def __init__(self, nc):
        self.nc = nc
        self.es = ExitStack()
        self.sems = {}
        self.count = {}
        self.watermark = {}
        self.last_write = {}
        self.readers = {}
        self.ops = []
        for e in ENGS:
            self._new_sem(e)

    def _new_sem(self, name):
        self.sems[name] = self.es.enter_context(self.nc.semaphore("sem_" + name))
        self.count[name] = 0

    def dma_group(self, name):
        if name not in self.sems:
            self._new_sem(name)
        return name

    def _need_waits(self, eng, reads, writes):
        need = {}

        def add(sem, val):
            if val > need.get(sem, 0):
                need[sem] = val

        for r in reads:
            lw = self.last_write.get(r)
            if lw:
                add(*lw)
        for w in writes:
            lw = self.last_write.get(w)
            if lw:
                add(*lw)
            for sem, val in self.readers.get(w, {}).items():
                add(sem, val)
        out = []
        for sem, val in need.items():
            if sem == eng == "tensor":
                continue
            if sem not in ENGS:
                val = self.count[sem]
            wm = self.watermark.get((eng, sem), 0)
            if val > wm:
                out.append((sem, val))
                self.watermark[(eng, sem)] = val
        return out

    def op(self, eng, emit_fn, reads=(), writes=(), tag=None):
        waits = self._need_waits(eng, reads, writes)
        self.count[eng] += 1
        v = self.count[eng]
        self.ops.append((eng, waits, emit_fn, eng, 1, tag))
        for w in writes:
            self.last_write[w] = (eng, v)
            self.readers[w] = {}
        for r in reads:
            self.readers.setdefault(r, {})[eng] = v

    def _dma_common(self, q, group, emit, reads, writes, tag):
        g = self.dma_group(group)
        waits = self._need_waits(q, reads, writes)
        self.count[g] += 16
        v = self.count[g]
        self.ops.append((q, waits, emit, g, 16, tag))
        for w in writes:
            self.last_write[w] = (g, v)
            self.readers[w] = {}
        for r in reads:
            self.readers.setdefault(r, {})[g] = v

    def dma(self, q, group, out, in_, reads=(), writes=(), tag=None, **kw):
        def emit(e, out=out, in_=in_, kw=kw):
            return e.dma_start(out=out, in_=in_, **kw)
        self._dma_common(q, group, emit, reads, writes, tag)

    def dma_tr(self, q, group, out, in_, reads=(), writes=(), tag=None):
        def emit(e, out=out, in_=in_):
            return e.dma_start_transpose(out=out, in_=in_)
        self._dma_common(q, group, emit, reads, writes, tag)

    def collective(self, emit_fn, reads=(), writes=(), tag="cc"):
        self._dma_common("gpsimd", "cc", emit_fn, reads, writes, tag)

    def nop_marker(self, eng, reads, writes):
        self.op(eng, lambda e: e.engine_nop(), reads=reads, writes=writes)

    def barrier(self):
        snap = dict(self.count)
        for e in ENGS:
            waits = []
            for sem, val in snap.items():
                if sem == e or val <= 0:
                    continue
                wm = self.watermark.get((e, sem), 0)
                if val > wm:
                    waits.append((sem, val))
                    self.watermark[(e, sem)] = val
            if waits:
                self.ops.append((e, waits, None, None, 0, "barrier"))

    def fence(self, eng, keys):
        waits = self._need_waits(eng, keys, ())
        if waits:
            self.ops.append((eng, waits, None, None, 0, "fence"))

    def build(self):
        per_eng = {}
        for rec in self.ops:
            per_eng.setdefault(rec[0], []).append(rec)
        block = self.es.enter_context(self.nc.Block())
        for eng_name in ENGS:
            recs = per_eng.get(eng_name, [])
            if not recs:
                continue
            section = getattr(block, eng_name)

            def body(engine, recs=recs):
                for (_e, waits, emit_fn, inc_sem, inc_val, _tag) in recs:
                    for sem, val in waits:
                        engine.wait_ge(self.sems[sem], val)
                    if emit_fn is not None:
                        inst = emit_fn(engine)
                        if inc_sem is not None:
                            inst.then_inc(self.sems[inc_sem], inc_val)

            section(body)



def host_layouts(inputs, T=2048):
    f32 = np.float32
    x = np.ascontiguousarray(np.asarray(inputs["L0_attn_x"], f32))
    w_in = np.asarray(inputs["L0_attn_w_in"], f32)
    conv_w = np.asarray(inputs["L0_attn_conv_w"], f32)
    conv_b = np.asarray(inputs["L0_attn_conv_b"], f32)
    dt_bias = np.asarray(inputs["L0_attn_dt_bias"], f32)
    A_log = np.asarray(inputs["L0_attn_A_log"], f32)
    Dv = np.asarray(inputs["L0_attn_D"], f32)
    norm_w = np.asarray(inputs["L0_attn_norm_w"], f32)
    w_out = np.asarray(inputs["L0_attn_w_out"], f32)
    gk = np.asarray(inputs["L0_ffn_gate_kernel"], f32)
    wi0 = np.asarray(inputs["L0_ffn_wi_0"], f32)
    wi1 = np.asarray(inputs["L0_ffn_wi_1"], f32)
    wo = np.asarray(inputs["L0_ffn_wo"], f32)

    w_in_p = np.zeros((2048, NT_IN * 128), f32)
    w_in_p[:, :D_IN_PROJ] = w_in
    w_in_r = np.ascontiguousarray(
        w_in_p.reshape(16, 128, NT_IN, 128).transpose(2, 1, 0, 3))
    w_out_r = np.ascontiguousarray(
        w_out.reshape(32, 128, 16, 128).transpose(2, 1, 0, 3))
    wi0_r = np.ascontiguousarray(
        wi0.reshape(8, 16, 128, 16, 128).transpose(0, 3, 2, 1, 4))
    wi1_r = np.ascontiguousarray(
        wi1.reshape(8, 16, 128, 16, 128).transpose(0, 3, 2, 1, 4))
    wo_r = np.ascontiguousarray(
        wo.reshape(8, 16, 128, 16, 128).transpose(0, 3, 2, 1, 4))
    gk_r = np.ascontiguousarray(gk.reshape(16, 128, 8).transpose(1, 0, 2))
    conv_w_r = np.ascontiguousarray(conv_w.reshape(NT_XBC, 128, 4).transpose(1, 0, 2))
    conv_b_r = np.ascontiguousarray(conv_b.reshape(NT_XBC, 128).transpose(1, 0))
    D_cols = np.ascontiguousarray(
        Dv.reshape(NPAIR, 2)[:, :, None].repeat(HEADDIM, 2)
        .transpose(1, 2, 0).reshape(128, NPAIR))
    normw_cols = np.ascontiguousarray(norm_w.reshape(NT_Z, 128).transpose(1, 0))
    dtb = np.ascontiguousarray(dt_bias.reshape(NHEADS, 1))
    alog = np.ascontiguousarray(A_log.reshape(NHEADS, 1))

    li = np.arange(CH)
    shared = dict(
        w_in_r=w_in_r, w_out_r=w_out_r, wi0_r=wi0_r, wi1_r=wi1_r, wo_r=wo_r,
        gk_r=gk_r, conv_w_r=conv_w_r, conv_b_r=conv_b_r, D_cols=D_cols,
        normw_cols=normw_cols, dt_bias_c=dtb, A_log_c=alog,
        ident_bf=np.eye(128, dtype=bfloat16),
        ident_f32=np.eye(128, dtype=f32),
        ones_row_f32=np.ones((1, 128), f32),
        ones_col_f32=np.ones((128, 1), f32),
        masktri=np.where(li[None, :] >= li[:, None], 0.0, NEG_BIG).astype(f32),
        misc_cols=np.tile(np.array([[EPS, -1.0, 1.0 / D_INNER, -0.5]], f32), (128, 1)),
    )

    in_maps = []
    for core in range(8):
        b, half = core // 2, core % 2
        x_tm = np.ascontiguousarray(x[b, half * T:(half + 1) * T])
        x_fm = np.ascontiguousarray(x_tm.T)
        x_ctx = np.zeros((128, 2048), f32)
        x_ptm = np.zeros((T, 2048), f32)
        if half == 1:
            x_ctx[0:3] = x[b, T - 3:T]
            x_ptm[:] = x[b, 0:T]
        m = dict(shared)
        m.update(x_tm=x_tm, x_fm=x_fm, x_ctx=x_ctx, x_ptm=x_ptm,
                 mask_first=np.full((128, 1), 0.0 if half else 1.0, f32),
                 mask_second=np.full((128, 1), 1.0 if half else 0.0, f32))
        in_maps.append(m)
    return in_maps



def build_program(T=2048, debug_taps=()):
    nc = bass.Bass("TRN2", target_bir_lowering=False)
    NCH = T // CH
    NST = T // 128
    LG = min(1024, T)
    NLG = T // LG
    HALF_T = T // 2
    p = Prog(nc)
    es = p.es

    def din(name, shape, dtype=F32):
        return nc.dram_tensor(name, list(shape), dtype, kind="ExternalInput").ap()

    def dint(name, shape, dtype):
        kind = "ExternalOutput" if name in debug_taps else "Internal"
        return nc.dram_tensor(name, list(shape), dtype, kind=kind).ap()

    x_tm = din("x_tm", (T, 2048))
    x_fm = din("x_fm", (2048, T))
    x_ctx = din("x_ctx", (128, 2048))
    x_ptm = din("x_ptm", (T, 2048))
    w_in_r = din("w_in_r", (NT_IN, 128, 16, 128))
    w_out_r = din("w_out_r", (16, 128, 32, 128))
    wi0_r = din("wi0_r", (8, 16, 128, 16, 128))
    wi1_r = din("wi1_r", (8, 16, 128, 16, 128))
    wo_r = din("wo_r", (8, 16, 128, 16, 128))
    gk_r = din("gk_r", (128, 16, 8))
    conv_w_d = din("conv_w_r", (128, NT_XBC, 4))
    conv_b_d = din("conv_b_r", (128, NT_XBC))
    D_cols_d = din("D_cols", (128, NPAIR))
    normw_d = din("normw_cols", (128, NT_Z))
    dtb_d = din("dt_bias_c", (NHEADS, 1))
    alog_d = din("A_log_c", (NHEADS, 1))
    ident_bf_d = din("ident_bf", (128, 128), BF16)
    ident_f32_d = din("ident_f32", (128, 128))
    ones_row_f32_d = din("ones_row_f32", (1, 128))
    ones_col_f32_d = din("ones_col_f32", (128, 1))
    masktri_d = din("masktri", (CH, CH))
    mask1_d = din("mask_first", (128, 1))
    mask2_d = din("mask_second", (128, 1))
    misc_d = din("misc_cols", (128, 4))

    out_d = nc.dram_tensor("out_fm", [16, 128, T], F32, kind="ExternalOutput").ap()

    ln1_d = dint("ln1_d", (T + 128, 2048), BF16)
    z_d = dint("z_d", (D_INNER, T), BF16)
    xbc_d = dint("xbc_d", (CONV_DIM, T), BF16)
    ln1p_d = dint("ln1p_d", (T, 2048), BF16)
    xbcp_d = dint("xbcp_d", (CONV_DIM, T), BF16)
    xh_d = dint("xh_d", (D_INNER, CH), BF16)
    acs_d = dint("acs_d", (NHEADS, CH), F32)
    ea_d = dint("ea_d", (NCH, NHEADS, CH), BF16)
    sc_d = dint("sc_d", (NHEADS, CH), BF16)
    y_d = dint("y_d", (NCH, 128, NT_Z * CH), BF16)
    x1_d = dint("x1_d", (16, 128, T), F32)
    cw_d = dint("cw_d", (8, T), F32)

    def sbuf(stack, name, shape, dtype, side=None):
        return stack.enter_context(nc.sbuf_tensor(name, list(shape), dtype, side=side))

    conv_w_sb = sbuf(es, "conv_w_sb", (128, NT_XBC * 4), F32)
    conv_b_sb = sbuf(es, "conv_b_sb", (128, NT_XBC), F32)
    D_cols_sb = sbuf(es, "D_cols_sb", (128, NPAIR), F32)
    normw_sb = sbuf(es, "normw_sb", (128, NT_Z), F32)
    dtb_sb = sbuf(es, "dtb_sb", (NHEADS, 1), F32)
    alog_sb = sbuf(es, "alog_sb", (NHEADS, 1), F32)
    expalog_sb = sbuf(es, "expalog_sb", (NHEADS, 1), F32)
    ident_sb = sbuf(es, "ident_sb", (128, 128), BF16)
    identf_sb = sbuf(es, "identf_sb", (128, 128), F32)
    ones_row_sb = sbuf(es, "ones_row_sb", (1, 128), F32)
    ones_col_sb = sbuf(es, "ones_col_sb", (128, 1), F32)
    masktri_sb = sbuf(es, "masktri_sb", (CH, CH), F32)
    mask1_sb = sbuf(es, "mask1_sb", (128, 1), F32)
    mask2_sb = sbuf(es, "mask2_sb", (128, 1), F32)
    misc_sb = sbuf(es, "misc_sb", (128, 4), F32)
    ms1 = ExitStack()
    dt_sb = sbuf(ms1, "dt_sb", (NHEADS, T), F32, side="right")
    ctx3_sb = sbuf(ms1, "ctx3_sb", (128, NT_XBC * 3), F32, side="right")
    xc = [sbuf(ms1, f"xc{i}", (128, NT_XBC * (CH + 3)), BF16, side="right") for i in range(2)]
    msp = ExitStack()
    dtp_sb = sbuf(msp, "dtp_sb", (NHEADS, T), F32, side="right")
    ms2 = ExitStack()
    S0b_sb = sbuf(ms2, "S0b_sb", (128, NHEADS * HEADDIM), BF16)
    Cfull_sb = sbuf(ms2, "Cfull_sb", (128, NCH * CH), BF16)
    Alast_sb = sbuf(ms2, "Alast_sb", (NHEADS, NCH), F32)
    cumprev_sb = sbuf(ms2, "cumprev_sb", (NHEADS, NCH), F32)

    KC = ("consts",)
    for dst, src in [
        (conv_w_sb, conv_w_d.rearrange("p a b -> p (a b)")), (conv_b_sb, conv_b_d),
        (D_cols_sb, D_cols_d), (normw_sb, normw_d), (dtb_sb, dtb_d),
        (alog_sb, alog_d), (ident_sb, ident_bf_d), (identf_sb, ident_f32_d),
        (ones_row_sb, ones_row_f32_d), (ones_col_sb, ones_col_f32_d),
        (masktri_sb, masktri_d), (mask1_sb, mask1_d), (mask2_sb, mask2_d),
        (misc_sb, misc_d),
    ]:
        p.dma("sync", "consts", dst[:], src, writes=[KC])
    p.op("scalar", lambda e: e.activation(expalog_sb[:], alog_sb[:], AF.Exp),
         reads=[KC], writes=[KC])

    pha1 = ExitStack()
    _x1b = sbuf(pha1, "a1_x0", (128, 2048), F32, side="right")
    xst = [_x1b, _x1b]
    _l1b = sbuf(pha1, "a1_ln0", (128, 2048), BF16, side="right")
    lnb = [_l1b, _l1b]
    st6 = sbuf(pha1, "a1_st6", (128, 24), F32, side="right")
    mv_sb = sbuf(pha1, "a1_mv", (128, 2), F32, side="right")
    rstd_sb = sbuf(pha1, "a1_rstd", (128, 1), F32, side="right")
    mr_sb = sbuf(pha1, "a1_mr", (128, 1), F32, side="right")

    order = list(range(NST + 1, 2 * NST + 1)) + list(range(NST + 1))
    for st in order:
        par = st % 2
        xb, lb = xst[par], lnb[par]
        kx, kl = ("a1x",), ("a1l",)
        if st < NST:
            src = x_tm[st * 128:(st + 1) * 128, :]
        elif st == NST:
            src = x_ctx[:, :]
        else:
            sp = st - NST - 1
            src = x_ptm[sp * 128:(sp + 1) * 128, :]
        p.dma("scalar", "a1_in", xb[:], src, writes=[kx])
        for q in range(4):
            p.op("vector", lambda e, xb=xb, q=q: e.bn_stats(
                out=st6[:, q * 6:(q + 1) * 6], in_=xb[:, q * 512:(q + 1) * 512]),
                reads=[kx], writes=[("a1st6",)])
        p.op("vector", lambda e: e.bn_aggr(out=mv_sb[:], in_=st6[:]),
             reads=[("a1st6",)], writes=[("a1mv",)])
        p.op("scalar", lambda e: e.activation(rstd_sb[:], mv_sb[:, 1:2], AF.Sqrt, bias=misc_sb[:, 0:1]),
             reads=[("a1mv",), KC], writes=[("a1rstd",)])
        p.op("vector", lambda e: e.reciprocal(rstd_sb[:], rstd_sb[:]),
             reads=[("a1rstd",)], writes=[("a1rstd",)])
        p.op("vector", lambda e: e.tensor_tensor(mr_sb[:], mv_sb[:, 0:1], rstd_sb[:], op=ALU.mult),
             reads=[("a1mv",), ("a1rstd",)], writes=[("a1mr",)])
        p.op("vector", lambda e, xb=xb, lb=lb: e.tensor_scalar(
            lb[:], xb[:], rstd_sb[:], mr_sb[:], op0=ALU.mult, op1=ALU.subtract),
            reads=[kx, ("a1rstd",), ("a1mr",)], writes=[kl])
        if st <= NST:
            p.dma("scalar", "a1_out", ln1_d[st * 128:(st + 1) * 128, :], lb[:],
                  reads=[kl], writes=[("d_ln1",)])
        else:
            sp = st - NST - 1
            p.dma("scalar", "a1_out", ln1p_d[sp * 128:(sp + 1) * 128, :], lb[:],
                  reads=[kl], writes=[("d_ln1p",)])

    c1s = ExitStack()
    convo_s = sbuf(c1s, "convo", (128, NT_XBC * CH), F32)
    convo2 = [convo_s, convo_s]
    _xbcs = sbuf(c1s, "xbcs0", (128, NT_XBC * CH), BF16)
    xbcs2 = [_xbcs, _xbcs]
    dtc2 = [sbuf(c1s, f"dtc{i}", (NHEADS, CH), F32) for i in range(2)]
    acs2 = [sbuf(c1s, f"acs{i}", (NHEADS, CH), F32) for i in range(2)]
    nacs2 = [sbuf(c1s, f"nacs{i}", (NHEADS, CH), F32) for i in range(2)]
    _expa = sbuf(c1s, "expa0", (NHEADS, CH), BF16)
    expa2 = [_expa, _expa]
    decay2 = [sbuf(c1s, f"decay{i}", (NHEADS, CH), F32) for i in range(2)]
    ddec2 = [sbuf(c1s, f"ddec{i}", (NHEADS, CH), F32) for i in range(2)]
    _t64a = sbuf(c1s, "t64a0", (NHEADS, CH), F32)
    t64a2 = [_t64a, _t64a]
    _t64b = sbuf(c1s, "t64b0", (NHEADS, CH), F32)
    t64b2 = [_t64b, _t64b]
    _eacol = sbuf(c1s, "eacol0", (NHEADS, 1), F32)
    eacol2 = [_eacol, _eacol]
    dtT2 = [sbuf(c1s, f"dtT{i}", (128, NHEADS), F32) for i in range(2)]
    ddecT2 = [sbuf(c1s, f"ddecT{i}", (128, NHEADS), F32) for i in range(2)]
    nacsT2 = [sbuf(c1s, f"nacsT{i}", (128, NHEADS), F32) for i in range(2)]
    EAb2 = [sbuf(c1s, f"EAb{i}", (128, NHEADS), F32) for i in range(2)]
    _earow = sbuf(c1s, "earow0", (1, NHEADS), F32)
    earow2 = [_earow, _earow]
    Btm = sbuf(c1s, "Btm", (128, CH), BF16)
    Gsb = sbuf(c1s, "Gsb", (128, CH), BF16)
    t128 = sbuf(c1s, "t128", (128, CH), F32)
    Xdec = sbuf(c1s, "Xdec", (128, D_INNER), BF16)
    _xhg = sbuf(c1s, "xhg0", (128, 1024), BF16)
    xhg2 = [_xhg, _xhg]
    S_sb = sbuf(c1s, "S_sb", (128, NHEADS * HEADDIM), F32)
    psS = [c1s.enter_context(nc.psum_tensor(f"psS{i}", [128, 512], F32)) for i in range(2)]
    psM = c1s.enter_context(nc.psum_tensor("psM", [128, 512], F32))
    psMb = c1s.enter_context(nc.psum_tensor("psMb", [128, 1024], BF16))
    KM, KMB = ("psM",), ("psMb",)
    CHW = CH + 3

    a2s = ExitStack()
    TC = T + 128
    ln1f = sbuf(a2s, "ln1f", (128, 16 * T), BF16, side="right")
    lnctxf = sbuf(a2s, "lnctxf", (128, 16 * 128), BF16, side="right")
    w2 = [sbuf(a2s, f"a2_w{i}", (128, 16 * 128), BF16, side="right") for i in range(2)]
    _evb = sbuf(a2s, "a2_eb0", (128, T), BF16, side="right")
    ev_b = [_evb, _evb]
    ctx_ev = sbuf(a2s, "a2_ctxev", (128, 128), F32, side="right")
    ps_a2 = [a2s.enter_context(nc.psum_tensor(f"psA{i}", [128, 512], F32))
             for i in range(4)]
    TSL = slices(T, 512)
    KLN1F = [("ln1f", kt) for kt in range(16)]

    def emit_a2_tile(nt, need_ctx, xbc_dst, kxbc_dst, dt_dst, kdt_dst, z_out):
        if True:
            par = nt % 2
            wb = w2[par]
            kw = ("a2w", par)
            p.dma("gpsimd", f"a2_w{par}", wb[:], w_in_r[nt].rearrange("d k n -> d (k n)"),
                  writes=[kw], tag=f"w_in {nt}")
            pset = [ps_a2[i] for i in range(len(TSL))]
            pk = [("psA", i) for i in range(len(TSL))]
            for kt in range(16):
                for s, (n0, n1) in enumerate(TSL):
                    p.op("tensor",
                         lambda e, wb=wb, kt=kt, n0=n0, n1=n1, ps=pset[s]:
                         e.matmul(ps[:, 0:n1 - n0], wb[:, kt * 128:(kt + 1) * 128],
                                  ln1f[:, kt * T + n0:kt * T + n1],
                                  start=(kt == 0), stop=(kt == 15)),
                         reads=[kw, KLN1F[kt]], writes=[pk[s]])
            is_z = nt < NT_Z
            is_dt = nt >= NT_Z + NT_XBC
            eb = ev_b[par]
            keb = ("a2eb",)
            for s, (n0, n1) in enumerate(TSL):
                if is_dt:
                    p.op("vector", lambda e, ps=pset[s], n0=n0, n1=n1, dt_dst=dt_dst:
                         e.tensor_copy(dt_dst[:, n0:n1], ps[0:64, 0:n1 - n0]),
                         reads=[pk[s]], writes=[kdt_dst])
                elif s % 2 == 0:
                    p.op("vector", lambda e, ps=pset[s], n0=n0, n1=n1, eb=eb:
                         e.tensor_copy(eb[:, n0:n1], ps[:, 0:n1 - n0]),
                         reads=[pk[s]], writes=[keb])
                else:
                    p.op("scalar", lambda e, ps=pset[s], n0=n0, n1=n1, eb=eb:
                         e.copy(eb[:, n0:n1], ps[:, 0:n1 - n0]),
                         reads=[pk[s]], writes=[keb])
            if is_z:
                p.dma("scalar", "a2_zout", z_out[nt * 128:(nt + 1) * 128, :], eb[:, 0:T],
                      reads=[keb], writes=[("d_z",)])
            elif not is_dt:
                j = nt - NT_Z
                p.dma("scalar", "a2_xout", xbc_dst[j * 128:(j + 1) * 128, :], eb[:, 0:T],
                      reads=[keb], writes=[kxbc_dst])
                if need_ctx:
                    pc, kpc = pset[0], pk[0]
                    for kt in range(16):
                        p.op("tensor", lambda e, wb=wb, kt=kt, pc=pc:
                             e.matmul(pc[:, 0:128], wb[:, kt * 128:(kt + 1) * 128],
                                      lnctxf[:, kt * 128:(kt + 1) * 128],
                                      start=(kt == 0), stop=(kt == 15)),
                             reads=[kw, KLN1F[kt]], writes=[kpc])
                    p.op("vector", lambda e, pc=pc: e.tensor_copy(ctx_ev[:], pc[:, 0:128]),
                         reads=[kpc], writes=[("a2ctx",)])
                    p.op("vector", lambda e, j=j: e.tensor_copy(
                        ctx3_sb[:, j * 3:(j + 1) * 3], ctx_ev[:, 0:3]),
                        reads=[("a2ctx",)], writes=[("ctx3",)])

    def emit_chunk(c, full, xbcd, kxbcd, dtbuf, kdtbuf):
        par = c % 2
        xcb = xc[par]
        kxc = ("xc", par)
        convo, xbcs = convo2[par], xbcs2[par]
        dtc, acs, nacs, expa = dtc2[par], acs2[par], nacs2[par], expa2[par]
        decay, ddec, t64a, t64b = decay2[par], ddec2[par], t64a2[par], t64b2[par]
        eacol, dtT, ddecT, nacsT = eacol2[par], dtT2[par], ddecT2[par], nacsT2[par]
        EAb, earow = EAb2[par], earow2[par]
        p.dma("sync", f"c1_xbc{par}",
              xcb[:].rearrange("p (nt w) -> p nt w", w=CHW)[:, :, 3:3 + CH],
              xbcd.rearrange("(nt p) t -> p nt t", p=128)[:, :, c * CH:(c + 1) * CH],
              reads=[kxbcd], writes=[kxc])
        for nt in range(NT_XBC):
            o = nt * CHW
            co = nt * CH
            if nt % 2 == 0 or not full:
                p.op("vector", lambda e, o=o, co=co, nt=nt, xcb=xcb: e.tensor_scalar(
                    convo[:, co:co + CH], xcb[:, o:o + CH],
                    conv_w_sb[:, 4 * nt:4 * nt + 1], conv_b_sb[:, nt:nt + 1],
                    op0=ALU.mult, op1=ALU.add),
                    reads=[kxc, KC], writes=[("convo", nt)])
                for tap in range(1, 4):
                    p.op("vector", lambda e, o=o, co=co, nt=nt, tap=tap, xcb=xcb:
                         e.scalar_tensor_tensor(
                             convo[:, co:co + CH], xcb[:, o + tap:o + tap + CH],
                             conv_w_sb[:, 4 * nt + tap:4 * nt + tap + 1],
                             convo[:, co:co + CH], op0=ALU.mult, op1=ALU.add),
                         reads=[kxc, KC, ("convo", nt)], writes=[("convo", nt)])
            else:
                for tap in range(4):
                    wv = conv_w_sb[:, 4 * nt + tap:4 * nt + tap + 1].to_broadcast((128, 1, CH))
                    if tap == 0:
                        p.op("gpsimd", lambda e, o=o, co=co, wv=wv, xcb=xcb: e.tensor_tensor(
                            convo[:, co:co + CH], xcb[:, o:o + CH], wv, op=ALU.mult),
                            reads=[kxc, KC], writes=[("convo", nt)])
                    else:
                        p.op("gpsimd", lambda e, o=o, tap=tap, wv=wv, xcb=xcb:
                             e.tensor_tensor(t128[:, 0:CH], xcb[:, o + tap:o + tap + CH],
                                             wv, op=ALU.mult),
                             reads=[kxc, KC], writes=[("t128",)])
                        p.op("gpsimd", lambda e, co=co, nt=nt: e.tensor_tensor(
                            convo[:, co:co + CH], convo[:, co:co + CH], t128[:, 0:CH],
                            op=ALU.add),
                            reads=[("t128",), ("convo", nt)], writes=[("convo", nt)])
                p.op("gpsimd", lambda e, co=co, nt=nt: e.tensor_tensor(
                    convo[:, co:co + CH], convo[:, co:co + CH],
                    conv_b_sb[:, nt:nt + 1].to_broadcast((128, 1, CH)), op=ALU.add),
                    reads=[KC, ("convo", nt)], writes=[("convo", nt)])
            eng = "vector" if (nt % 2 == 0 or not full) else "gpsimd"
            p.op(eng, lambda e, o=o, opar=1 - par, xcb=xcb: e.tensor_copy(
                xc[opar][:, o:o + 3], xcb[:, o + CH:o + CH + 3]),
                reads=[kxc], writes=[("xc", 1 - par)])
        KCV = [("convo", nt) for nt in range(NT_XBC)]
        if full:
            for nt in range(0, NT_XBC, 6):
                p.op("tensor", lambda e: e.matmul(
                    psM[:, 448:448 + 64], ones_row_sb[:], earow[:],
                    start=True, stop=True),
                    reads=[("convo", nt), ("earow",), KC], writes=[KM])
        p.op("scalar", lambda e: e.activation(xbcs[:], convo[:], AF.Silu),
             reads=KCV, writes=[("xbcs",)])
        B_nm = xbcs[:, NT_Z * CH:(NT_Z + 1) * CH]
        C_nm = xbcs[:, (NT_Z + 1) * CH:(NT_Z + 2) * CH]
        if full:
            p.op("vector", lambda e, c=c, C_nm=C_nm: e.tensor_copy(
                Cfull_sb[:, c * CH:(c + 1) * CH], C_nm),
                reads=[("xbcs",)], writes=[("Cfull",)])

        p.op("vector", lambda e, c=c, dtbuf=dtbuf: e.tensor_scalar(
            dtc[:], dtbuf[:, c * CH:(c + 1) * CH], dtb_sb[:], None, op0=ALU.add),
            reads=[kdtbuf, KC], writes=[("dtc", par)])
        p.op("scalar", lambda e: e.activation(t64a[:], dtc[:], AF.Abs),
             reads=[("dtc", par)], writes=[("t64a",)])
        p.op("scalar", lambda e: e.activation(t64a[:], t64a[:], AF.Exp, scale=misc_sb[0:NHEADS, 1:2]),
             reads=[("t64a",), KC], writes=[("t64a",)])
        p.op("scalar", lambda e: e.activation(t64b[:], t64a[:], AF.Ln, bias=1.0),
             reads=[("t64a",)], writes=[("t64b",)])
        p.op("vector", lambda e: e.tensor_scalar_max(dtc[:], dtc[:], 0.0),
             reads=[("dtc", par), ("t64a",)], writes=[("dtc", par)])
        p.op("vector", lambda e: e.tensor_tensor(dtc[:], dtc[:], t64b[:], op=ALU.add),
             reads=[("dtc", par), ("t64b",)], writes=[("dtc", par)])
        p.op("vector", lambda e: e.tensor_scalar(
            t64a[:], dtc[:], expalog_sb[:], None, op0=ALU.mult),
            reads=[("dtc", par), KC], writes=[("t64a",)])
        p.op("vector", lambda e: e.tensor_tensor_scan(
            nacs[:], t64a[:], t64a[:], 0.0, op0=ALU.add, op1=ALU.bypass),
            reads=[("t64a",)], writes=[("nacs", par)])
        p.op("vector", lambda e: e.tensor_scalar(acs[:], nacs[:], -1.0, None, op0=ALU.mult),
             reads=[("nacs", par)], writes=[("acs", par)])
        if full:
            p.dma("scalar", "c1_acs", acs_d[:, :], acs[:], reads=[("acs", par)], writes=[("d_acs",)])
            p.op("scalar", lambda e: e.activation(expa[:], acs[:], AF.Exp),
                 reads=[("acs", par)], writes=[("expa",)])
            p.dma("scalar", "c1_ea", ea_d[c], expa[:], reads=[("expa",)], writes=[("d_ea",)])
            p.op("vector", lambda e, c=c: e.tensor_copy(Alast_sb[:, c:c + 1], acs[:, CH - 1:CH]),
                 reads=[("acs", par)], writes=[("Alast",)])
        p.op("scalar", lambda e: e.activation(decay[:], nacs[:], AF.Exp,
                                              bias=acs[:, CH - 1:CH]),
             reads=[("nacs", par), ("acs", par)], writes=[("decay", par)])
        p.op("vector", lambda e: e.tensor_tensor(ddec[:], dtc[:], decay[:], op=ALU.mult),
             reads=[("dtc", par), ("decay", par)], writes=[("ddec", par)])
        p.op("scalar", lambda e: e.activation(eacol[:], acs[:, CH - 1:CH], AF.Exp),
             reads=[("acs", par)], writes=[("eacol",)])
        p.op("tensor", lambda e: e.transpose(psM[0:1, 384:384 + NHEADS], eacol[:],
                                             identf_sb[0:NHEADS, 0:NHEADS]),
             reads=[("eacol",), KC], writes=[KM])
        p.op("vector", lambda e: e.tensor_copy(earow[:], psM[0:1, 384:384 + NHEADS]),
             reads=[KM], writes=[("earow",)])
        p.op("tensor", lambda e: e.matmul(psM[:, 256:256 + NHEADS], ones_row_sb[:],
                                          earow[:], start=True, stop=True),
             reads=[("earow",), KC], writes=[KM])
        p.op("vector", lambda e: e.tensor_copy(EAb[:], psM[:, 256:256 + NHEADS]),
             reads=[KM], writes=[("EAb", par)])
        trs = [(ddec, ddecT, ("ddec", par), ("ddecT", par))]
        if full:
            trs += [(dtc, dtT, ("dtc", par), ("dtT", par)), (nacs, nacsT, ("nacs", par), ("nacsT", par))]
        for tsrc, tdst, rk, wk in trs:
            p.op("tensor", lambda e, tsrc=tsrc: e.transpose(
                psM[:, 128:128 + NHEADS], tsrc[:], identf_sb[0:NHEADS, 0:NHEADS]),
                reads=[rk, KC], writes=[KM])
            p.op("vector", lambda e, tdst=tdst: e.tensor_copy(tdst[:], psM[:, 128:128 + NHEADS]),
                 reads=[KM], writes=[wk])

        xhg = xhg2[par]
        for grp in range(4):
            for i8 in range(8):
                jx = grp * 8 + i8
                p.op("tensor", lambda e, i8=i8, jx=jx, xbcs=xbcs: e.transpose(
                    psMb[:, i8 * 128:(i8 + 1) * 128], xbcs[:, jx * CH:(jx + 1) * CH],
                    ident_sb[:]),
                    reads=[("xbcs",), KC], writes=[KMB])
            h16 = slice(grp * 16, (grp + 1) * 16)
            p.op("scalar", lambda e, xhg=xhg: e.copy(
                xhg[:, 0:1024], psMb[:, 0:1024]),
                reads=[KMB], writes=[("xhg",)])
            if full:
                p.op("vector", lambda e, grp=grp, h16=h16, dtT=dtT, xhg=xhg: e.tensor_tensor(
                    Xp[:, grp * 1024:(grp + 1) * 1024], xhg[:, 0:1024],
                    rep_last(dtT[:, h16], HEADDIM), op=ALU.mult),
                    reads=[("xhg",), ("dtT", par)], writes=[("Xp",)])
            p.op("vector", lambda e, grp=grp, h16=h16, ddecT=ddecT, xhg=xhg: e.tensor_tensor(
                Xdec[:, grp * 1024:(grp + 1) * 1024], xhg[:, 0:1024],
                rep_last(ddecT[:, h16], HEADDIM), op=ALU.mult),
                reads=[("xhg",), ("ddecT", par)], writes=[("Xdec",)])

        if full:
            p.op("tensor", lambda e, B_nm=B_nm, C_nm=C_nm: e.matmul(
                psM[:, 0:CH], B_nm, C_nm, start=True, stop=True),
                reads=[("xbcs",)], writes=[KM])
            p.op("vector", lambda e: e.tensor_copy(Gsb[:], psM[:, 0:CH]),
                 reads=[KM], writes=[("G",)])
        p.op("tensor", lambda e, B_nm=B_nm: e.transpose(psMb[:, 0:CH], B_nm, ident_sb[:]),
             reads=[("xbcs",), KC], writes=[KMB])
        p.op("vector", lambda e: e.tensor_copy(Btm[:], psMb[:, 0:CH]),
             reads=[KMB], writes=[("Btm",)])

        for hh in range(NHEADS // HHALF):
            h0 = hh * HHALF
            if full:
                Dm = Dm2[hh % 2]
                kdm = ("Dm", hh % 2)
                p.op("vector", lambda e, h0=h0, Dm=Dm: e.tensor_tensor(
                    Dm[:], rep_mid(masktri_sb[:], HHALF),
                    rep_last(nacsT[:, h0:h0 + HHALF], CH), op=ALU.add),
                    reads=[KC, ("nacsT", par)], writes=[kdm])
                p.dma("gpsimd", f"c1_abc{hh % 2}", Dm[:],
                      acs_d.rearrange("h l -> (h l)").unsqueeze(0)[:, h0 * CH:(h0 + HHALF) * CH]
                      .to_broadcast((128, HHALF * CH)),
                      reads=[("d_acs",), kdm], writes=[kdm],
                      accum_op=ALU.add)
                p.dma("sync", "c1_eabc", eabc[:],
                      ea_d[c].rearrange("h l -> (h l)").unsqueeze(0)[:, h0 * CH:(h0 + HHALF) * CH]
                      .to_broadcast((128, HHALF * CH)),
                      reads=[("d_ea",)], writes=[("eabc",)])
                p.op("scalar", lambda e, Dm=Dm: e.activation(Lh[:], Dm[:], AF.Exp),
                     reads=[kdm], writes=[("Lh",)])
                p.op("vector", lambda e: e.tensor_tensor(
                    Mth[:], Lh[:], rep_mid(Gsb[:], HHALF), op=ALU.mult),
                    reads=[("Lh",), ("G",)], writes=[("Mth",)])
                p.op("gpsimd", lambda e, C_nm=C_nm: e.tensor_tensor(
                    Cexp[:], eabc[:], rep_mid(C_nm, HHALF), op=ALU.mult),
                    reads=[("eabc",), ("xbcs",)], writes=[("Cexp",)])

                for jj in range(HHALF // 2):
                    j = hh * (HHALF // 2) + jj
                    bank, kb = psY[jj % 2], ("psY", jj % 2)
                    for u in range(2):
                        h = 2 * j + u
                        lh = h - h0
                        rows = slice(64 * u, 64 * u + 64)
                        p.op("tensor", lambda e, bank=bank, rows=rows, h=h, lh=lh:
                             e.matmul(bank[rows, 0:CH], Xp[:, h * 64:h * 64 + 64],
                                      Mth[:, lh * CH:(lh + 1) * CH], start=True, stop=False),
                             reads=[("Xp",), ("Mth",)], writes=[kb])
                        p.op("tensor", lambda e, bank=bank, rows=rows, h=h, lh=lh:
                             e.matmul(bank[rows, 0:CH], Sb_sb[:, h * 64:h * 64 + 64],
                                      Cexp[:, lh * CH:(lh + 1) * CH], start=False, stop=True),
                             reads=[("Sb",), ("Cexp",)], writes=[kb])
                    p.op("vector", lambda e, j=j, bank=bank: e.scalar_tensor_tensor(
                        ysb[:, j * CH:(j + 1) * CH], xbcs[:, j * CH:(j + 1) * CH],
                        D_cols_sb[:, j:j + 1], bank[:, 0:CH], op0=ALU.mult, op1=ALU.add),
                        reads=[("xbcs",), kb, KC], writes=[("ysb",)])

            for g in range(HHALF // 8):
                gg = hh * (HHALF // 8) + g
                bank, kb = psS[gg % 2], ("psS", gg % 2)
                hbase = h0 + g * 8
                for i in range(8):
                    h = hbase + i
                    p.op("tensor", lambda e, bank=bank, i=i, h=h: e.matmul(
                        bank[:, i * 64:(i + 1) * 64], Btm[:],
                        Xdec[:, h * 64:(h + 1) * 64], start=True, stop=True),
                        reads=[("Btm",), ("Xdec",)], writes=[kb])
                base = hbase * 64
                p.op("vector", lambda e, base=base, hbase=hbase: e.tensor_tensor(
                    S_sb[:, base:base + 512], S_sb[:, base:base + 512],
                    rep_last(EAb[:, hbase:hbase + 8], 64), op=ALU.mult),
                    reads=[("S",), ("EAb", par)], writes=[("S",)])
                p.op("vector", lambda e, base=base, bank=bank: e.tensor_tensor(
                    S_sb[:, base:base + 512], S_sb[:, base:base + 512],
                    bank[:, 0:512], op=ALU.add),
                    reads=[("S",), kb], writes=[("S",)])
        if full:
            p.op("vector", lambda e: e.tensor_copy(Sb_sb[:], S_sb[:]),
                 reads=[("S",)], writes=[("Sb",)])
            p.dma("sync", "c1_y", y_d[c], ysb[:], reads=[("ysb",)], writes=[("d_y", c)])

    for kt in range(16):
        p.dma_tr("sync", "a2_lnf", ln1f[:, kt * T:kt * T + T],
                 ln1p_d[0:T, kt * 128:(kt + 1) * 128],
                 reads=[("d_ln1p",)], writes=[("ln1f", kt)])
    for nt in range(NT_Z, NT_IN):
        emit_a2_tile(nt, False, xbcp_d, ("d_xbcp",), dtp_sb, ("dtp_sb",), None)

    for kt in range(16):
        p.dma_tr("sync", "a2_lnf", ln1f[:, kt * T:kt * T + T],
                 ln1_d[0:T, kt * 128:(kt + 1) * 128],
                 reads=[("d_ln1",)], writes=[("ln1f", kt)])
        p.dma_tr("sync", "a2_lnf", lnctxf[:, kt * 128:(kt + 1) * 128],
                 ln1_d[T:T + 128, kt * 128:(kt + 1) * 128],
                 reads=[("d_ln1",)], writes=[("ln1f", kt)])

    p.op("vector", lambda e: e.memset(S_sb[:], 0.0), writes=[("S",)])
    p.op("vector", lambda e: e.memset(
        xc[0][:].rearrange("p (nt w) -> p nt w", w=CHW)[:, :, 0:3], 0.0),
        writes=[("xc", 0)])
    ti = 0
    for c in range(NCH):
        quota = (NT_IN * (c + 1) + NCH - 1) // NCH
        while ti < min(quota, NT_IN):
            emit_a2_tile(ti, True, xbc_d, ("d_xbc",), dt_sb, ("dt_sb",), z_d)
            ti += 1
        emit_chunk(c, False, xbcp_d, ("d_xbcp",), dtp_sb, ("dtp_sb",))
    while ti < NT_IN:
        emit_a2_tile(ti, True, xbc_d, ("d_xbc",), dt_sb, ("dt_sb",), z_d)
        ti += 1
    p.op("vector", lambda e: e.tensor_copy(S0b_sb[:], S_sb[:]),
         reads=[("S",)], writes=[("S0b",)])

    p.barrier()
    a2s.close()
    pha1.close()
    msp.close()

    c1x = ExitStack()
    Xp = sbuf(c1x, "Xp", (128, D_INNER), BF16)
    Dm2 = [sbuf(c1x, f"Dm{i}", (128, HHALF * CH), F32) for i in range(2)]
    Lh = sbuf(c1x, "Lh", (128, HHALF * CH), BF16)
    Mth = sbuf(c1x, "Mth", (128, HHALF * CH), BF16)
    eabc = sbuf(c1x, "eabc", (128, HHALF * CH), BF16)
    Cexp = sbuf(c1x, "Cexp", (128, HHALF * CH), BF16)
    ysb = sbuf(c1x, "ysb", (128, NT_Z * CH), BF16)
    Sb_sb = sbuf(c1x, "Sb_sb", (128, NHEADS * HEADDIM), BF16)
    psY = [c1x.enter_context(nc.psum_tensor(f"psY{i}", [128, 512], F32)) for i in range(2)]

    p.op("vector", lambda e: e.memset(S_sb[:], 0.0), reads=[("S0b",)], writes=[("S",)])
    p.op("vector", lambda e: e.memset(Sb_sb[:], 0.0), writes=[("Sb",)])
    p.op("vector", lambda e: e.tensor_copy(
        xc[0][:].rearrange("p (nt w) -> p nt w", w=CHW)[:, :, 0:3],
        ctx3_sb[:].rearrange("p (nt w) -> p nt w", w=3)),
        reads=[("ctx3",)], writes=[("xc", 0)])
    for c in range(NCH):
        emit_chunk(c, True, xbc_d, ("d_xbc",), dt_sb, ("dt_sb",))

    p.op("vector", lambda e: e.tensor_tensor_scan(
        cumprev_sb[:, 1:NCH], Alast_sb[:, 0:NCH - 1], Alast_sb[:, 0:NCH - 1], 0.0,
        op0=ALU.add, op1=ALU.bypass),
        reads=[("Alast",)], writes=[("cumprev",)])
    p.op("vector", lambda e: e.memset(cumprev_sb[:, 0:1], 0.0), writes=[("cumprev",)])

    p.barrier()
    c1x.close()
    c1s.close()
    ms1.close()

    ph = ExitStack()
    yn_sb = sbuf(ph, "yn_sb", (128, NT_Z * HALF_T), BF16)
    ez = sbuf(ph, "ez", (NHEADS, CH), BF16)
    edc = sbuf(ph, "edc", (NHEADS, 1), F32)
    scc = sbuf(ph, "scc", (NHEADS, CH), BF16)
    scbc = sbuf(ph, "scbc", (128, HHALF * CH), BF16)
    Cph = sbuf(ph, "Cph", (128, HHALF * CH), BF16)
    yc2 = sbuf(ph, "yc2", (128, NT_Z * CH), F32)
    zc2 = sbuf(ph, "zc2", (128, NT_Z * CH), BF16)
    slz = sbuf(ph, "slz", (128, NT_Z * CH), F32)
    sqz = sbuf(ph, "sqz", (128, NT_Z * CH), F32)
    rrow = sbuf(ph, "rrow", (1, CH), F32)
    psC = [ph.enter_context(nc.psum_tensor(f"psC{i}", [128, 512], F32)) for i in range(2)]
    psR = ph.enter_context(nc.psum_tensor("psR", [128, 1024], F32))
    wD = [sbuf(ph, f"wD{i}", (128, 32 * 128), BF16) for i in range(2)]
    x0D2 = [sbuf(ph, f"x0D{i}", (128, HALF_T), F32) for i in range(2)]
    lnD2 = [sbuf(ph, f"lnD{i}", (128, HALF_T), BF16) for i in range(2)]
    x1D2 = [sbuf(ph, f"x1D{i}", (128, HALF_T), F32) for i in range(2)]
    psD = [ph.enter_context(nc.psum_tensor(f"psD{i}", [128, 512], F32)) for i in range(4)]
    DSL = slices(HALF_T, 512)

    def emit_d_group(half, g):
        n0g, n1g = g * 512, min(HALF_T, (g + 1) * 512)
        W = n1g - n0g
        for dt in range(16):
            par = dt % 2
            wb = wD[par]
            kw = ("wD", par)
            p.dma("gpsimd", f"d_w{par}", wb[:], w_out_r[dt].rearrange("f k d -> f (k d)"),
                  writes=[kw])
            ps = psD[par * 2 + (g % 2)]
            pkk = ("psD", par * 2 + (g % 2))
            for kt in range(32):
                p.op("tensor", lambda e, wb=wb, kt=kt, ps=ps:
                     e.matmul(ps[:, 0:W], wb[:, kt * 128:(kt + 1) * 128],
                              yn_sb[:, kt * HALF_T + n0g:kt * HALF_T + n1g],
                              start=(kt == 0), stop=(kt == 31)),
                     reads=[kw, ("yn", g)], writes=[pkk])
            x0D, lnD, x1D = x0D2[par], lnD2[par], x1D2[par]
            t0a = half * HALF_T + n0g
            p.dma("sync", f"d_x0{par}", x0D[:, 0:W],
                  x_fm[dt * 128:(dt + 1) * 128, t0a:t0a + W],
                  writes=[("x0D", par)])
            p.dma_tr("sync", f"d_ln{par}", lnD[:, 0:W],
                     ln1_d[t0a:t0a + W, dt * 128:(dt + 1) * 128],
                     reads=[("d_ln1",)], writes=[("lnD", par)])
            p.op("vector", lambda e, x1D=x1D, x0D=x0D, lnD=lnD: e.tensor_tensor(
                x1D[:, 0:W], x0D[:, 0:W], lnD[:, 0:W], op=ALU.add),
                reads=[("x0D", par), ("lnD", par)], writes=[("x1D", par)])
            p.op("vector", lambda e, ps=ps, x1D=x1D: e.tensor_tensor(
                x1D[:, 0:W], x1D[:, 0:W], ps[:, 0:W], op=ALU.add),
                reads=[("x1D", par), pkk], writes=[("x1D", par)])
            p.dma("scalar", f"d_x1{par}",
                  x1_d[dt][:, t0a:t0a + W], x1D[:, 0:W],
                  reads=[("x1D", par)], writes=[("d_x1",)])

    for half in range(2):
        for c in range(half * (NCH // 2), (half + 1) * (NCH // 2)):
            zrel = c - half * (NCH // 2)
            p.dma("sync", "c2_ez", ez[:], ea_d[c], reads=[("d_ea",)], writes=[("ez",)])
            p.op("scalar", lambda e, c=c: e.activation(edc[:], cumprev_sb[:, c:c + 1], AF.Exp),
                 reads=[("cumprev",)], writes=[("edc",)])
            p.op("vector", lambda e: e.tensor_tensor(edc[:], edc[:], mask2_sb[0:NHEADS, :],
                                                     op=ALU.mult),
                 reads=[("edc",), KC], writes=[("edc",)])
            p.op("vector", lambda e: e.tensor_scalar(scc[:], ez[:], edc[:], None, op0=ALU.mult),
                 reads=[("ez",), ("edc",)], writes=[("scc",)])
            p.dma("scalar", "c2_sc", sc_d[:, :], scc[:], reads=[("scc",)], writes=[("d_sc",)])
            p.dma("gpsimd", "c2_y", yc2[:], y_d[c], reads=[("d_y", c)], writes=[("yc2",)])
            for hh in range(NHEADS // HHALF):
                h0 = hh * HHALF
                p.dma("sync", "c2_scbc", scbc[:],
                      sc_d.rearrange("h l -> (h l)").unsqueeze(0)[:, h0 * CH:(h0 + HHALF) * CH]
                      .to_broadcast((128, HHALF * CH)),
                      reads=[("d_sc",)], writes=[("scbc",)])
                p.op("vector", lambda e, c=c: e.tensor_tensor(
                    Cph[:], scbc[:], rep_mid(Cfull_sb[:, c * CH:(c + 1) * CH], HHALF),
                    op=ALU.mult), reads=[("scbc",), ("Cfull",)], writes=[("Cph",)])
                for jj in range(HHALF // 2):
                    j = hh * (HHALF // 2) + jj
                    bank, kb = psC[jj % 2], ("psC", jj % 2)
                    for u in range(2):
                        h = 2 * j + u
                        lh = h - h0
                        rows = slice(64 * u, 64 * u + 64)
                        p.op("tensor", lambda e, bank=bank, rows=rows, h=h, lh=lh:
                             e.matmul(bank[rows, 0:CH], S0b_sb[:, h * 64:h * 64 + 64],
                                      Cph[:, lh * CH:(lh + 1) * CH], start=True, stop=True),
                             reads=[("S0b",), ("Cph",)], writes=[kb])
                    p.op("vector", lambda e, j=j, bank=bank: e.tensor_tensor(
                        yc2[:, j * CH:(j + 1) * CH], yc2[:, j * CH:(j + 1) * CH],
                        bank[:, 0:CH], op=ALU.add),
                        reads=[("yc2",), kb], writes=[("yc2",)])
            p.dma("sync", "c2_z", zc2[:].rearrange("p (nt l) -> p nt l", l=CH),
                  z_d.rearrange("(nt p) t -> p nt t", p=128)[:, :, c * CH:(c + 1) * CH],
                  reads=[("d_z",)], writes=[("zc2",)])
            p.op("scalar", lambda e: e.activation(slz[:], zc2[:], AF.Silu),
                 reads=[("zc2",)], writes=[("slz",)])
            p.op("vector", lambda e: e.tensor_tensor(yc2[:], yc2[:], slz[:], op=ALU.mult),
                 reads=[("yc2",), ("slz",)], writes=[("yc2",)])
            p.op("scalar", lambda e: e.activation(sqz[:], yc2[:], AF.Square),
                 reads=[("yc2",)], writes=[("sqz",)])
            for nt in range(NT_Z):
                p.op("tensor", lambda e, nt=nt: e.matmul(
                    psR[0:1, 0:CH], ones_col_sb[:], sqz[:, nt * CH:(nt + 1) * CH],
                    start=(nt == 0), stop=(nt == NT_Z - 1)),
                    reads=[("sqz",), KC], writes=[("psRs",)])
            p.op("scalar", lambda e: e.activation(rrow[:], psR[0:1, 0:CH], AF.Ln,
                                                  bias=misc_sb[0:1, 0:1],
                                                  scale=misc_sb[0:1, 2:3]),
                 reads=[("psRs",), KC], writes=[("rrow",)])
            p.op("scalar", lambda e: e.activation(rrow[:], rrow[:], AF.Exp,
                                                  scale=misc_sb[0:1, 3:4]),
                 reads=[("rrow",), KC], writes=[("rrow",)])
            p.op("tensor", lambda e: e.matmul(psR[:, 512:512 + CH], ones_row_sb[:],
                                              rrow[:], start=True, stop=True),
                 reads=[("rrow",), KC], writes=[("psRb",)])
            for nt in range(NT_Z):
                p.op("vector", lambda e, nt=nt, zrel=zrel: e.scalar_tensor_tensor(
                    yn_sb[:, nt * HALF_T + zrel * CH:nt * HALF_T + (zrel + 1) * CH],
                    yc2[:, nt * CH:(nt + 1) * CH], normw_sb[:, nt:nt + 1],
                    psR[:, 512:512 + CH], op0=ALU.mult, op1=ALU.mult),
                    reads=[("yc2",), ("psRb",), KC], writes=[("yn", zrel // 4)])
            if ((zrel + 1) * CH) % 512 == 0 or zrel == NCH // 2 - 1:
                if ((zrel + 1) * CH) % 512 == 0 or ((zrel + 1) * CH) == HALF_T:
                    emit_d_group(half, zrel * CH // 512)


    p.barrier()
    ph.close()
    ms2.close()

    ph = ExitStack()
    x2acc = sbuf(ph, "x2acc", (128, 16 * LG), F32)
    ln2 = sbuf(ph, "ln2", (128, 16 * LG), BF16)
    hsb = sbuf(ph, "hsb", (128, 16 * LG), BF16)
    sqE = [sbuf(ph, f"sqE{i}", (128, LG), F32) for i in range(2)]
    gkE = sbuf(ph, "gkE", (128, 16 * 8), BF16)
    rowsE = sbuf(ph, "rowsE", (1, 3 * LG), F32)
    abE = sbuf(ph, "abE", (128, 2 * LG), F32)
    g8 = sbuf(ph, "g8", (128, 8), F32)
    m8 = sbuf(ph, "m8", (128, 8), F32)
    maskE = sbuf(ph, "maskE", (128, 8), F32)
    eallE = sbuf(ph, "eallE", (128, 8), F32)
    em2E = sbuf(ph, "em2E", (128, 1), F32)
    denE = sbuf(ph, "denE", (128, 1), F32)
    negm1E = sbuf(ph, "negm1E", (128, 1), F32)
    cwE = sbuf(ph, "cwE", (128, 8), F32)
    cwTE = sbuf(ph, "cwTE", (8, LG), F32)
    cwrow = sbuf(ph, "cwrow", (1, LG), F32)
    cwb = sbuf(ph, "cwb", (128, LG), F32)
    usb = sbuf(ph, "usb", (128, LG), F32)
    usb2 = sbuf(ph, "usb2", (128, LG), F32)
    ssb = sbuf(ph, "ssb", (128, LG), BF16)
    ssb2 = sbuf(ph, "ssb2", (128, LG), BF16)
    wiA = [sbuf(ph, f"wiA{i}", (128, 16 * 128), BF16) for i in range(2)]
    wiB = [sbuf(ph, f"wiB{i}", (128, 16 * 128), BF16) for i in range(2)]
    woE = [sbuf(ph, f"woE{i}", (128, 16 * 128), BF16) for i in range(2)]
    psH0 = ph.enter_context(nc.psum_tensor("psH0", [128, 1024], F32))
    psH1 = ph.enter_context(nc.psum_tensor("psH1", [128, 1024], F32))
    psW = ph.enter_context(nc.psum_tensor("psW", [128, 1024], F32))
    psE = ph.enter_context(nc.psum_tensor("psE", [128, 1024], F32))
    KH0, KH1, KW_, KE = ("psH0",), ("psH1",), ("psW",), ("psE",)
    LSL = slices(LG, 512)

    p.dma("gpsimd", "e_gk", gkE[:], gk_r.rearrange("p k e -> p (k e)"), writes=[("gkE",)])

    for lg in range(NLG):
        t0 = lg * LG
        for dt in range(16):
            p.dma("sync", "e_x1", x2acc[:, dt * LG:(dt + 1) * LG],
                  x1_d[dt][:, t0:t0 + LG], reads=[("d_x1",)], writes=[("x2a", dt)])
        for dt in range(16):
            par = dt % 2
            p.op("scalar", lambda e, dt=dt, par=par: e.activation(
                sqE[par][:], x2acc[:, dt * LG:(dt + 1) * LG], AF.Square),
                reads=[("x2a", dt)], writes=[("sqE", par)])
            for (n0, n1) in LSL:
                p.op("tensor", lambda e, dt=dt, n0=n0, n1=n1: e.matmul(
                    psE[0:1, n0:n1], ones_col_sb[:], x2acc[:, dt * LG + n0:dt * LG + n1],
                    start=(dt == 0), stop=(dt == 15)),
                    reads=[("x2a", dt), KC], writes=[KE])
                p.op("tensor", lambda e, par=par, n0=n0, n1=n1, dt=dt: e.matmul(
                    psW[0:1, n0:n1], ones_col_sb[:], sqE[par][:, n0:n1],
                    start=(dt == 0), stop=(dt == 15)),
                    reads=[("sqE", par), KC], writes=[KW_])
        p.op("vector", lambda e: e.tensor_scalar(
            rowsE[0:1, 0:LG], psE[0:1, 0:LG], 1.0 / D_MODEL, None, op0=ALU.mult),
            reads=[KE], writes=[("rowsE",)])
        p.op("vector", lambda e: e.tensor_scalar(
            rowsE[0:1, LG:2 * LG], psW[0:1, 0:LG], 1.0 / D_MODEL, None, op0=ALU.mult),
            reads=[KW_, ("rowsE",)], writes=[("rowsE",)])
        p.op("vector", lambda e: e.tensor_tensor(
            rowsE[0:1, 2 * LG:3 * LG], rowsE[0:1, 0:LG], rowsE[0:1, 0:LG], op=ALU.mult),
            reads=[("rowsE",)], writes=[("rowsE",)])
        p.op("vector", lambda e: e.tensor_tensor(
            rowsE[0:1, LG:2 * LG], rowsE[0:1, LG:2 * LG], rowsE[0:1, 2 * LG:3 * LG],
            op=ALU.subtract), reads=[("rowsE",)], writes=[("rowsE",)])
        p.op("scalar", lambda e: e.activation(rowsE[0:1, LG:2 * LG],
                                              rowsE[0:1, LG:2 * LG], AF.Sqrt,
                                              bias=misc_sb[0:1, 0:1]),
             reads=[("rowsE",), KC], writes=[("rowsE",)])
        p.op("vector", lambda e: e.reciprocal(rowsE[0:1, LG:2 * LG], rowsE[0:1, LG:2 * LG]),
             reads=[("rowsE",)], writes=[("rowsE",)])
        p.op("vector", lambda e: e.tensor_tensor(
            rowsE[0:1, 0:LG], rowsE[0:1, 0:LG], rowsE[0:1, LG:2 * LG], op=ALU.mult),
            reads=[("rowsE",)], writes=[("rowsE",)])
        for (n0, n1) in LSL:
            p.op("tensor", lambda e, n0=n0, n1=n1: e.matmul(
                psE[:, n0:n1], ones_row_sb[:], rowsE[0:1, LG + n0:LG + n1],
                start=True, stop=True), reads=[("rowsE",), KC], writes=[KE])
            p.op("vector", lambda e, n0=n0, n1=n1: e.tensor_copy(
                abE[:, n0:n1], psE[:, n0:n1]), reads=[KE], writes=[("abE",)])
            p.op("tensor", lambda e, n0=n0, n1=n1: e.matmul(
                psW[:, n0:n1], ones_row_sb[:], rowsE[0:1, n0:n1],
                start=True, stop=True), reads=[("rowsE",), KC], writes=[KW_])
            p.op("vector", lambda e, n0=n0, n1=n1: e.tensor_copy(
                abE[:, LG + n0:LG + n1], psW[:, n0:n1]), reads=[KW_], writes=[("abE",)])
        for dt in range(16):
            par = dt % 2
            p.op("vector", lambda e, dt=dt, par=par: e.tensor_tensor(
                sqE[par][:], x2acc[:, dt * LG:(dt + 1) * LG], abE[:, 0:LG], op=ALU.mult),
                reads=[("x2a", dt), ("abE",)], writes=[("sqE", par)])
            p.op("vector", lambda e, dt=dt, par=par: e.tensor_tensor(
                ln2[:, dt * LG:(dt + 1) * LG], sqE[par][:], abE[:, LG:2 * LG],
                op=ALU.subtract),
                reads=[("sqE", par), ("abE",)], writes=[("ln2", dt)])
        KLN2 = [("ln2", dt) for dt in range(16)]
        for st in range(LG // 128):
            for kt in range(16):
                p.op("tensor", lambda e, kt=kt, st=st: e.matmul(
                    psE[:, 0:8], ln2[:, kt * LG + st * 128:kt * LG + (st + 1) * 128],
                    gkE[:, kt * 8:(kt + 1) * 8], start=(kt == 0), stop=(kt == 15)),
                    reads=[KLN2[kt], ("gkE",)], writes=[KE])
            p.op("vector", lambda e: e.tensor_copy(g8[:], psE[:, 0:8]),
                 reads=[KE], writes=[("g8",)])
            p.op("vector", lambda e: e.max(out=m8[:], in_=g8[:]),
                 reads=[("g8",)], writes=[("m8",)])
            p.op("vector", lambda e: e.tensor_scalar(maskE[:], g8[:], m8[:, 1:2], None,
                                                     op0=ALU.is_ge),
                 reads=[("g8",), ("m8",)], writes=[("maskE",)])
            p.op("vector", lambda e: e.tensor_scalar(negm1E[:], m8[:, 0:1], -1.0, None,
                                                     op0=ALU.mult),
                 reads=[("m8",)], writes=[("negm1",)])
            p.op("scalar", lambda e: e.activation(eallE[:], g8[:], AF.Exp, bias=negm1E[:]),
                 reads=[("g8",), ("negm1",)], writes=[("eall",)])
            p.op("scalar", lambda e: e.activation(em2E[:], m8[:, 1:2], AF.Exp, bias=negm1E[:]),
                 reads=[("m8",), ("negm1",)], writes=[("em2",)])
            p.op("vector", lambda e: e.tensor_scalar(denE[:], em2E[:], 1.0, None, op0=ALU.add),
                 reads=[("em2",)], writes=[("den",)])
            p.op("vector", lambda e: e.reciprocal(denE[:], denE[:]),
                 reads=[("den",)], writes=[("den",)])
            p.op("vector", lambda e: e.scalar_tensor_tensor(
                cwE[:], eallE[:], denE[:], maskE[:], op0=ALU.mult, op1=ALU.mult),
                reads=[("eall",), ("den",), ("maskE",)], writes=[("cwE",)])
            p.op("tensor", lambda e: e.transpose(psE[0:8, 512:512 + 128], cwE[:],
                                                 identf_sb[:]),
                 reads=[("cwE",), KC], writes=[KE])
            p.op("vector", lambda e, st=st: e.tensor_copy(
                cwTE[:, st * 128:(st + 1) * 128], psE[0:8, 512:512 + 128]),
                reads=[KE], writes=[("cwT",)])
        p.dma("scalar", "e_cw", cw_d[:, t0:t0 + LG], cwTE[:],
              reads=[("cwT",)], writes=[("d_cw",)])

        for ex in range(8):
            p.dma("sync", "e_cwrow", cwrow[:], cw_d[ex:ex + 1, t0:t0 + LG],
                  reads=[("d_cw",)], writes=[("cwrow",)])
            for (n0, n1) in LSL:
                p.op("tensor", lambda e, n0=n0, n1=n1: e.matmul(
                    psE[:, n0:n1], ones_row_sb[:], cwrow[0:1, n0:n1],
                    start=True, stop=True), reads=[("cwrow",), KC], writes=[KE])
                p.op("vector", lambda e, n0=n0, n1=n1: e.tensor_copy(
                    cwb[:, n0:n1], psE[:, n0:n1]), reads=[KE], writes=[("cwb",)])
            for mt in range(16):
                par = mt % 2
                ph0, kh0 = (psH0, KH0) if par == 0 else (psW, KW_)
                ph1, kh1 = (psH1, KH1) if par == 0 else (psE, KE)
                p.dma("gpsimd", f"e_wi0_{par}", wiA[par][:],
                      wi0_r[ex, mt].rearrange("d k m -> d (k m)"), writes=[("wiA", par)])
                p.dma("gpsimd", f"e_wi1_{par}", wiB[par][:],
                      wi1_r[ex, mt].rearrange("d k m -> d (k m)"), writes=[("wiB", par)])
                for kt in range(16):
                    for (n0, n1) in LSL:
                        p.op("tensor", lambda e, par=par, kt=kt, n0=n0, n1=n1, ph0=ph0:
                             e.matmul(ph0[:, n0:n1], wiA[par][:, kt * 128:(kt + 1) * 128],
                                      ln2[:, kt * LG + n0:kt * LG + n1],
                                      start=(kt == 0), stop=(kt == 15)),
                             reads=[("wiA", par), KLN2[kt]], writes=[kh0])
                        p.op("tensor", lambda e, par=par, kt=kt, n0=n0, n1=n1, ph1=ph1:
                             e.matmul(ph1[:, n0:n1], wiB[par][:, kt * 128:(kt + 1) * 128],
                                      ln2[:, kt * LG + n0:kt * LG + n1],
                                      start=(kt == 0), stop=(kt == 15)),
                             reads=[("wiB", par), KLN2[kt]], writes=[kh1])
                usbp, kusb = (usb, ("usb", 0)) if par == 0 else (usb2, ("usb", 1))
                ssbp, kssb = (ssb, ("ssb", 0)) if par == 0 else (ssb2, ("ssb", 1))
                p.op("vector", lambda e, ph1=ph1, usbp=usbp: e.tensor_tensor(
                    usbp[:], ph1[:, 0:LG], cwb[:], op=ALU.mult),
                    reads=[kh1, ("cwb",)], writes=[kusb])
                p.op("scalar", lambda e, ph0=ph0, ssbp=ssbp: e.activation(
                    ssbp[:], ph0[:, 0:LG], AF.Silu),
                    reads=[kh0], writes=[kssb])
                p.op("vector", lambda e, mt=mt, usbp=usbp, ssbp=ssbp: e.tensor_tensor(
                    hsb[:, mt * LG:(mt + 1) * LG], usbp[:], ssbp[:], op=ALU.mult),
                    reads=[kusb, kssb], writes=[("hsb", mt)])
            for dt in range(16):
                par = dt % 2
                pw, kw2 = (psH0, KH0) if par == 0 else (psW, KW_)
                p.dma("gpsimd", f"e_wo_{par}", woE[par][:],
                      wo_r[ex, dt].rearrange("m t d -> m (t d)"), writes=[("woE", par)])
                for mt in range(16):
                    for (n0, n1) in LSL:
                        p.op("tensor", lambda e, par=par, mt=mt, n0=n0, n1=n1, pw=pw:
                             e.matmul(pw[:, n0:n1], woE[par][:, mt * 128:(mt + 1) * 128],
                                      hsb[:, mt * LG + n0:mt * LG + n1],
                                      start=(mt == 0), stop=(mt == 15)),
                             reads=[("woE", par), ("hsb", mt)], writes=[kw2])
                for (n0, n1) in LSL:
                    p.op("vector", lambda e, dt=dt, n0=n0, n1=n1, pw=pw: e.tensor_tensor(
                        x2acc[:, dt * LG + n0:dt * LG + n1],
                        x2acc[:, dt * LG + n0:dt * LG + n1], pw[:, n0:n1], op=ALU.add),
                        reads=[("x2a", dt), kw2], writes=[("x2a", dt)])
        for dt in range(16):
            p.dma("scalar", "e_out", out_d[dt][:, t0:t0 + LG],
                  x2acc[:, dt * LG:(dt + 1) * LG],
                  reads=[("x2a", dt)], writes=[("d_out",)])

    p.fence("gpsimd", [("d_out",)])
    p.build()
    return nc


def kernel(**inputs):
    T = 2048
    in_maps = host_layouts(inputs, T)
    nc = build_program(T)
    res = run_bass_kernel_spmd(nc, in_maps, core_ids=list(range(8)))
    out = np.zeros((4, 4096, 2048), np.float32)
    for core in range(8):
        b, half = core // 2, core % 2
        o = np.asarray(res.results[core]["out_fm"])
        out[b, half * T:(half + 1) * T, :] = o.reshape(2048, T).T
    return out


# version 13
# speedup vs baseline: 1.1646x; 1.0289x over previous
import numpy as np
import ml_dtypes
from contextlib import ExitStack

import concourse.bass as bass
import concourse.mybir as mybir
from concourse.bass_utils import run_bass_kernel_spmd

F32 = mybir.dt.float32
BF16 = mybir.dt.bfloat16
AF = mybir.ActivationFunctionType
ALU = mybir.AluOpType
bfloat16 = ml_dtypes.bfloat16

D_MODEL = 2048
HEADDIM = 64
CH = 128
D_INNER = 4096
NHEADS = 64
NPAIR = 32
CONV_DIM = 4352
D_IN_PROJ = 8512
NT_Z = 32
NT_XBC = 34
NT_IN = 67
EPS = 1e-5
NEG_BIG = -1e9
HHALF = 16

ENGS = ("tensor", "vector", "scalar", "gpsimd", "sync")


def rep_mid(ap2d, reps):
    return ap2d.unsqueeze(1).to_broadcast((ap2d.shape[0], reps, ap2d.shape[1]))


def rep_last(ap2d, reps):
    return ap2d.to_broadcast((ap2d.shape[0], ap2d.shape[1], reps))


def slices(total, step):
    out = []
    o = 0
    while o < total:
        out.append((o, min(total, o + step)))
        o += step
    return out


class Prog:
    def __init__(self, nc):
        self.nc = nc
        self.es = ExitStack()
        self.sems = {}
        self.count = {}
        self.watermark = {}
        self.last_write = {}
        self.readers = {}
        self.ops = []
        for e in ENGS:
            self._new_sem(e)

    def _new_sem(self, name):
        self.sems[name] = self.es.enter_context(self.nc.semaphore("sem_" + name))
        self.count[name] = 0

    def dma_group(self, name):
        if name not in self.sems:
            self._new_sem(name)
        return name

    def _need_waits(self, eng, reads, writes):
        need = {}

        def add(sem, val):
            if val > need.get(sem, 0):
                need[sem] = val

        for r in reads:
            lw = self.last_write.get(r)
            if lw:
                add(*lw)
        for w in writes:
            lw = self.last_write.get(w)
            if lw:
                add(*lw)
            for sem, val in self.readers.get(w, {}).items():
                add(sem, val)
        out = []
        for sem, val in need.items():
            if sem == eng == "tensor":
                continue
            if sem not in ENGS:
                val = self.count[sem]
            wm = self.watermark.get((eng, sem), 0)
            if val > wm:
                out.append((sem, val))
                self.watermark[(eng, sem)] = val
        return out

    def op(self, eng, emit_fn, reads=(), writes=(), tag=None):
        waits = self._need_waits(eng, reads, writes)
        self.count[eng] += 1
        v = self.count[eng]
        self.ops.append((eng, waits, emit_fn, eng, 1, tag))
        for w in writes:
            self.last_write[w] = (eng, v)
            self.readers[w] = {}
        for r in reads:
            self.readers.setdefault(r, {})[eng] = v

    def _dma_common(self, q, group, emit, reads, writes, tag):
        g = self.dma_group(group)
        waits = self._need_waits(q, reads, writes)
        self.count[g] += 16
        v = self.count[g]
        self.ops.append((q, waits, emit, g, 16, tag))
        for w in writes:
            self.last_write[w] = (g, v)
            self.readers[w] = {}
        for r in reads:
            self.readers.setdefault(r, {})[g] = v

    def dma(self, q, group, out, in_, reads=(), writes=(), tag=None, **kw):
        def emit(e, out=out, in_=in_, kw=kw):
            return e.dma_start(out=out, in_=in_, **kw)
        self._dma_common(q, group, emit, reads, writes, tag)

    def dma_tr(self, q, group, out, in_, reads=(), writes=(), tag=None):
        def emit(e, out=out, in_=in_):
            return e.dma_start_transpose(out=out, in_=in_)
        self._dma_common(q, group, emit, reads, writes, tag)

    def collective(self, emit_fn, reads=(), writes=(), tag="cc"):
        self._dma_common("gpsimd", "cc", emit_fn, reads, writes, tag)

    def nop_marker(self, eng, reads, writes):
        self.op(eng, lambda e: e.engine_nop(), reads=reads, writes=writes)

    def barrier(self):
        snap = dict(self.count)
        for e in ENGS:
            waits = []
            for sem, val in snap.items():
                if sem == e or val <= 0:
                    continue
                wm = self.watermark.get((e, sem), 0)
                if val > wm:
                    waits.append((sem, val))
                    self.watermark[(e, sem)] = val
            if waits:
                self.ops.append((e, waits, None, None, 0, "barrier"))

    def fence(self, eng, keys):
        waits = self._need_waits(eng, keys, ())
        if waits:
            self.ops.append((eng, waits, None, None, 0, "fence"))

    def build(self):
        per_eng = {}
        for rec in self.ops:
            per_eng.setdefault(rec[0], []).append(rec)
        block = self.es.enter_context(self.nc.Block())
        for eng_name in ENGS:
            recs = per_eng.get(eng_name, [])
            if not recs:
                continue
            section = getattr(block, eng_name)

            def body(engine, recs=recs):
                for (_e, waits, emit_fn, inc_sem, inc_val, _tag) in recs:
                    for sem, val in waits:
                        engine.wait_ge(self.sems[sem], val)
                    if emit_fn is not None:
                        inst = emit_fn(engine)
                        if inc_sem is not None:
                            inst.then_inc(self.sems[inc_sem], inc_val)

            section(body)



def host_layouts(inputs, T=2048):
    f32 = np.float32
    x = np.ascontiguousarray(np.asarray(inputs["L0_attn_x"], f32))
    w_in = np.asarray(inputs["L0_attn_w_in"], f32)
    conv_w = np.asarray(inputs["L0_attn_conv_w"], f32)
    conv_b = np.asarray(inputs["L0_attn_conv_b"], f32)
    dt_bias = np.asarray(inputs["L0_attn_dt_bias"], f32)
    A_log = np.asarray(inputs["L0_attn_A_log"], f32)
    Dv = np.asarray(inputs["L0_attn_D"], f32)
    norm_w = np.asarray(inputs["L0_attn_norm_w"], f32)
    w_out = np.asarray(inputs["L0_attn_w_out"], f32)
    gk = np.asarray(inputs["L0_ffn_gate_kernel"], f32)
    wi0 = np.asarray(inputs["L0_ffn_wi_0"], f32)
    wi1 = np.asarray(inputs["L0_ffn_wi_1"], f32)
    wo = np.asarray(inputs["L0_ffn_wo"], f32)

    w_in_p = np.zeros((2048, NT_IN * 128), f32)
    w_in_p[:, :D_IN_PROJ] = w_in
    w_in_r = np.ascontiguousarray(
        w_in_p.reshape(16, 128, NT_IN, 128).transpose(2, 1, 0, 3))
    w_out_r = np.ascontiguousarray(
        w_out.reshape(32, 128, 16, 128).transpose(2, 1, 0, 3))
    wi0_r = np.ascontiguousarray(
        wi0.reshape(8, 16, 128, 16, 128).transpose(0, 3, 2, 1, 4))
    wi1_r = np.ascontiguousarray(
        wi1.reshape(8, 16, 128, 16, 128).transpose(0, 3, 2, 1, 4))
    wo_r = np.ascontiguousarray(
        wo.reshape(8, 16, 128, 16, 128).transpose(0, 3, 2, 1, 4))
    gk_r = np.ascontiguousarray(gk.reshape(16, 128, 8).transpose(1, 0, 2))
    conv_w_r = np.ascontiguousarray(conv_w.reshape(NT_XBC, 128, 4).transpose(1, 0, 2))
    conv_b_r = np.ascontiguousarray(conv_b.reshape(NT_XBC, 128).transpose(1, 0))
    D_cols = np.ascontiguousarray(
        Dv.reshape(NPAIR, 2)[:, :, None].repeat(HEADDIM, 2)
        .transpose(1, 2, 0).reshape(128, NPAIR))
    normw_cols = np.ascontiguousarray(norm_w.reshape(NT_Z, 128).transpose(1, 0))
    dtb = np.ascontiguousarray(dt_bias.reshape(NHEADS, 1))
    alog = np.ascontiguousarray(A_log.reshape(NHEADS, 1))

    li = np.arange(CH)
    shared = dict(
        w_in_r=w_in_r, w_out_r=w_out_r, wi0_r=wi0_r, wi1_r=wi1_r, wo_r=wo_r,
        gk_r=gk_r, conv_w_r=conv_w_r, conv_b_r=conv_b_r, D_cols=D_cols,
        normw_cols=normw_cols, dt_bias_c=dtb, A_log_c=alog,
        ident_bf=np.eye(128, dtype=bfloat16),
        ident_f32=np.eye(128, dtype=f32),
        ones_row_f32=np.ones((1, 128), f32),
        ones_col_f32=np.ones((128, 1), f32),
        masktri=np.where(li[None, :] >= li[:, None], 0.0, NEG_BIG).astype(f32),
        misc_cols=np.tile(np.array([[EPS, -1.0, 1.0 / D_INNER, -0.5]], f32), (128, 1)),
    )

    in_maps = []
    for core in range(8):
        b, half = core // 2, core % 2
        x_tm = np.ascontiguousarray(x[b, half * T:(half + 1) * T])
        x_fm = np.ascontiguousarray(x_tm.T)
        x_ctx = np.zeros((128, 2048), f32)
        x_ptm = np.zeros((T, 2048), f32)
        if half == 1:
            x_ctx[0:3] = x[b, T - 3:T]
            x_ptm[:] = x[b, 0:T]
        m = dict(shared)
        m.update(x_tm=x_tm, x_fm=x_fm, x_ctx=x_ctx, x_ptm=x_ptm,
                 mask_first=np.full((128, 1), 0.0 if half else 1.0, f32),
                 mask_second=np.full((128, 1), 1.0 if half else 0.0, f32))
        in_maps.append(m)
    return in_maps



def build_program(T=2048, debug_taps=()):
    nc = bass.Bass("TRN2", target_bir_lowering=False)
    NCH = T // CH
    NST = T // 128
    LG = min(1024, T)
    NLG = T // LG
    HALF_T = T // 2
    p = Prog(nc)
    es = p.es

    def din(name, shape, dtype=F32):
        return nc.dram_tensor(name, list(shape), dtype, kind="ExternalInput").ap()

    def dint(name, shape, dtype):
        kind = "ExternalOutput" if name in debug_taps else "Internal"
        return nc.dram_tensor(name, list(shape), dtype, kind=kind).ap()

    x_tm = din("x_tm", (T, 2048))
    x_fm = din("x_fm", (2048, T))
    x_ctx = din("x_ctx", (128, 2048))
    x_ptm = din("x_ptm", (T, 2048))
    w_in_r = din("w_in_r", (NT_IN, 128, 16, 128))
    w_out_r = din("w_out_r", (16, 128, 32, 128))
    wi0_r = din("wi0_r", (8, 16, 128, 16, 128))
    wi1_r = din("wi1_r", (8, 16, 128, 16, 128))
    wo_r = din("wo_r", (8, 16, 128, 16, 128))
    gk_r = din("gk_r", (128, 16, 8))
    conv_w_d = din("conv_w_r", (128, NT_XBC, 4))
    conv_b_d = din("conv_b_r", (128, NT_XBC))
    D_cols_d = din("D_cols", (128, NPAIR))
    normw_d = din("normw_cols", (128, NT_Z))
    dtb_d = din("dt_bias_c", (NHEADS, 1))
    alog_d = din("A_log_c", (NHEADS, 1))
    ident_bf_d = din("ident_bf", (128, 128), BF16)
    ident_f32_d = din("ident_f32", (128, 128))
    ones_row_f32_d = din("ones_row_f32", (1, 128))
    ones_col_f32_d = din("ones_col_f32", (128, 1))
    masktri_d = din("masktri", (CH, CH))
    mask1_d = din("mask_first", (128, 1))
    mask2_d = din("mask_second", (128, 1))
    misc_d = din("misc_cols", (128, 4))

    out_d = nc.dram_tensor("out_fm", [16, 128, T], F32, kind="ExternalOutput").ap()

    ln1_d = dint("ln1_d", (T + 128, 2048), BF16)
    z_d = dint("z_d", (D_INNER, T), BF16)
    xbc_d = dint("xbc_d", (CONV_DIM, T), BF16)
    ln1p_d = dint("ln1p_d", (T, 2048), BF16)
    xbcp_d = dint("xbcp_d", (CONV_DIM, T), BF16)
    xh_d = dint("xh_d", (D_INNER, CH), BF16)
    acs_d = dint("acs_d", (NHEADS, CH), F32)
    ea_d = dint("ea_d", (NCH, NHEADS, CH), BF16)
    sc_d = dint("sc_d", (NHEADS, CH), BF16)
    y_d = dint("y_d", (NCH, 128, NT_Z * CH), BF16)
    x1_d = dint("x1_d", (16, 128, T), F32)
    cw_d = dint("cw_d", (8, T), F32)

    def sbuf(stack, name, shape, dtype, side=None):
        return stack.enter_context(nc.sbuf_tensor(name, list(shape), dtype, side=side))

    conv_w_sb = sbuf(es, "conv_w_sb", (128, NT_XBC * 4), F32)
    conv_b_sb = sbuf(es, "conv_b_sb", (128, NT_XBC), F32)
    D_cols_sb = sbuf(es, "D_cols_sb", (128, NPAIR), F32)
    normw_sb = sbuf(es, "normw_sb", (128, NT_Z), F32)
    dtb_sb = sbuf(es, "dtb_sb", (NHEADS, 1), F32)
    alog_sb = sbuf(es, "alog_sb", (NHEADS, 1), F32)
    expalog_sb = sbuf(es, "expalog_sb", (NHEADS, 1), F32)
    ident_sb = sbuf(es, "ident_sb", (128, 128), BF16)
    identf_sb = sbuf(es, "identf_sb", (128, 128), F32)
    ones_row_sb = sbuf(es, "ones_row_sb", (1, 128), F32)
    ones_col_sb = sbuf(es, "ones_col_sb", (128, 1), F32)
    masktri_sb = sbuf(es, "masktri_sb", (CH, CH), F32)
    mask1_sb = sbuf(es, "mask1_sb", (128, 1), F32)
    mask2_sb = sbuf(es, "mask2_sb", (128, 1), F32)
    misc_sb = sbuf(es, "misc_sb", (128, 4), F32)
    ms1 = ExitStack()
    dt_sb = sbuf(ms1, "dt_sb", (NHEADS, T), F32, side="right")
    ctx3_sb = sbuf(ms1, "ctx3_sb", (128, NT_XBC * 3), F32, side="right")
    xc = [sbuf(ms1, f"xc{i}", (128, NT_XBC * (CH + 3)), BF16, side="right") for i in range(2)]
    msp = ExitStack()
    dtp_sb = sbuf(msp, "dtp_sb", (NHEADS, T), F32, side="right")
    ms2 = ExitStack()
    S0b_sb = sbuf(ms2, "S0b_sb", (128, NHEADS * HEADDIM), BF16)
    Cfull_sb = sbuf(ms2, "Cfull_sb", (128, NCH * CH), BF16)
    Alast_sb = sbuf(ms2, "Alast_sb", (NHEADS, NCH), F32)
    cumprev_sb = sbuf(ms2, "cumprev_sb", (NHEADS, NCH), F32)

    KC = ("consts",)
    for dst, src in [
        (conv_w_sb, conv_w_d.rearrange("p a b -> p (a b)")), (conv_b_sb, conv_b_d),
        (D_cols_sb, D_cols_d), (normw_sb, normw_d), (dtb_sb, dtb_d),
        (alog_sb, alog_d), (ident_sb, ident_bf_d), (identf_sb, ident_f32_d),
        (ones_row_sb, ones_row_f32_d), (ones_col_sb, ones_col_f32_d),
        (masktri_sb, masktri_d), (mask1_sb, mask1_d), (mask2_sb, mask2_d),
        (misc_sb, misc_d),
    ]:
        p.dma("sync", "consts", dst[:], src, writes=[KC])
    p.op("scalar", lambda e: e.activation(expalog_sb[:], alog_sb[:], AF.Exp),
         reads=[KC], writes=[KC])

    pha1 = ExitStack()
    _x1b = sbuf(pha1, "a1_x0", (128, 2048), F32, side="right")
    xst = [_x1b, _x1b]
    _l1b = sbuf(pha1, "a1_ln0", (128, 2048), BF16, side="right")
    lnb = [_l1b, _l1b]
    st6 = sbuf(pha1, "a1_st6", (128, 24), F32, side="right")
    mv_sb = sbuf(pha1, "a1_mv", (128, 2), F32, side="right")
    rstd_sb = sbuf(pha1, "a1_rstd", (128, 1), F32, side="right")
    mr_sb = sbuf(pha1, "a1_mr", (128, 1), F32, side="right")

    order = list(range(NST + 1, 2 * NST + 1)) + list(range(NST + 1))
    for st in order:
        par = st % 2
        xb, lb = xst[par], lnb[par]
        kx, kl = ("a1x",), ("a1l",)
        if st < NST:
            src = x_tm[st * 128:(st + 1) * 128, :]
        elif st == NST:
            src = x_ctx[:, :]
        else:
            sp = st - NST - 1
            src = x_ptm[sp * 128:(sp + 1) * 128, :]
        p.dma("scalar", "a1_in", xb[:], src, writes=[kx])
        for q in range(4):
            p.op("vector", lambda e, xb=xb, q=q: e.bn_stats(
                out=st6[:, q * 6:(q + 1) * 6], in_=xb[:, q * 512:(q + 1) * 512]),
                reads=[kx], writes=[("a1st6",)])
        p.op("vector", lambda e: e.bn_aggr(out=mv_sb[:], in_=st6[:]),
             reads=[("a1st6",)], writes=[("a1mv",)])
        p.op("scalar", lambda e: e.activation(rstd_sb[:], mv_sb[:, 1:2], AF.Sqrt, bias=misc_sb[:, 0:1]),
             reads=[("a1mv",), KC], writes=[("a1rstd",)])
        p.op("vector", lambda e: e.reciprocal(rstd_sb[:], rstd_sb[:]),
             reads=[("a1rstd",)], writes=[("a1rstd",)])
        p.op("vector", lambda e: e.tensor_tensor(mr_sb[:], mv_sb[:, 0:1], rstd_sb[:], op=ALU.mult),
             reads=[("a1mv",), ("a1rstd",)], writes=[("a1mr",)])
        p.op("vector", lambda e, xb=xb, lb=lb: e.tensor_scalar(
            lb[:], xb[:], rstd_sb[:], mr_sb[:], op0=ALU.mult, op1=ALU.subtract),
            reads=[kx, ("a1rstd",), ("a1mr",)], writes=[kl])
        if st <= NST:
            p.dma("scalar", "a1_out", ln1_d[st * 128:(st + 1) * 128, :], lb[:],
                  reads=[kl], writes=[("d_ln1",)])
        else:
            sp = st - NST - 1
            p.dma("scalar", "a1_out", ln1p_d[sp * 128:(sp + 1) * 128, :], lb[:],
                  reads=[kl], writes=[("d_ln1p",)])

    c1s = ExitStack()
    convo_s = sbuf(c1s, "convo", (128, NT_XBC * CH), F32)
    convo2 = [convo_s, convo_s]
    _xbcs = sbuf(c1s, "xbcs0", (128, NT_XBC * CH), BF16)
    xbcs2 = [_xbcs, _xbcs]
    dtc2 = [sbuf(c1s, f"dtc{i}", (NHEADS, CH), F32) for i in range(2)]
    acs2 = [sbuf(c1s, f"acs{i}", (NHEADS, CH), F32) for i in range(2)]
    nacs2 = [sbuf(c1s, f"nacs{i}", (NHEADS, CH), F32) for i in range(2)]
    _expa = sbuf(c1s, "expa0", (NHEADS, CH), BF16)
    expa2 = [_expa, _expa]
    decay2 = [sbuf(c1s, f"decay{i}", (NHEADS, CH), F32) for i in range(2)]
    ddec2 = [sbuf(c1s, f"ddec{i}", (NHEADS, CH), F32) for i in range(2)]
    _t64a = sbuf(c1s, "t64a0", (NHEADS, CH), F32)
    t64a2 = [_t64a, _t64a]
    _t64b = sbuf(c1s, "t64b0", (NHEADS, CH), F32)
    t64b2 = [_t64b, _t64b]
    _eacol = sbuf(c1s, "eacol0", (NHEADS, 1), F32)
    eacol2 = [_eacol, _eacol]
    dtT2 = [sbuf(c1s, f"dtT{i}", (128, NHEADS), F32) for i in range(2)]
    ddecT2 = [sbuf(c1s, f"ddecT{i}", (128, NHEADS), F32) for i in range(2)]
    nacsT2 = [sbuf(c1s, f"nacsT{i}", (128, NHEADS), F32) for i in range(2)]
    EAb2 = [sbuf(c1s, f"EAb{i}", (128, NHEADS), F32) for i in range(2)]
    _earow = sbuf(c1s, "earow0", (1, NHEADS), F32)
    earow2 = [_earow, _earow]
    Btm = sbuf(c1s, "Btm", (128, CH), BF16)
    Gsb = sbuf(c1s, "Gsb", (128, CH), BF16)
    t128 = sbuf(c1s, "t128", (128, CH), F32)
    Xdec = sbuf(c1s, "Xdec", (128, D_INNER), BF16)
    _xhg = sbuf(c1s, "xhg0", (128, 1024), BF16)
    xhg2 = [_xhg, _xhg]
    S_sb = sbuf(c1s, "S_sb", (128, NHEADS * HEADDIM), F32)
    psS = [c1s.enter_context(nc.psum_tensor(f"psS{i}", [128, 512], F32)) for i in range(2)]
    psM = c1s.enter_context(nc.psum_tensor("psM", [128, 512], F32))
    psMb = c1s.enter_context(nc.psum_tensor("psMb", [128, 1024], BF16))
    KM, KMB = ("psM",), ("psMb",)
    CHW = CH + 3

    a2s = ExitStack()
    TC = T + 128
    ln1f = sbuf(a2s, "ln1f", (128, 16 * T), BF16, side="right")
    lnctxf = sbuf(a2s, "lnctxf", (128, 16 * 128), BF16, side="right")
    w2 = [sbuf(a2s, f"a2_w{i}", (128, 16 * 128), BF16, side="right") for i in range(2)]
    _evb = sbuf(a2s, "a2_eb0", (128, T), BF16, side="right")
    ev_b = [_evb, _evb]
    ctx_ev = sbuf(a2s, "a2_ctxev", (128, 128), F32, side="right")
    ps_a2 = [a2s.enter_context(nc.psum_tensor(f"psA{i}", [128, 512], F32))
             for i in range(4)]
    TSL = slices(T, 512)
    KLN1F = [("ln1f", kt) for kt in range(16)]

    def emit_a2_tile(nt, need_ctx, xbc_dst, kxbc_dst, dt_dst, kdt_dst, z_out):
        if True:
            par = nt % 2
            wb = w2[par]
            kw = ("a2w", par)
            p.dma("gpsimd", f"a2_w{par}", wb[:], w_in_r[nt].rearrange("d k n -> d (k n)"),
                  writes=[kw], tag=f"w_in {nt}")
            pset = [ps_a2[i] for i in range(len(TSL))]
            pk = [("psA", i) for i in range(len(TSL))]
            for kt in range(16):
                for s, (n0, n1) in enumerate(TSL):
                    p.op("tensor",
                         lambda e, wb=wb, kt=kt, n0=n0, n1=n1, ps=pset[s]:
                         e.matmul(ps[:, 0:n1 - n0], wb[:, kt * 128:(kt + 1) * 128],
                                  ln1f[:, kt * T + n0:kt * T + n1],
                                  start=(kt == 0), stop=(kt == 15)),
                         reads=[kw, KLN1F[kt]], writes=[pk[s]])
            is_z = nt < NT_Z
            is_dt = nt >= NT_Z + NT_XBC
            eb = ev_b[par]
            keb = ("a2eb",)
            for s, (n0, n1) in enumerate(TSL):
                if is_dt:
                    p.op("vector", lambda e, ps=pset[s], n0=n0, n1=n1, dt_dst=dt_dst:
                         e.tensor_copy(dt_dst[:, n0:n1], ps[0:64, 0:n1 - n0]),
                         reads=[pk[s]], writes=[kdt_dst])
                elif s % 2 == 0:
                    p.op("vector", lambda e, ps=pset[s], n0=n0, n1=n1, eb=eb:
                         e.tensor_copy(eb[:, n0:n1], ps[:, 0:n1 - n0]),
                         reads=[pk[s]], writes=[keb])
                else:
                    p.op("scalar", lambda e, ps=pset[s], n0=n0, n1=n1, eb=eb:
                         e.copy(eb[:, n0:n1], ps[:, 0:n1 - n0]),
                         reads=[pk[s]], writes=[keb])
            if is_z:
                p.dma("scalar", "a2_zout", z_out[nt * 128:(nt + 1) * 128, :], eb[:, 0:T],
                      reads=[keb], writes=[("d_z",)])
            elif not is_dt:
                j = nt - NT_Z
                p.dma("scalar", "a2_xout", xbc_dst[j * 128:(j + 1) * 128, :], eb[:, 0:T],
                      reads=[keb], writes=[kxbc_dst])
                if need_ctx:
                    pc, kpc = pset[0], pk[0]
                    for kt in range(16):
                        p.op("tensor", lambda e, wb=wb, kt=kt, pc=pc:
                             e.matmul(pc[:, 0:128], wb[:, kt * 128:(kt + 1) * 128],
                                      lnctxf[:, kt * 128:(kt + 1) * 128],
                                      start=(kt == 0), stop=(kt == 15)),
                             reads=[kw, KLN1F[kt]], writes=[kpc])
                    p.op("vector", lambda e, pc=pc: e.tensor_copy(ctx_ev[:], pc[:, 0:128]),
                         reads=[kpc], writes=[("a2ctx",)])
                    p.op("vector", lambda e, j=j: e.tensor_copy(
                        ctx3_sb[:, j * 3:(j + 1) * 3], ctx_ev[:, 0:3]),
                        reads=[("a2ctx",)], writes=[("ctx3",)])

    def emit_chunk(c, full, xbcd, kxbcd, dtbuf, kdtbuf):
        par = c % 2
        xcb = xc[par]
        kxc = ("xc", par)
        convo, xbcs = convo2[par], xbcs2[par]
        dtc, acs, nacs, expa = dtc2[par], acs2[par], nacs2[par], expa2[par]
        decay, ddec, t64a, t64b = decay2[par], ddec2[par], t64a2[par], t64b2[par]
        eacol, dtT, ddecT, nacsT = eacol2[par], dtT2[par], ddecT2[par], nacsT2[par]
        EAb, earow = EAb2[par], earow2[par]
        p.dma("sync", f"c1_xbc{par}",
              xcb[:].rearrange("p (nt w) -> p nt w", w=CHW)[:, :, 3:3 + CH],
              xbcd.rearrange("(nt p) t -> p nt t", p=128)[:, :, c * CH:(c + 1) * CH],
              reads=[kxbcd], writes=[kxc])
        for nt in range(NT_XBC):
            o = nt * CHW
            co = nt * CH
            if nt % 2 == 0 or not full:
                p.op("vector", lambda e, o=o, co=co, nt=nt, xcb=xcb: e.tensor_scalar(
                    convo[:, co:co + CH], xcb[:, o:o + CH],
                    conv_w_sb[:, 4 * nt:4 * nt + 1], conv_b_sb[:, nt:nt + 1],
                    op0=ALU.mult, op1=ALU.add),
                    reads=[kxc, KC], writes=[("convo", nt)])
                for tap in range(1, 4):
                    p.op("vector", lambda e, o=o, co=co, nt=nt, tap=tap, xcb=xcb:
                         e.scalar_tensor_tensor(
                             convo[:, co:co + CH], xcb[:, o + tap:o + tap + CH],
                             conv_w_sb[:, 4 * nt + tap:4 * nt + tap + 1],
                             convo[:, co:co + CH], op0=ALU.mult, op1=ALU.add),
                         reads=[kxc, KC, ("convo", nt)], writes=[("convo", nt)])
            else:
                for tap in range(4):
                    wv = conv_w_sb[:, 4 * nt + tap:4 * nt + tap + 1].to_broadcast((128, 1, CH))
                    if tap == 0:
                        p.op("gpsimd", lambda e, o=o, co=co, wv=wv, xcb=xcb: e.tensor_tensor(
                            convo[:, co:co + CH], xcb[:, o:o + CH], wv, op=ALU.mult),
                            reads=[kxc, KC], writes=[("convo", nt)])
                    else:
                        p.op("gpsimd", lambda e, o=o, tap=tap, wv=wv, xcb=xcb:
                             e.tensor_tensor(t128[:, 0:CH], xcb[:, o + tap:o + tap + CH],
                                             wv, op=ALU.mult),
                             reads=[kxc, KC], writes=[("t128",)])
                        p.op("gpsimd", lambda e, co=co, nt=nt: e.tensor_tensor(
                            convo[:, co:co + CH], convo[:, co:co + CH], t128[:, 0:CH],
                            op=ALU.add),
                            reads=[("t128",), ("convo", nt)], writes=[("convo", nt)])
                p.op("gpsimd", lambda e, co=co, nt=nt: e.tensor_tensor(
                    convo[:, co:co + CH], convo[:, co:co + CH],
                    conv_b_sb[:, nt:nt + 1].to_broadcast((128, 1, CH)), op=ALU.add),
                    reads=[KC, ("convo", nt)], writes=[("convo", nt)])
            eng = "vector" if (nt % 2 == 0 or not full) else "gpsimd"
            p.op(eng, lambda e, o=o, opar=1 - par, xcb=xcb: e.tensor_copy(
                xc[opar][:, o:o + 3], xcb[:, o + CH:o + CH + 3]),
                reads=[kxc], writes=[("xc", 1 - par)])
        KCV = [("convo", nt) for nt in range(NT_XBC)]
        if full:
            for nt in range(0, NT_XBC, 6):
                p.op("tensor", lambda e: e.matmul(
                    psM[:, 448:448 + 64], ones_row_sb[:], earow[:],
                    start=True, stop=True),
                    reads=[("convo", nt), ("earow",), KC], writes=[KM])
        p.op("scalar", lambda e: e.activation(xbcs[:], convo[:], AF.Silu),
             reads=KCV, writes=[("xbcs",)])
        B_nm = xbcs[:, NT_Z * CH:(NT_Z + 1) * CH]
        C_nm = xbcs[:, (NT_Z + 1) * CH:(NT_Z + 2) * CH]
        if full:
            p.op("vector", lambda e, c=c, C_nm=C_nm: e.tensor_copy(
                Cfull_sb[:, c * CH:(c + 1) * CH], C_nm),
                reads=[("xbcs",)], writes=[("Cfull",)])

        p.op("vector", lambda e, c=c, dtbuf=dtbuf: e.tensor_scalar(
            dtc[:], dtbuf[:, c * CH:(c + 1) * CH], dtb_sb[:], None, op0=ALU.add),
            reads=[kdtbuf, KC], writes=[("dtc", par)])
        p.op("scalar", lambda e: e.activation(t64a[:], dtc[:], AF.Abs),
             reads=[("dtc", par)], writes=[("t64a",)])
        p.op("scalar", lambda e: e.activation(t64a[:], t64a[:], AF.Exp, scale=misc_sb[0:NHEADS, 1:2]),
             reads=[("t64a",), KC], writes=[("t64a",)])
        p.op("scalar", lambda e: e.activation(t64b[:], t64a[:], AF.Ln, bias=1.0),
             reads=[("t64a",)], writes=[("t64b",)])
        p.op("vector", lambda e: e.tensor_scalar_max(dtc[:], dtc[:], 0.0),
             reads=[("dtc", par), ("t64a",)], writes=[("dtc", par)])
        p.op("vector", lambda e: e.tensor_tensor(dtc[:], dtc[:], t64b[:], op=ALU.add),
             reads=[("dtc", par), ("t64b",)], writes=[("dtc", par)])
        p.op("vector", lambda e: e.tensor_scalar(
            t64a[:], dtc[:], expalog_sb[:], None, op0=ALU.mult),
            reads=[("dtc", par), KC], writes=[("t64a",)])
        p.op("vector", lambda e: e.tensor_tensor_scan(
            nacs[:], t64a[:], t64a[:], 0.0, op0=ALU.add, op1=ALU.bypass),
            reads=[("t64a",)], writes=[("nacs", par)])
        p.op("vector", lambda e: e.tensor_scalar(acs[:], nacs[:], -1.0, None, op0=ALU.mult),
             reads=[("nacs", par)], writes=[("acs", par)])
        if full:
            p.dma("scalar", "c1_acs", acs_d[:, :], acs[:], reads=[("acs", par)], writes=[("d_acs",)])
            p.op("scalar", lambda e: e.activation(expa[:], acs[:], AF.Exp),
                 reads=[("acs", par)], writes=[("expa",)])
            p.dma("scalar", "c1_ea", ea_d[c], expa[:], reads=[("expa",)], writes=[("d_ea",)])
            p.op("vector", lambda e, c=c: e.tensor_copy(Alast_sb[:, c:c + 1], acs[:, CH - 1:CH]),
                 reads=[("acs", par)], writes=[("Alast",)])
        p.op("scalar", lambda e: e.activation(decay[:], nacs[:], AF.Exp,
                                              bias=acs[:, CH - 1:CH]),
             reads=[("nacs", par), ("acs", par)], writes=[("decay", par)])
        p.op("vector", lambda e: e.tensor_tensor(ddec[:], dtc[:], decay[:], op=ALU.mult),
             reads=[("dtc", par), ("decay", par)], writes=[("ddec", par)])
        p.op("scalar", lambda e: e.activation(eacol[:], acs[:, CH - 1:CH], AF.Exp),
             reads=[("acs", par)], writes=[("eacol",)])
        p.op("tensor", lambda e: e.transpose(psM[0:1, 384:384 + NHEADS], eacol[:],
                                             identf_sb[0:NHEADS, 0:NHEADS]),
             reads=[("eacol",), KC], writes=[KM])
        p.op("vector", lambda e: e.tensor_copy(earow[:], psM[0:1, 384:384 + NHEADS]),
             reads=[KM], writes=[("earow",)])
        p.op("tensor", lambda e: e.matmul(psM[:, 256:256 + NHEADS], ones_row_sb[:],
                                          earow[:], start=True, stop=True),
             reads=[("earow",), KC], writes=[KM])
        p.op("vector", lambda e: e.tensor_copy(EAb[:], psM[:, 256:256 + NHEADS]),
             reads=[KM], writes=[("EAb", par)])
        trs = [(ddec, ddecT, ("ddec", par), ("ddecT", par))]
        if full:
            trs += [(dtc, dtT, ("dtc", par), ("dtT", par)), (nacs, nacsT, ("nacs", par), ("nacsT", par))]
        for tsrc, tdst, rk, wk in trs:
            p.op("tensor", lambda e, tsrc=tsrc: e.transpose(
                psM[:, 128:128 + NHEADS], tsrc[:], identf_sb[0:NHEADS, 0:NHEADS]),
                reads=[rk, KC], writes=[KM])
            p.op("vector", lambda e, tdst=tdst: e.tensor_copy(tdst[:], psM[:, 128:128 + NHEADS]),
                 reads=[KM], writes=[wk])

        xhg = xhg2[par]
        for grp in range(4):
            for i8 in range(8):
                jx = grp * 8 + i8
                p.op("tensor", lambda e, i8=i8, jx=jx, xbcs=xbcs: e.transpose(
                    psMb[:, i8 * 128:(i8 + 1) * 128], xbcs[:, jx * CH:(jx + 1) * CH],
                    ident_sb[:]),
                    reads=[("xbcs",), KC], writes=[KMB])
            h16 = slice(grp * 16, (grp + 1) * 16)
            p.op("scalar", lambda e, xhg=xhg: e.copy(
                xhg[:, 0:1024], psMb[:, 0:1024]),
                reads=[KMB], writes=[("xhg",)])
            if full:
                p.op("vector", lambda e, grp=grp, h16=h16, dtT=dtT, xhg=xhg: e.tensor_tensor(
                    Xp[:, grp * 1024:(grp + 1) * 1024], xhg[:, 0:1024],
                    rep_last(dtT[:, h16], HEADDIM), op=ALU.mult),
                    reads=[("xhg",), ("dtT", par)], writes=[("Xp",)])
            p.op("vector", lambda e, grp=grp, h16=h16, ddecT=ddecT, xhg=xhg: e.tensor_tensor(
                Xdec[:, grp * 1024:(grp + 1) * 1024], xhg[:, 0:1024],
                rep_last(ddecT[:, h16], HEADDIM), op=ALU.mult),
                reads=[("xhg",), ("ddecT", par)], writes=[("Xdec",)])

        if full:
            p.op("tensor", lambda e, B_nm=B_nm, C_nm=C_nm: e.matmul(
                psM[:, 0:CH], B_nm, C_nm, start=True, stop=True),
                reads=[("xbcs",)], writes=[KM])
            p.op("vector", lambda e: e.tensor_copy(Gsb[:], psM[:, 0:CH]),
                 reads=[KM], writes=[("G",)])
        p.op("tensor", lambda e, B_nm=B_nm: e.transpose(psMb[:, 0:CH], B_nm, ident_sb[:]),
             reads=[("xbcs",), KC], writes=[KMB])
        p.op("vector", lambda e: e.tensor_copy(Btm[:], psMb[:, 0:CH]),
             reads=[KMB], writes=[("Btm",)])

        for hh in range(NHEADS // HHALF):
            h0 = hh * HHALF
            if full:
                Dm = Dm2[hh % 2]
                kdm = ("Dm", hh % 2)
                p.dma("sync", "c1_eabc", eabc[:],
                      ea_d[c].rearrange("h l -> (h l)").unsqueeze(0)[:, h0 * CH:(h0 + HHALF) * CH]
                      .to_broadcast((128, HHALF * CH)),
                      reads=[("d_ea",)], writes=[("eabc",)])
                p.op("gpsimd", lambda e, C_nm=C_nm: e.tensor_tensor(
                    Cexp[:], eabc[:], rep_mid(C_nm, HHALF), op=ALU.mult),
                    reads=[("eabc",), ("xbcs",)], writes=[("Cexp",)])
                p.op("vector", lambda e, h0=h0, Dm=Dm: e.tensor_tensor(
                    Dm[:], rep_mid(masktri_sb[:], HHALF),
                    rep_last(nacsT[:, h0:h0 + HHALF], CH), op=ALU.add),
                    reads=[KC, ("nacsT", par)], writes=[kdm])
                p.dma("gpsimd", f"c1_abc{hh % 2}", Dm[:],
                      acs_d.rearrange("h l -> (h l)").unsqueeze(0)[:, h0 * CH:(h0 + HHALF) * CH]
                      .to_broadcast((128, HHALF * CH)),
                      reads=[("d_acs",), kdm], writes=[kdm],
                      accum_op=ALU.add)
                p.op("scalar", lambda e, Dm=Dm: e.activation(Lh[:], Dm[:], AF.Exp),
                     reads=[kdm], writes=[("Lh",)])
                p.op("vector", lambda e: e.tensor_tensor(
                    Mth[:], Lh[:], rep_mid(Gsb[:], HHALF), op=ALU.mult),
                    reads=[("Lh",), ("G",)], writes=[("Mth",)])

                for jj in range(HHALF // 2):
                    j = hh * (HHALF // 2) + jj
                    bank, kb = psY[jj % 2], ("psY", jj % 2)
                    for u in range(2):
                        h = 2 * j + u
                        lh = h - h0
                        rows = slice(64 * u, 64 * u + 64)
                        p.op("tensor", lambda e, bank=bank, rows=rows, h=h, lh=lh:
                             e.matmul(bank[rows, 0:CH], Xp[:, h * 64:h * 64 + 64],
                                      Mth[:, lh * CH:(lh + 1) * CH], start=True, stop=False),
                             reads=[("Xp",), ("Mth",)], writes=[kb])
                        p.op("tensor", lambda e, bank=bank, rows=rows, h=h, lh=lh:
                             e.matmul(bank[rows, 0:CH], Sb_sb[:, h * 64:h * 64 + 64],
                                      Cexp[:, lh * CH:(lh + 1) * CH], start=False, stop=True),
                             reads=[("Sb",), ("Cexp",)], writes=[kb])
                    p.op("vector", lambda e, j=j, bank=bank: e.scalar_tensor_tensor(
                        ysb[:, j * CH:(j + 1) * CH], xbcs[:, j * CH:(j + 1) * CH],
                        D_cols_sb[:, j:j + 1], bank[:, 0:CH], op0=ALU.mult, op1=ALU.add),
                        reads=[("xbcs",), kb, KC], writes=[("ysb",)])

            for g in range(HHALF // 8):
                gg = hh * (HHALF // 8) + g
                bank, kb = psS[gg % 2], ("psS", gg % 2)
                hbase = h0 + g * 8
                for i in range(8):
                    h = hbase + i
                    p.op("tensor", lambda e, bank=bank, i=i, h=h: e.matmul(
                        bank[:, i * 64:(i + 1) * 64], Btm[:],
                        Xdec[:, h * 64:(h + 1) * 64], start=True, stop=True),
                        reads=[("Btm",), ("Xdec",)], writes=[kb])
                base = hbase * 64
                p.op("vector", lambda e, base=base, hbase=hbase: e.tensor_tensor(
                    S_sb[:, base:base + 512], S_sb[:, base:base + 512],
                    rep_last(EAb[:, hbase:hbase + 8], 64), op=ALU.mult),
                    reads=[("S",), ("EAb", par)], writes=[("S",)])
                p.op("vector", lambda e, base=base, bank=bank: e.tensor_tensor(
                    S_sb[:, base:base + 512], S_sb[:, base:base + 512],
                    bank[:, 0:512], op=ALU.add),
                    reads=[("S",), kb], writes=[("S",)])
        if full:
            p.op("vector", lambda e: e.tensor_copy(Sb_sb[:], S_sb[:]),
                 reads=[("S",)], writes=[("Sb",)])
            p.dma("sync", "c1_y", y_d[c], ysb[:], reads=[("ysb",)], writes=[("d_y", c)])

    for kt in range(16):
        p.dma_tr("sync", "a2_lnf", ln1f[:, kt * T:kt * T + T],
                 ln1p_d[0:T, kt * 128:(kt + 1) * 128],
                 reads=[("d_ln1p",)], writes=[("ln1f", kt)])
    for nt in range(NT_Z, NT_IN):
        emit_a2_tile(nt, False, xbcp_d, ("d_xbcp",), dtp_sb, ("dtp_sb",), None)

    for kt in range(16):
        p.dma_tr("sync", "a2_lnf", ln1f[:, kt * T:kt * T + T],
                 ln1_d[0:T, kt * 128:(kt + 1) * 128],
                 reads=[("d_ln1",)], writes=[("ln1f", kt)])
        p.dma_tr("sync", "a2_lnf", lnctxf[:, kt * 128:(kt + 1) * 128],
                 ln1_d[T:T + 128, kt * 128:(kt + 1) * 128],
                 reads=[("d_ln1",)], writes=[("ln1f", kt)])

    p.op("vector", lambda e: e.memset(S_sb[:], 0.0), writes=[("S",)])
    p.op("vector", lambda e: e.memset(
        xc[0][:].rearrange("p (nt w) -> p nt w", w=CHW)[:, :, 0:3], 0.0),
        writes=[("xc", 0)])
    ti = 0
    for c in range(NCH):
        quota = (NT_IN * (c + 1) + NCH - 1) // NCH
        while ti < min(quota, NT_IN):
            emit_a2_tile(ti, True, xbc_d, ("d_xbc",), dt_sb, ("dt_sb",), z_d)
            ti += 1
        emit_chunk(c, False, xbcp_d, ("d_xbcp",), dtp_sb, ("dtp_sb",))
    while ti < NT_IN:
        emit_a2_tile(ti, True, xbc_d, ("d_xbc",), dt_sb, ("dt_sb",), z_d)
        ti += 1
    p.op("vector", lambda e: e.tensor_copy(S0b_sb[:], S_sb[:]),
         reads=[("S",)], writes=[("S0b",)])

    p.barrier()
    a2s.close()
    pha1.close()
    msp.close()

    c1x = ExitStack()
    Xp = sbuf(c1x, "Xp", (128, D_INNER), BF16)
    Dm2 = [sbuf(c1x, f"Dm{i}", (128, HHALF * CH), F32) for i in range(2)]
    Lh = sbuf(c1x, "Lh", (128, HHALF * CH), BF16)
    Mth = sbuf(c1x, "Mth", (128, HHALF * CH), BF16)
    eabc = sbuf(c1x, "eabc", (128, HHALF * CH), BF16)
    Cexp = sbuf(c1x, "Cexp", (128, HHALF * CH), BF16)
    ysb = sbuf(c1x, "ysb", (128, NT_Z * CH), BF16)
    Sb_sb = sbuf(c1x, "Sb_sb", (128, NHEADS * HEADDIM), BF16)
    psY = [c1x.enter_context(nc.psum_tensor(f"psY{i}", [128, 512], F32)) for i in range(2)]

    p.op("vector", lambda e: e.memset(S_sb[:], 0.0), reads=[("S0b",)], writes=[("S",)])
    p.op("vector", lambda e: e.memset(Sb_sb[:], 0.0), writes=[("Sb",)])
    p.op("vector", lambda e: e.tensor_copy(
        xc[0][:].rearrange("p (nt w) -> p nt w", w=CHW)[:, :, 0:3],
        ctx3_sb[:].rearrange("p (nt w) -> p nt w", w=3)),
        reads=[("ctx3",)], writes=[("xc", 0)])
    for c in range(NCH):
        emit_chunk(c, True, xbc_d, ("d_xbc",), dt_sb, ("dt_sb",))

    p.op("vector", lambda e: e.tensor_tensor_scan(
        cumprev_sb[:, 1:NCH], Alast_sb[:, 0:NCH - 1], Alast_sb[:, 0:NCH - 1], 0.0,
        op0=ALU.add, op1=ALU.bypass),
        reads=[("Alast",)], writes=[("cumprev",)])
    p.op("vector", lambda e: e.memset(cumprev_sb[:, 0:1], 0.0), writes=[("cumprev",)])

    p.barrier()
    c1x.close()
    c1s.close()
    ms1.close()

    ph = ExitStack()
    yn_sb = sbuf(ph, "yn_sb", (128, NT_Z * HALF_T), BF16)
    ez = sbuf(ph, "ez", (NHEADS, CH), BF16)
    edc = sbuf(ph, "edc", (NHEADS, 1), F32)
    scc = sbuf(ph, "scc", (NHEADS, CH), BF16)
    scbc = sbuf(ph, "scbc", (128, HHALF * CH), BF16)
    Cph = sbuf(ph, "Cph", (128, HHALF * CH), BF16)
    yc2 = sbuf(ph, "yc2", (128, NT_Z * CH), F32)
    zc2 = sbuf(ph, "zc2", (128, NT_Z * CH), BF16)
    slz = sbuf(ph, "slz", (128, NT_Z * CH), F32)
    sqz = sbuf(ph, "sqz", (128, NT_Z * CH), F32)
    rrow = sbuf(ph, "rrow", (1, CH), F32)
    psC = [ph.enter_context(nc.psum_tensor(f"psC{i}", [128, 512], F32)) for i in range(2)]
    psR = ph.enter_context(nc.psum_tensor("psR", [128, 1024], F32))
    wD = [sbuf(ph, f"wD{i}", (128, 32 * 128), BF16) for i in range(2)]
    x0D2 = [sbuf(ph, f"x0D{i}", (128, HALF_T), F32) for i in range(2)]
    lnD2 = [sbuf(ph, f"lnD{i}", (128, HALF_T), BF16) for i in range(2)]
    x1D2 = [sbuf(ph, f"x1D{i}", (128, HALF_T), F32) for i in range(2)]
    psD = [ph.enter_context(nc.psum_tensor(f"psD{i}", [128, 512], F32)) for i in range(4)]
    DSL = slices(HALF_T, 512)

    def emit_d_group(half, g):
        n0g, n1g = g * 512, min(HALF_T, (g + 1) * 512)
        W = n1g - n0g
        for dt in range(16):
            par = dt % 2
            wb = wD[par]
            kw = ("wD", par)
            p.dma("gpsimd", f"d_w{par}", wb[:], w_out_r[dt].rearrange("f k d -> f (k d)"),
                  writes=[kw])
            ps = psD[par * 2 + (g % 2)]
            pkk = ("psD", par * 2 + (g % 2))
            for kt in range(32):
                p.op("tensor", lambda e, wb=wb, kt=kt, ps=ps:
                     e.matmul(ps[:, 0:W], wb[:, kt * 128:(kt + 1) * 128],
                              yn_sb[:, kt * HALF_T + n0g:kt * HALF_T + n1g],
                              start=(kt == 0), stop=(kt == 31)),
                     reads=[kw, ("yn", g)], writes=[pkk])
            x0D, lnD, x1D = x0D2[par], lnD2[par], x1D2[par]
            t0a = half * HALF_T + n0g
            p.dma("sync", f"d_x0{par}", x0D[:, 0:W],
                  x_fm[dt * 128:(dt + 1) * 128, t0a:t0a + W],
                  writes=[("x0D", par)])
            p.dma_tr("sync", f"d_ln{par}", lnD[:, 0:W],
                     ln1_d[t0a:t0a + W, dt * 128:(dt + 1) * 128],
                     reads=[("d_ln1",)], writes=[("lnD", par)])
            p.op("vector", lambda e, x1D=x1D, x0D=x0D, lnD=lnD: e.tensor_tensor(
                x1D[:, 0:W], x0D[:, 0:W], lnD[:, 0:W], op=ALU.add),
                reads=[("x0D", par), ("lnD", par)], writes=[("x1D", par)])
            p.op("vector", lambda e, ps=ps, x1D=x1D: e.tensor_tensor(
                x1D[:, 0:W], x1D[:, 0:W], ps[:, 0:W], op=ALU.add),
                reads=[("x1D", par), pkk], writes=[("x1D", par)])
            p.dma("scalar", f"d_x1{par}",
                  x1_d[dt][:, t0a:t0a + W], x1D[:, 0:W],
                  reads=[("x1D", par)], writes=[("d_x1",)])

    for half in range(2):
        for c in range(half * (NCH // 2), (half + 1) * (NCH // 2)):
            zrel = c - half * (NCH // 2)
            p.dma("sync", "c2_ez", ez[:], ea_d[c], reads=[("d_ea",)], writes=[("ez",)])
            p.op("scalar", lambda e, c=c: e.activation(edc[:], cumprev_sb[:, c:c + 1], AF.Exp),
                 reads=[("cumprev",)], writes=[("edc",)])
            p.op("vector", lambda e: e.tensor_tensor(edc[:], edc[:], mask2_sb[0:NHEADS, :],
                                                     op=ALU.mult),
                 reads=[("edc",), KC], writes=[("edc",)])
            p.op("vector", lambda e: e.tensor_scalar(scc[:], ez[:], edc[:], None, op0=ALU.mult),
                 reads=[("ez",), ("edc",)], writes=[("scc",)])
            p.dma("scalar", "c2_sc", sc_d[:, :], scc[:], reads=[("scc",)], writes=[("d_sc",)])
            p.dma("gpsimd", "c2_y", yc2[:], y_d[c], reads=[("d_y", c)], writes=[("yc2",)])
            for hh in range(NHEADS // HHALF):
                h0 = hh * HHALF
                p.dma("sync", "c2_scbc", scbc[:],
                      sc_d.rearrange("h l -> (h l)").unsqueeze(0)[:, h0 * CH:(h0 + HHALF) * CH]
                      .to_broadcast((128, HHALF * CH)),
                      reads=[("d_sc",)], writes=[("scbc",)])
                p.op("vector", lambda e, c=c: e.tensor_tensor(
                    Cph[:], scbc[:], rep_mid(Cfull_sb[:, c * CH:(c + 1) * CH], HHALF),
                    op=ALU.mult), reads=[("scbc",), ("Cfull",)], writes=[("Cph",)])
                for jj in range(HHALF // 2):
                    j = hh * (HHALF // 2) + jj
                    bank, kb = psC[jj % 2], ("psC", jj % 2)
                    for u in range(2):
                        h = 2 * j + u
                        lh = h - h0
                        rows = slice(64 * u, 64 * u + 64)
                        p.op("tensor", lambda e, bank=bank, rows=rows, h=h, lh=lh:
                             e.matmul(bank[rows, 0:CH], S0b_sb[:, h * 64:h * 64 + 64],
                                      Cph[:, lh * CH:(lh + 1) * CH], start=True, stop=True),
                             reads=[("S0b",), ("Cph",)], writes=[kb])
                    p.op("vector", lambda e, j=j, bank=bank: e.tensor_tensor(
                        yc2[:, j * CH:(j + 1) * CH], yc2[:, j * CH:(j + 1) * CH],
                        bank[:, 0:CH], op=ALU.add),
                        reads=[("yc2",), kb], writes=[("yc2",)])
            p.dma("sync", "c2_z", zc2[:].rearrange("p (nt l) -> p nt l", l=CH),
                  z_d.rearrange("(nt p) t -> p nt t", p=128)[:, :, c * CH:(c + 1) * CH],
                  reads=[("d_z",)], writes=[("zc2",)])
            p.op("scalar", lambda e: e.activation(slz[:], zc2[:], AF.Silu),
                 reads=[("zc2",)], writes=[("slz",)])
            p.op("vector", lambda e: e.tensor_tensor(yc2[:], yc2[:], slz[:], op=ALU.mult),
                 reads=[("yc2",), ("slz",)], writes=[("yc2",)])
            p.op("scalar", lambda e: e.activation(sqz[:], yc2[:], AF.Square),
                 reads=[("yc2",)], writes=[("sqz",)])
            for nt in range(NT_Z):
                p.op("tensor", lambda e, nt=nt: e.matmul(
                    psR[0:1, 0:CH], ones_col_sb[:], sqz[:, nt * CH:(nt + 1) * CH],
                    start=(nt == 0), stop=(nt == NT_Z - 1)),
                    reads=[("sqz",), KC], writes=[("psRs",)])
            p.op("scalar", lambda e: e.activation(rrow[:], psR[0:1, 0:CH], AF.Ln,
                                                  bias=misc_sb[0:1, 0:1],
                                                  scale=misc_sb[0:1, 2:3]),
                 reads=[("psRs",), KC], writes=[("rrow",)])
            p.op("scalar", lambda e: e.activation(rrow[:], rrow[:], AF.Exp,
                                                  scale=misc_sb[0:1, 3:4]),
                 reads=[("rrow",), KC], writes=[("rrow",)])
            p.op("tensor", lambda e: e.matmul(psR[:, 512:512 + CH], ones_row_sb[:],
                                              rrow[:], start=True, stop=True),
                 reads=[("rrow",), KC], writes=[("psRb",)])
            for nt in range(NT_Z):
                p.op("vector", lambda e, nt=nt, zrel=zrel: e.scalar_tensor_tensor(
                    yn_sb[:, nt * HALF_T + zrel * CH:nt * HALF_T + (zrel + 1) * CH],
                    yc2[:, nt * CH:(nt + 1) * CH], normw_sb[:, nt:nt + 1],
                    psR[:, 512:512 + CH], op0=ALU.mult, op1=ALU.mult),
                    reads=[("yc2",), ("psRb",), KC], writes=[("yn", zrel // 4)])
            if ((zrel + 1) * CH) % 512 == 0 or zrel == NCH // 2 - 1:
                if ((zrel + 1) * CH) % 512 == 0 or ((zrel + 1) * CH) == HALF_T:
                    emit_d_group(half, zrel * CH // 512)


    p.barrier()
    ph.close()
    ms2.close()

    ph = ExitStack()
    x2acc = sbuf(ph, "x2acc", (128, 16 * LG), F32)
    ln2 = sbuf(ph, "ln2", (128, 16 * LG), BF16)
    hsb = sbuf(ph, "hsb", (128, 16 * LG), BF16)
    sqE = [sbuf(ph, f"sqE{i}", (128, LG), F32) for i in range(2)]
    gkE = sbuf(ph, "gkE", (128, 16 * 8), BF16)
    rowsE = sbuf(ph, "rowsE", (1, 3 * LG), F32)
    abE = sbuf(ph, "abE", (128, 2 * LG), F32)
    g8 = sbuf(ph, "g8", (128, 8), F32)
    m8 = sbuf(ph, "m8", (128, 8), F32)
    maskE = sbuf(ph, "maskE", (128, 8), F32)
    eallE = sbuf(ph, "eallE", (128, 8), F32)
    em2E = sbuf(ph, "em2E", (128, 1), F32)
    denE = sbuf(ph, "denE", (128, 1), F32)
    negm1E = sbuf(ph, "negm1E", (128, 1), F32)
    cwE = sbuf(ph, "cwE", (128, 8), F32)
    cwTE = sbuf(ph, "cwTE", (8, LG), F32)
    cwrow = sbuf(ph, "cwrow", (1, LG), F32)
    cwb = sbuf(ph, "cwb", (128, LG), F32)
    usb = sbuf(ph, "usb", (128, LG), F32)
    usb2 = sbuf(ph, "usb2", (128, LG), F32)
    ssb = sbuf(ph, "ssb", (128, LG), BF16)
    ssb2 = sbuf(ph, "ssb2", (128, LG), BF16)
    wiA = [sbuf(ph, f"wiA{i}", (128, 16 * 128), BF16) for i in range(2)]
    wiB = [sbuf(ph, f"wiB{i}", (128, 16 * 128), BF16) for i in range(2)]
    woE = [sbuf(ph, f"woE{i}", (128, 16 * 128), BF16) for i in range(2)]
    psH0 = ph.enter_context(nc.psum_tensor("psH0", [128, 1024], F32))
    psH1 = ph.enter_context(nc.psum_tensor("psH1", [128, 1024], F32))
    psW = ph.enter_context(nc.psum_tensor("psW", [128, 1024], F32))
    psE = ph.enter_context(nc.psum_tensor("psE", [128, 1024], F32))
    KH0, KH1, KW_, KE = ("psH0",), ("psH1",), ("psW",), ("psE",)
    LSL = slices(LG, 512)

    p.dma("gpsimd", "e_gk", gkE[:], gk_r.rearrange("p k e -> p (k e)"), writes=[("gkE",)])

    for lg in range(NLG):
        t0 = lg * LG
        for dt in range(16):
            p.dma("sync", "e_x1", x2acc[:, dt * LG:(dt + 1) * LG],
                  x1_d[dt][:, t0:t0 + LG], reads=[("d_x1",)], writes=[("x2a", dt)])
        for dt in range(16):
            par = dt % 2
            p.op("scalar", lambda e, dt=dt, par=par: e.activation(
                sqE[par][:], x2acc[:, dt * LG:(dt + 1) * LG], AF.Square),
                reads=[("x2a", dt)], writes=[("sqE", par)])
            for (n0, n1) in LSL:
                p.op("tensor", lambda e, dt=dt, n0=n0, n1=n1: e.matmul(
                    psE[0:1, n0:n1], ones_col_sb[:], x2acc[:, dt * LG + n0:dt * LG + n1],
                    start=(dt == 0), stop=(dt == 15)),
                    reads=[("x2a", dt), KC], writes=[KE])
                p.op("tensor", lambda e, par=par, n0=n0, n1=n1, dt=dt: e.matmul(
                    psW[0:1, n0:n1], ones_col_sb[:], sqE[par][:, n0:n1],
                    start=(dt == 0), stop=(dt == 15)),
                    reads=[("sqE", par), KC], writes=[KW_])
        p.op("vector", lambda e: e.tensor_scalar(
            rowsE[0:1, 0:LG], psE[0:1, 0:LG], 1.0 / D_MODEL, None, op0=ALU.mult),
            reads=[KE], writes=[("rowsE",)])
        p.op("vector", lambda e: e.tensor_scalar(
            rowsE[0:1, LG:2 * LG], psW[0:1, 0:LG], 1.0 / D_MODEL, None, op0=ALU.mult),
            reads=[KW_, ("rowsE",)], writes=[("rowsE",)])
        p.op("vector", lambda e: e.tensor_tensor(
            rowsE[0:1, 2 * LG:3 * LG], rowsE[0:1, 0:LG], rowsE[0:1, 0:LG], op=ALU.mult),
            reads=[("rowsE",)], writes=[("rowsE",)])
        p.op("vector", lambda e: e.tensor_tensor(
            rowsE[0:1, LG:2 * LG], rowsE[0:1, LG:2 * LG], rowsE[0:1, 2 * LG:3 * LG],
            op=ALU.subtract), reads=[("rowsE",)], writes=[("rowsE",)])
        p.op("scalar", lambda e: e.activation(rowsE[0:1, LG:2 * LG],
                                              rowsE[0:1, LG:2 * LG], AF.Sqrt,
                                              bias=misc_sb[0:1, 0:1]),
             reads=[("rowsE",), KC], writes=[("rowsE",)])
        p.op("vector", lambda e: e.reciprocal(rowsE[0:1, LG:2 * LG], rowsE[0:1, LG:2 * LG]),
             reads=[("rowsE",)], writes=[("rowsE",)])
        p.op("vector", lambda e: e.tensor_tensor(
            rowsE[0:1, 0:LG], rowsE[0:1, 0:LG], rowsE[0:1, LG:2 * LG], op=ALU.mult),
            reads=[("rowsE",)], writes=[("rowsE",)])
        for (n0, n1) in LSL:
            p.op("tensor", lambda e, n0=n0, n1=n1: e.matmul(
                psE[:, n0:n1], ones_row_sb[:], rowsE[0:1, LG + n0:LG + n1],
                start=True, stop=True), reads=[("rowsE",), KC], writes=[KE])
            p.op("vector", lambda e, n0=n0, n1=n1: e.tensor_copy(
                abE[:, n0:n1], psE[:, n0:n1]), reads=[KE], writes=[("abE",)])
            p.op("tensor", lambda e, n0=n0, n1=n1: e.matmul(
                psW[:, n0:n1], ones_row_sb[:], rowsE[0:1, n0:n1],
                start=True, stop=True), reads=[("rowsE",), KC], writes=[KW_])
            p.op("vector", lambda e, n0=n0, n1=n1: e.tensor_copy(
                abE[:, LG + n0:LG + n1], psW[:, n0:n1]), reads=[KW_], writes=[("abE",)])
        for dt in range(16):
            par = dt % 2
            p.op("vector", lambda e, dt=dt, par=par: e.tensor_tensor(
                sqE[par][:], x2acc[:, dt * LG:(dt + 1) * LG], abE[:, 0:LG], op=ALU.mult),
                reads=[("x2a", dt), ("abE",)], writes=[("sqE", par)])
            p.op("vector", lambda e, dt=dt, par=par: e.tensor_tensor(
                ln2[:, dt * LG:(dt + 1) * LG], sqE[par][:], abE[:, LG:2 * LG],
                op=ALU.subtract),
                reads=[("sqE", par), ("abE",)], writes=[("ln2", dt)])
        KLN2 = [("ln2", dt) for dt in range(16)]
        for st in range(LG // 128):
            for kt in range(16):
                p.op("tensor", lambda e, kt=kt, st=st: e.matmul(
                    psE[:, 0:8], ln2[:, kt * LG + st * 128:kt * LG + (st + 1) * 128],
                    gkE[:, kt * 8:(kt + 1) * 8], start=(kt == 0), stop=(kt == 15)),
                    reads=[KLN2[kt], ("gkE",)], writes=[KE])
            p.op("vector", lambda e: e.tensor_copy(g8[:], psE[:, 0:8]),
                 reads=[KE], writes=[("g8",)])
            p.op("vector", lambda e: e.max(out=m8[:], in_=g8[:]),
                 reads=[("g8",)], writes=[("m8",)])
            p.op("vector", lambda e: e.tensor_scalar(maskE[:], g8[:], m8[:, 1:2], None,
                                                     op0=ALU.is_ge),
                 reads=[("g8",), ("m8",)], writes=[("maskE",)])
            p.op("vector", lambda e: e.tensor_scalar(negm1E[:], m8[:, 0:1], -1.0, None,
                                                     op0=ALU.mult),
                 reads=[("m8",)], writes=[("negm1",)])
            p.op("scalar", lambda e: e.activation(eallE[:], g8[:], AF.Exp, bias=negm1E[:]),
                 reads=[("g8",), ("negm1",)], writes=[("eall",)])
            p.op("scalar", lambda e: e.activation(em2E[:], m8[:, 1:2], AF.Exp, bias=negm1E[:]),
                 reads=[("m8",), ("negm1",)], writes=[("em2",)])
            p.op("vector", lambda e: e.tensor_scalar(denE[:], em2E[:], 1.0, None, op0=ALU.add),
                 reads=[("em2",)], writes=[("den",)])
            p.op("vector", lambda e: e.reciprocal(denE[:], denE[:]),
                 reads=[("den",)], writes=[("den",)])
            p.op("vector", lambda e: e.scalar_tensor_tensor(
                cwE[:], eallE[:], denE[:], maskE[:], op0=ALU.mult, op1=ALU.mult),
                reads=[("eall",), ("den",), ("maskE",)], writes=[("cwE",)])
            p.op("tensor", lambda e: e.transpose(psE[0:8, 512:512 + 128], cwE[:],
                                                 identf_sb[:]),
                 reads=[("cwE",), KC], writes=[KE])
            p.op("vector", lambda e, st=st: e.tensor_copy(
                cwTE[:, st * 128:(st + 1) * 128], psE[0:8, 512:512 + 128]),
                reads=[KE], writes=[("cwT",)])
        p.dma("scalar", "e_cw", cw_d[:, t0:t0 + LG], cwTE[:],
              reads=[("cwT",)], writes=[("d_cw",)])

        for ex in range(8):
            p.dma("sync", "e_cwrow", cwrow[:], cw_d[ex:ex + 1, t0:t0 + LG],
                  reads=[("d_cw",)], writes=[("cwrow",)])
            for (n0, n1) in LSL:
                p.op("tensor", lambda e, n0=n0, n1=n1: e.matmul(
                    psE[:, n0:n1], ones_row_sb[:], cwrow[0:1, n0:n1],
                    start=True, stop=True), reads=[("cwrow",), KC], writes=[KE])
                p.op("vector", lambda e, n0=n0, n1=n1: e.tensor_copy(
                    cwb[:, n0:n1], psE[:, n0:n1]), reads=[KE], writes=[("cwb",)])
            for mt in range(16):
                par = mt % 2
                ph0, kh0 = (psH0, KH0) if par == 0 else (psW, KW_)
                ph1, kh1 = (psH1, KH1) if par == 0 else (psE, KE)
                p.dma("gpsimd", f"e_wi0_{par}", wiA[par][:],
                      wi0_r[ex, mt].rearrange("d k m -> d (k m)"), writes=[("wiA", par)])
                p.dma("gpsimd", f"e_wi1_{par}", wiB[par][:],
                      wi1_r[ex, mt].rearrange("d k m -> d (k m)"), writes=[("wiB", par)])
                for kt in range(16):
                    for (n0, n1) in LSL:
                        p.op("tensor", lambda e, par=par, kt=kt, n0=n0, n1=n1, ph0=ph0:
                             e.matmul(ph0[:, n0:n1], wiA[par][:, kt * 128:(kt + 1) * 128],
                                      ln2[:, kt * LG + n0:kt * LG + n1],
                                      start=(kt == 0), stop=(kt == 15)),
                             reads=[("wiA", par), KLN2[kt]], writes=[kh0])
                        p.op("tensor", lambda e, par=par, kt=kt, n0=n0, n1=n1, ph1=ph1:
                             e.matmul(ph1[:, n0:n1], wiB[par][:, kt * 128:(kt + 1) * 128],
                                      ln2[:, kt * LG + n0:kt * LG + n1],
                                      start=(kt == 0), stop=(kt == 15)),
                             reads=[("wiB", par), KLN2[kt]], writes=[kh1])
                usbp, kusb = (usb, ("usb", 0)) if par == 0 else (usb2, ("usb", 1))
                ssbp, kssb = (ssb, ("ssb", 0)) if par == 0 else (ssb2, ("ssb", 1))
                p.op("vector", lambda e, ph1=ph1, usbp=usbp: e.tensor_tensor(
                    usbp[:], ph1[:, 0:LG], cwb[:], op=ALU.mult),
                    reads=[kh1, ("cwb",)], writes=[kusb])
                p.op("scalar", lambda e, ph0=ph0, ssbp=ssbp: e.activation(
                    ssbp[:], ph0[:, 0:LG], AF.Silu),
                    reads=[kh0], writes=[kssb])
                p.op("vector", lambda e, mt=mt, usbp=usbp, ssbp=ssbp: e.tensor_tensor(
                    hsb[:, mt * LG:(mt + 1) * LG], usbp[:], ssbp[:], op=ALU.mult),
                    reads=[kusb, kssb], writes=[("hsb", mt)])
            for dt in range(16):
                par = dt % 2
                pw, kw2 = (psH0, KH0) if par == 0 else (psW, KW_)
                p.dma("gpsimd", f"e_wo_{par}", woE[par][:],
                      wo_r[ex, dt].rearrange("m t d -> m (t d)"), writes=[("woE", par)])
                for mt in range(16):
                    for (n0, n1) in LSL:
                        p.op("tensor", lambda e, par=par, mt=mt, n0=n0, n1=n1, pw=pw:
                             e.matmul(pw[:, n0:n1], woE[par][:, mt * 128:(mt + 1) * 128],
                                      hsb[:, mt * LG + n0:mt * LG + n1],
                                      start=(mt == 0), stop=(mt == 15)),
                             reads=[("woE", par), ("hsb", mt)], writes=[kw2])
                for (n0, n1) in LSL:
                    p.op("vector", lambda e, dt=dt, n0=n0, n1=n1, pw=pw: e.tensor_tensor(
                        x2acc[:, dt * LG + n0:dt * LG + n1],
                        x2acc[:, dt * LG + n0:dt * LG + n1], pw[:, n0:n1], op=ALU.add),
                        reads=[("x2a", dt), kw2], writes=[("x2a", dt)])
        for dt in range(16):
            p.dma("scalar", "e_out", out_d[dt][:, t0:t0 + LG],
                  x2acc[:, dt * LG:(dt + 1) * LG],
                  reads=[("x2a", dt)], writes=[("d_out",)])

    p.fence("gpsimd", [("d_out",)])
    p.build()
    return nc


def kernel(**inputs):
    T = 2048
    in_maps = host_layouts(inputs, T)
    nc = build_program(T)
    res = run_bass_kernel_spmd(nc, in_maps, core_ids=list(range(8)))
    out = np.zeros((4, 4096, 2048), np.float32)
    for core in range(8):
        b, half = core // 2, core % 2
        o = np.asarray(res.results[core]["out_fm"])
        out[b, half * T:(half + 1) * T, :] = o.reshape(2048, T).T
    return out


# version 14
# speedup vs baseline: 1.1841x; 1.0168x over previous
import numpy as np
import ml_dtypes
from contextlib import ExitStack

import concourse.bass as bass
import concourse.mybir as mybir
from concourse.bass_utils import run_bass_kernel_spmd

F32 = mybir.dt.float32
BF16 = mybir.dt.bfloat16
AF = mybir.ActivationFunctionType
ALU = mybir.AluOpType
bfloat16 = ml_dtypes.bfloat16

D_MODEL = 2048
HEADDIM = 64
CH = 128
D_INNER = 4096
NHEADS = 64
NPAIR = 32
CONV_DIM = 4352
D_IN_PROJ = 8512
NT_Z = 32
NT_XBC = 34
NT_IN = 67
EPS = 1e-5
NEG_BIG = -1e9
HHALF = 16

ENGS = ("tensor", "vector", "scalar", "gpsimd", "sync")


def rep_mid(ap2d, reps):
    return ap2d.unsqueeze(1).to_broadcast((ap2d.shape[0], reps, ap2d.shape[1]))


def rep_last(ap2d, reps):
    return ap2d.to_broadcast((ap2d.shape[0], ap2d.shape[1], reps))


def slices(total, step):
    out = []
    o = 0
    while o < total:
        out.append((o, min(total, o + step)))
        o += step
    return out


class Prog:
    def __init__(self, nc):
        self.nc = nc
        self.es = ExitStack()
        self.sems = {}
        self.count = {}
        self.watermark = {}
        self.last_write = {}
        self.readers = {}
        self.ops = []
        for e in ENGS:
            self._new_sem(e)

    def _new_sem(self, name):
        self.sems[name] = self.es.enter_context(self.nc.semaphore("sem_" + name))
        self.count[name] = 0

    def dma_group(self, name):
        if name not in self.sems:
            self._new_sem(name)
        return name

    def _need_waits(self, eng, reads, writes):
        need = {}

        def add(sem, val):
            if val > need.get(sem, 0):
                need[sem] = val

        for r in reads:
            lw = self.last_write.get(r)
            if lw:
                add(*lw)
        for w in writes:
            lw = self.last_write.get(w)
            if lw:
                add(*lw)
            for sem, val in self.readers.get(w, {}).items():
                add(sem, val)
        out = []
        for sem, val in need.items():
            if sem == eng == "tensor":
                continue
            if sem not in ENGS:
                val = self.count[sem]
            wm = self.watermark.get((eng, sem), 0)
            if val > wm:
                out.append((sem, val))
                self.watermark[(eng, sem)] = val
        return out

    def op(self, eng, emit_fn, reads=(), writes=(), tag=None):
        waits = self._need_waits(eng, reads, writes)
        self.count[eng] += 1
        v = self.count[eng]
        self.ops.append((eng, waits, emit_fn, eng, 1, tag))
        for w in writes:
            self.last_write[w] = (eng, v)
            self.readers[w] = {}
        for r in reads:
            self.readers.setdefault(r, {})[eng] = v

    def _dma_common(self, q, group, emit, reads, writes, tag):
        g = self.dma_group(group)
        waits = self._need_waits(q, reads, writes)
        self.count[g] += 16
        v = self.count[g]
        self.ops.append((q, waits, emit, g, 16, tag))
        for w in writes:
            self.last_write[w] = (g, v)
            self.readers[w] = {}
        for r in reads:
            self.readers.setdefault(r, {})[g] = v

    def dma(self, q, group, out, in_, reads=(), writes=(), tag=None, **kw):
        def emit(e, out=out, in_=in_, kw=kw):
            return e.dma_start(out=out, in_=in_, **kw)
        self._dma_common(q, group, emit, reads, writes, tag)

    def dma_tr(self, q, group, out, in_, reads=(), writes=(), tag=None):
        def emit(e, out=out, in_=in_):
            return e.dma_start_transpose(out=out, in_=in_)
        self._dma_common(q, group, emit, reads, writes, tag)

    def collective(self, emit_fn, reads=(), writes=(), tag="cc"):
        self._dma_common("gpsimd", "cc", emit_fn, reads, writes, tag)

    def nop_marker(self, eng, reads, writes):
        self.op(eng, lambda e: e.engine_nop(), reads=reads, writes=writes)

    def barrier(self):
        snap = dict(self.count)
        for e in ENGS:
            waits = []
            for sem, val in snap.items():
                if sem == e or val <= 0:
                    continue
                wm = self.watermark.get((e, sem), 0)
                if val > wm:
                    waits.append((sem, val))
                    self.watermark[(e, sem)] = val
            if waits:
                self.ops.append((e, waits, None, None, 0, "barrier"))

    def fence(self, eng, keys):
        waits = self._need_waits(eng, keys, ())
        if waits:
            self.ops.append((eng, waits, None, None, 0, "fence"))

    def build(self):
        per_eng = {}
        for rec in self.ops:
            per_eng.setdefault(rec[0], []).append(rec)
        block = self.es.enter_context(self.nc.Block())
        for eng_name in ENGS:
            recs = per_eng.get(eng_name, [])
            if not recs:
                continue
            section = getattr(block, eng_name)

            def body(engine, recs=recs):
                for (_e, waits, emit_fn, inc_sem, inc_val, _tag) in recs:
                    for sem, val in waits:
                        engine.wait_ge(self.sems[sem], val)
                    if emit_fn is not None:
                        inst = emit_fn(engine)
                        if inc_sem is not None:
                            inst.then_inc(self.sems[inc_sem], inc_val)

            section(body)



def host_layouts(inputs, T=2048):
    f32 = np.float32
    x = np.ascontiguousarray(np.asarray(inputs["L0_attn_x"], f32))
    w_in = np.asarray(inputs["L0_attn_w_in"], f32)
    conv_w = np.asarray(inputs["L0_attn_conv_w"], f32)
    conv_b = np.asarray(inputs["L0_attn_conv_b"], f32)
    dt_bias = np.asarray(inputs["L0_attn_dt_bias"], f32)
    A_log = np.asarray(inputs["L0_attn_A_log"], f32)
    Dv = np.asarray(inputs["L0_attn_D"], f32)
    norm_w = np.asarray(inputs["L0_attn_norm_w"], f32)
    w_out = np.asarray(inputs["L0_attn_w_out"], f32)
    gk = np.asarray(inputs["L0_ffn_gate_kernel"], f32)
    wi0 = np.asarray(inputs["L0_ffn_wi_0"], f32)
    wi1 = np.asarray(inputs["L0_ffn_wi_1"], f32)
    wo = np.asarray(inputs["L0_ffn_wo"], f32)

    w_in_p = np.zeros((2048, NT_IN * 128), f32)
    w_in_p[:, :D_IN_PROJ] = w_in
    w_in_r = np.ascontiguousarray(
        w_in_p.reshape(16, 128, NT_IN, 128).transpose(2, 1, 0, 3))
    w_out_r = np.ascontiguousarray(
        w_out.reshape(32, 128, 16, 128).transpose(2, 1, 0, 3))
    wi0_r = np.ascontiguousarray(
        wi0.reshape(8, 16, 128, 16, 128).transpose(0, 3, 2, 1, 4))
    wi1_r = np.ascontiguousarray(
        wi1.reshape(8, 16, 128, 16, 128).transpose(0, 3, 2, 1, 4))
    wo_r = np.ascontiguousarray(
        wo.reshape(8, 16, 128, 16, 128).transpose(0, 3, 2, 1, 4))
    gk_r = np.ascontiguousarray(gk.reshape(16, 128, 8).transpose(1, 0, 2))
    conv_w_r = np.ascontiguousarray(conv_w.reshape(NT_XBC, 128, 4).transpose(1, 0, 2))
    conv_b_r = np.ascontiguousarray(conv_b.reshape(NT_XBC, 128).transpose(1, 0))
    D_cols = np.ascontiguousarray(
        Dv.reshape(NPAIR, 2)[:, :, None].repeat(HEADDIM, 2)
        .transpose(1, 2, 0).reshape(128, NPAIR))
    normw_cols = np.ascontiguousarray(norm_w.reshape(NT_Z, 128).transpose(1, 0))
    dtb = np.ascontiguousarray(dt_bias.reshape(NHEADS, 1))
    alog = np.ascontiguousarray(A_log.reshape(NHEADS, 1))

    li = np.arange(CH)
    shared = dict(
        w_in_r=w_in_r, w_out_r=w_out_r, wi0_r=wi0_r, wi1_r=wi1_r, wo_r=wo_r,
        gk_r=gk_r, conv_w_r=conv_w_r, conv_b_r=conv_b_r, D_cols=D_cols,
        normw_cols=normw_cols, dt_bias_c=dtb, A_log_c=alog,
        ident_bf=np.eye(128, dtype=bfloat16),
        ident_f32=np.eye(128, dtype=f32),
        ones_row_f32=np.ones((1, 128), f32),
        ones_col_f32=np.ones((128, 1), f32),
        masktri=np.where(li[None, :] >= li[:, None], 0.0, NEG_BIG).astype(f32),
        misc_cols=np.tile(np.array([[EPS, -1.0, 1.0 / D_INNER]], f32), (128, 1)),
    )

    in_maps = []
    for core in range(8):
        b, half = core // 2, core % 2
        x_tm = np.ascontiguousarray(x[b, half * T:(half + 1) * T])
        x_fm = np.ascontiguousarray(x_tm.T)
        x_ctx = np.zeros((128, 2048), f32)
        x_ptm = np.zeros((T, 2048), f32)
        if half == 1:
            x_ctx[0:3] = x[b, T - 3:T]
            x_ptm[:] = x[b, 0:T]
        m = dict(shared)
        m.update(x_tm=x_tm, x_fm=x_fm, x_ctx=x_ctx, x_ptm=x_ptm,
                 mask_first=np.full((128, 1), 0.0 if half else 1.0, f32),
                 mask_second=np.full((128, 1), 1.0 if half else 0.0, f32))
        in_maps.append(m)
    return in_maps



def build_program(T=2048, debug_taps=()):
    nc = bass.Bass("TRN2", target_bir_lowering=False)
    NCH = T // CH
    NST = T // 128
    LG = min(1024, T)
    NLG = T // LG
    HALF_T = T // 2
    p = Prog(nc)
    es = p.es

    def din(name, shape, dtype=F32):
        return nc.dram_tensor(name, list(shape), dtype, kind="ExternalInput").ap()

    def dint(name, shape, dtype):
        kind = "ExternalOutput" if name in debug_taps else "Internal"
        return nc.dram_tensor(name, list(shape), dtype, kind=kind).ap()

    x_tm = din("x_tm", (T, 2048))
    x_fm = din("x_fm", (2048, T))
    x_ctx = din("x_ctx", (128, 2048))
    x_ptm = din("x_ptm", (T, 2048))
    w_in_r = din("w_in_r", (NT_IN, 128, 16, 128))
    w_out_r = din("w_out_r", (16, 128, 32, 128))
    wi0_r = din("wi0_r", (8, 16, 128, 16, 128))
    wi1_r = din("wi1_r", (8, 16, 128, 16, 128))
    wo_r = din("wo_r", (8, 16, 128, 16, 128))
    gk_r = din("gk_r", (128, 16, 8))
    conv_w_d = din("conv_w_r", (128, NT_XBC, 4))
    conv_b_d = din("conv_b_r", (128, NT_XBC))
    D_cols_d = din("D_cols", (128, NPAIR))
    normw_d = din("normw_cols", (128, NT_Z))
    dtb_d = din("dt_bias_c", (NHEADS, 1))
    alog_d = din("A_log_c", (NHEADS, 1))
    ident_bf_d = din("ident_bf", (128, 128), BF16)
    ident_f32_d = din("ident_f32", (128, 128))
    ones_row_f32_d = din("ones_row_f32", (1, 128))
    ones_col_f32_d = din("ones_col_f32", (128, 1))
    masktri_d = din("masktri", (CH, CH))
    mask1_d = din("mask_first", (128, 1))
    mask2_d = din("mask_second", (128, 1))
    misc_d = din("misc_cols", (128, 3))

    out_d = nc.dram_tensor("out_fm", [16, 128, T], F32, kind="ExternalOutput").ap()

    ln1_d = dint("ln1_d", (T + 128, 2048), BF16)
    z_d = dint("z_d", (D_INNER, T), BF16)
    xbc_d = dint("xbc_d", (CONV_DIM, T), BF16)
    ln1p_d = dint("ln1p_d", (T, 2048), BF16)
    xbcp_d = dint("xbcp_d", (CONV_DIM, T), BF16)
    xh_d = dint("xh_d", (D_INNER, CH), BF16)
    acs_d = dint("acs_d", (NHEADS, CH), F32)
    ea_d = dint("ea_d", (NCH, NHEADS, CH), BF16)
    sc_d = dint("sc_d", (NHEADS, CH), BF16)
    y_d = dint("y_d", (NCH, 128, NT_Z * CH), BF16)
    x1_d = dint("x1_d", (16, 128, T), F32)
    cw_d = dint("cw_d", (8, T), F32)

    def sbuf(stack, name, shape, dtype, side=None):
        return stack.enter_context(nc.sbuf_tensor(name, list(shape), dtype, side=side))

    conv_w_sb = sbuf(es, "conv_w_sb", (128, NT_XBC * 4), F32)
    conv_b_sb = sbuf(es, "conv_b_sb", (128, NT_XBC), F32)
    D_cols_sb = sbuf(es, "D_cols_sb", (128, NPAIR), F32)
    normw_sb = sbuf(es, "normw_sb", (128, NT_Z), F32)
    dtb_sb = sbuf(es, "dtb_sb", (NHEADS, 1), F32)
    alog_sb = sbuf(es, "alog_sb", (NHEADS, 1), F32)
    expalog_sb = sbuf(es, "expalog_sb", (NHEADS, 1), F32)
    ident_sb = sbuf(es, "ident_sb", (128, 128), BF16)
    identf_sb = sbuf(es, "identf_sb", (128, 128), F32)
    ones_row_sb = sbuf(es, "ones_row_sb", (1, 128), F32)
    ones_col_sb = sbuf(es, "ones_col_sb", (128, 1), F32)
    masktri_sb = sbuf(es, "masktri_sb", (CH, CH), F32)
    mask1_sb = sbuf(es, "mask1_sb", (128, 1), F32)
    mask2_sb = sbuf(es, "mask2_sb", (128, 1), F32)
    misc_sb = sbuf(es, "misc_sb", (128, 3), F32)
    ms1 = ExitStack()
    dt_sb = sbuf(ms1, "dt_sb", (NHEADS, T), F32, side="right")
    dtp_sb = sbuf(ms1, "dtp_sb", (NHEADS, T), F32, side="right")
    ctx3_sb = sbuf(ms1, "ctx3_sb", (128, NT_XBC * 3), F32, side="right")
    xc = [sbuf(ms1, f"xc{i}", (128, NT_XBC * (CH + 3)), BF16, side="right") for i in range(2)]
    ms2 = ExitStack()
    S0b_sb = sbuf(ms2, "S0b_sb", (128, NHEADS * HEADDIM), BF16)
    Cfull_sb = sbuf(ms2, "Cfull_sb", (128, NCH * CH), BF16)
    Alast_sb = sbuf(ms2, "Alast_sb", (NHEADS, NCH), F32)
    cumprev_sb = sbuf(ms2, "cumprev_sb", (NHEADS, NCH), F32)

    KC = ("consts",)
    for dst, src in [
        (conv_w_sb, conv_w_d.rearrange("p a b -> p (a b)")), (conv_b_sb, conv_b_d),
        (D_cols_sb, D_cols_d), (normw_sb, normw_d), (dtb_sb, dtb_d),
        (alog_sb, alog_d), (ident_sb, ident_bf_d), (identf_sb, ident_f32_d),
        (ones_row_sb, ones_row_f32_d), (ones_col_sb, ones_col_f32_d),
        (masktri_sb, masktri_d), (mask1_sb, mask1_d), (mask2_sb, mask2_d),
        (misc_sb, misc_d),
    ]:
        p.dma("sync", "consts", dst[:], src, writes=[KC])
    p.op("scalar", lambda e: e.activation(expalog_sb[:], alog_sb[:], AF.Exp),
         reads=[KC], writes=[KC])

    ph = ExitStack()
    xst = [sbuf(ph, f"a1_x{i}", (128, 2048), F32) for i in range(2)]
    lnb = [sbuf(ph, f"a1_ln{i}", (128, 2048), BF16) for i in range(2)]
    st6 = sbuf(ph, "a1_st6", (128, 24), F32)
    mv_sb = sbuf(ph, "a1_mv", (128, 2), F32)
    rstd_sb = sbuf(ph, "a1_rstd", (128, 1), F32)
    mr_sb = sbuf(ph, "a1_mr", (128, 1), F32)

    for st in range(2 * NST + 1):
        par = st % 2
        xb, lb = xst[par], lnb[par]
        kx, kl = ("a1x", par), ("a1l", par)
        if st < NST:
            src = x_tm[st * 128:(st + 1) * 128, :]
        elif st == NST:
            src = x_ctx[:, :]
        else:
            sp = st - NST - 1
            src = x_ptm[sp * 128:(sp + 1) * 128, :]
        p.dma("scalar", f"a1_in{par}", xb[:], src, writes=[kx])
        for q in range(4):
            p.op("vector", lambda e, xb=xb, q=q: e.bn_stats(
                out=st6[:, q * 6:(q + 1) * 6], in_=xb[:, q * 512:(q + 1) * 512]),
                reads=[kx], writes=[("a1st6",)])
        p.op("vector", lambda e: e.bn_aggr(out=mv_sb[:], in_=st6[:]),
             reads=[("a1st6",)], writes=[("a1mv",)])
        p.op("scalar", lambda e: e.activation(rstd_sb[:], mv_sb[:, 1:2], AF.Sqrt, bias=misc_sb[:, 0:1]),
             reads=[("a1mv",), KC], writes=[("a1rstd",)])
        p.op("vector", lambda e: e.reciprocal(rstd_sb[:], rstd_sb[:]),
             reads=[("a1rstd",)], writes=[("a1rstd",)])
        p.op("vector", lambda e: e.tensor_tensor(mr_sb[:], mv_sb[:, 0:1], rstd_sb[:], op=ALU.mult),
             reads=[("a1mv",), ("a1rstd",)], writes=[("a1mr",)])
        p.op("vector", lambda e, xb=xb, lb=lb: e.tensor_scalar(
            lb[:], xb[:], rstd_sb[:], mr_sb[:], op0=ALU.mult, op1=ALU.subtract),
            reads=[kx, ("a1rstd",), ("a1mr",)], writes=[kl])
        if st <= NST:
            p.dma("scalar", "a1_out", ln1_d[st * 128:(st + 1) * 128, :], lb[:],
                  reads=[kl], writes=[("d_ln1",)])
        else:
            sp = st - NST - 1
            p.dma("scalar", "a1_out", ln1p_d[sp * 128:(sp + 1) * 128, :], lb[:],
                  reads=[kl], writes=[("d_ln1p",)])

    p.barrier()
    ph.close()

    c1s = ExitStack()
    convo = sbuf(c1s, "convo", (128, NT_XBC * CH), F32)
    xbcs = sbuf(c1s, "xbcs", (128, NT_XBC * CH), BF16)
    dtc = sbuf(c1s, "dtc", (NHEADS, CH), F32)
    acs = sbuf(c1s, "acs", (NHEADS, CH), F32)
    nacs = sbuf(c1s, "nacs", (NHEADS, CH), F32)
    expa = sbuf(c1s, "expa", (NHEADS, CH), BF16)
    decay = sbuf(c1s, "decay", (NHEADS, CH), F32)
    ddec = sbuf(c1s, "ddec", (NHEADS, CH), F32)
    t64a = sbuf(c1s, "t64a", (NHEADS, CH), F32)
    t64b = sbuf(c1s, "t64b", (NHEADS, CH), F32)
    eacol = sbuf(c1s, "eacol", (NHEADS, 1), F32)
    dtT = sbuf(c1s, "dtT", (128, NHEADS), F32)
    ddecT = sbuf(c1s, "ddecT", (128, NHEADS), F32)
    nacsT = sbuf(c1s, "nacsT", (128, NHEADS), F32)
    EAb = sbuf(c1s, "EAb", (128, NHEADS), F32)
    earow = sbuf(c1s, "earow", (1, NHEADS), F32)
    xhtm = sbuf(c1s, "xhtm", (128, D_INNER), BF16)
    Xdec = sbuf(c1s, "Xdec", (128, D_INNER), BF16)
    Btm = sbuf(c1s, "Btm", (128, CH), BF16)
    Gsb = sbuf(c1s, "Gsb", (128, CH), BF16)
    t128 = sbuf(c1s, "t128", (128, CH), F32)
    S_sb = sbuf(c1s, "S_sb", (128, NHEADS * HEADDIM), F32)
    psS = [c1s.enter_context(nc.psum_tensor(f"psS{i}", [128, 512], F32)) for i in range(2)]
    psM = c1s.enter_context(nc.psum_tensor("psM", [128, 512], F32))
    psMb = c1s.enter_context(nc.psum_tensor("psMb", [128, 1024], BF16))
    KM, KMB = ("psM",), ("psMb",)
    CHW = CH + 3

    a2s = ExitStack()
    TC = T + 128
    ln1f = sbuf(a2s, "ln1f", (128, 16 * TC), BF16, side="right")
    w2 = [sbuf(a2s, f"a2_w{i}", (128, 16 * 128), BF16, side="right") for i in range(2)]
    ev_b = [sbuf(a2s, f"a2_eb{i}", (128, T), BF16, side="right") for i in range(2)]
    ev_f = sbuf(a2s, "a2_ef", (128, T), F32, side="right")
    ctx_ev = sbuf(a2s, "a2_ctxev", (128, 128), F32, side="right")
    ps_a2 = [a2s.enter_context(nc.psum_tensor(f"psA{i}", [128, 512], F32))
             for i in range(4)]
    TSL = slices(T, 512)
    KLN1F = [("ln1f", kt) for kt in range(16)]

    def emit_a2_tile(nt, need_ctx, xbc_dst, kxbc_dst, dt_dst, kdt_dst, z_out):
        if True:
            par = nt % 2
            wb = w2[par]
            kw = ("a2w", par)
            p.dma("gpsimd", f"a2_w{par}", wb[:], w_in_r[nt].rearrange("d k n -> d (k n)"),
                  writes=[kw], tag=f"w_in {nt}")
            pset = [ps_a2[i] for i in range(len(TSL))]
            pk = [("psA", i) for i in range(len(TSL))]
            for kt in range(16):
                for s, (n0, n1) in enumerate(TSL):
                    p.op("tensor",
                         lambda e, wb=wb, kt=kt, n0=n0, n1=n1, ps=pset[s]:
                         e.matmul(ps[:, 0:n1 - n0], wb[:, kt * 128:(kt + 1) * 128],
                                  ln1f[:, kt * TC + n0:kt * TC + n1],
                                  start=(kt == 0), stop=(kt == 15)),
                         reads=[kw, KLN1F[kt]], writes=[pk[s]])
            is_z = nt < NT_Z
            is_dt = nt >= NT_Z + NT_XBC
            eb = ev_b[par]
            keb = ("a2eb", par)
            for s, (n0, n1) in enumerate(TSL):
                if is_dt:
                    p.op("vector", lambda e, ps=pset[s], n0=n0, n1=n1:
                         e.tensor_copy(ev_f[:, n0:n1], ps[:, 0:n1 - n0]),
                         reads=[pk[s]], writes=[("a2ef",)])
                elif s % 2 == 0:
                    p.op("vector", lambda e, ps=pset[s], n0=n0, n1=n1, eb=eb:
                         e.tensor_copy(eb[:, n0:n1], ps[:, 0:n1 - n0]),
                         reads=[pk[s]], writes=[keb])
                else:
                    p.op("scalar", lambda e, ps=pset[s], n0=n0, n1=n1, eb=eb:
                         e.copy(eb[:, n0:n1], ps[:, 0:n1 - n0]),
                         reads=[pk[s]], writes=[keb])
            if is_z:
                p.dma("scalar", "a2_zout", z_out[nt * 128:(nt + 1) * 128, :], eb[:, 0:T],
                      reads=[keb], writes=[("d_z",)])
            elif not is_dt:
                j = nt - NT_Z
                p.dma("scalar", "a2_xout", xbc_dst[j * 128:(j + 1) * 128, :], eb[:, 0:T],
                      reads=[keb], writes=[kxbc_dst])
                if need_ctx:
                    pc, kpc = pset[0], pk[0]
                    for kt in range(16):
                        p.op("tensor", lambda e, wb=wb, kt=kt, pc=pc:
                             e.matmul(pc[:, 0:128], wb[:, kt * 128:(kt + 1) * 128],
                                      ln1f[:, kt * TC + T:kt * TC + T + 128],
                                      start=(kt == 0), stop=(kt == 15)),
                             reads=[kw, KLN1F[kt]], writes=[kpc])
                    p.op("vector", lambda e, pc=pc: e.tensor_copy(ctx_ev[:], pc[:, 0:128]),
                         reads=[kpc], writes=[("a2ctx",)])
                    p.op("vector", lambda e, j=j: e.tensor_copy(
                        ctx3_sb[:, j * 3:(j + 1) * 3], ctx_ev[:, 0:3]),
                        reads=[("a2ctx",)], writes=[("ctx3",)])
            else:
                p.op("vector", lambda e, dt_dst=dt_dst: e.tensor_copy(
                    dt_dst[:, 0:T], ev_f[0:64, 0:T]),
                    reads=[("a2ef",)], writes=[kdt_dst])

    def emit_chunk(c, full, xbcd, kxbcd, dtbuf, kdtbuf):
        par = c % 2
        xcb = xc[par]
        kxc = ("xc", par)
        p.dma("sync", f"c1_xbc{par}",
              xcb[:].rearrange("p (nt w) -> p nt w", w=CHW)[:, :, 3:3 + CH],
              xbcd.rearrange("(nt p) t -> p nt t", p=128)[:, :, c * CH:(c + 1) * CH],
              reads=[kxbcd], writes=[kxc])
        for nt in range(NT_XBC):
            o = nt * CHW
            co = nt * CH
            if nt % 2 == 0:
                p.op("vector", lambda e, o=o, co=co, nt=nt, xcb=xcb: e.tensor_scalar(
                    convo[:, co:co + CH], xcb[:, o:o + CH],
                    conv_w_sb[:, 4 * nt:4 * nt + 1], conv_b_sb[:, nt:nt + 1],
                    op0=ALU.mult, op1=ALU.add),
                    reads=[kxc, KC], writes=[("convo", nt)])
                for tap in range(1, 4):
                    p.op("vector", lambda e, o=o, co=co, nt=nt, tap=tap, xcb=xcb:
                         e.scalar_tensor_tensor(
                             convo[:, co:co + CH], xcb[:, o + tap:o + tap + CH],
                             conv_w_sb[:, 4 * nt + tap:4 * nt + tap + 1],
                             convo[:, co:co + CH], op0=ALU.mult, op1=ALU.add),
                         reads=[kxc, KC, ("convo", nt)], writes=[("convo", nt)])
            else:
                for tap in range(4):
                    wv = conv_w_sb[:, 4 * nt + tap:4 * nt + tap + 1].to_broadcast((128, 1, CH))
                    if tap == 0:
                        p.op("gpsimd", lambda e, o=o, co=co, wv=wv, xcb=xcb: e.tensor_tensor(
                            convo[:, co:co + CH], xcb[:, o:o + CH], wv, op=ALU.mult),
                            reads=[kxc, KC], writes=[("convo", nt)])
                    else:
                        p.op("gpsimd", lambda e, o=o, tap=tap, wv=wv, xcb=xcb:
                             e.tensor_tensor(t128[:, 0:CH], xcb[:, o + tap:o + tap + CH],
                                             wv, op=ALU.mult),
                             reads=[kxc, KC], writes=[("t128",)])
                        p.op("gpsimd", lambda e, co=co, nt=nt: e.tensor_tensor(
                            convo[:, co:co + CH], convo[:, co:co + CH], t128[:, 0:CH],
                            op=ALU.add),
                            reads=[("t128",), ("convo", nt)], writes=[("convo", nt)])
                p.op("gpsimd", lambda e, co=co, nt=nt: e.tensor_tensor(
                    convo[:, co:co + CH], convo[:, co:co + CH],
                    conv_b_sb[:, nt:nt + 1].to_broadcast((128, 1, CH)), op=ALU.add),
                    reads=[KC, ("convo", nt)], writes=[("convo", nt)])
            eng = "vector" if nt % 2 == 0 else "gpsimd"
            p.op(eng, lambda e, o=o, opar=1 - par, xcb=xcb: e.tensor_copy(
                xc[opar][:, o:o + 3], xcb[:, o + CH:o + CH + 3]),
                reads=[kxc], writes=[("xc", 1 - par)])
        KCV = [("convo", nt) for nt in range(NT_XBC)]
        if full:
            for nt in range(0, NT_XBC, 6):
                p.op("tensor", lambda e: e.matmul(
                    psM[:, 448:448 + 64], ones_row_sb[:], earow[:],
                    start=True, stop=True),
                    reads=[("convo", nt), ("earow",), KC], writes=[KM])
        p.op("scalar", lambda e: e.activation(xbcs[:], convo[:], AF.Silu),
             reads=KCV, writes=[("xbcs",)])
        B_nm = xbcs[:, NT_Z * CH:(NT_Z + 1) * CH]
        C_nm = xbcs[:, (NT_Z + 1) * CH:(NT_Z + 2) * CH]
        if full:
            p.op("vector", lambda e, c=c, C_nm=C_nm: e.tensor_copy(
                Cfull_sb[:, c * CH:(c + 1) * CH], C_nm),
                reads=[("xbcs",)], writes=[("Cfull",)])

        p.op("vector", lambda e, c=c, dtbuf=dtbuf: e.tensor_scalar(
            dtc[:], dtbuf[:, c * CH:(c + 1) * CH], dtb_sb[:], None, op0=ALU.add),
            reads=[kdtbuf, KC], writes=[("dtc",)])
        p.op("scalar", lambda e: e.activation(t64a[:], dtc[:], AF.Abs),
             reads=[("dtc",)], writes=[("t64a",)])
        p.op("scalar", lambda e: e.activation(t64a[:], t64a[:], AF.Exp, scale=misc_sb[0:NHEADS, 1:2]),
             reads=[("t64a",), KC], writes=[("t64a",)])
        p.op("scalar", lambda e: e.activation(t64b[:], t64a[:], AF.Ln, bias=1.0),
             reads=[("t64a",)], writes=[("t64b",)])
        p.op("vector", lambda e: e.tensor_scalar_max(dtc[:], dtc[:], 0.0),
             reads=[("dtc",), ("t64a",)], writes=[("dtc",)])
        p.op("vector", lambda e: e.tensor_tensor(dtc[:], dtc[:], t64b[:], op=ALU.add),
             reads=[("dtc",), ("t64b",)], writes=[("dtc",)])
        p.op("vector", lambda e: e.tensor_scalar(
            t64a[:], dtc[:], expalog_sb[:], None, op0=ALU.mult),
            reads=[("dtc",), KC], writes=[("t64a",)])
        p.op("vector", lambda e: e.tensor_tensor_scan(
            nacs[:], t64a[:], t64a[:], 0.0, op0=ALU.add, op1=ALU.bypass),
            reads=[("t64a",)], writes=[("nacs",)])
        p.op("vector", lambda e: e.tensor_scalar(acs[:], nacs[:], -1.0, None, op0=ALU.mult),
             reads=[("nacs",)], writes=[("acs",)])
        if full:
            p.dma("scalar", "c1_acs", acs_d[:, :], acs[:], reads=[("acs",)], writes=[("d_acs",)])
            p.op("scalar", lambda e: e.activation(expa[:], acs[:], AF.Exp),
                 reads=[("acs",)], writes=[("expa",)])
            p.dma("scalar", "c1_ea", ea_d[c], expa[:], reads=[("expa",)], writes=[("d_ea",)])
            p.op("vector", lambda e, c=c: e.tensor_copy(Alast_sb[:, c:c + 1], acs[:, CH - 1:CH]),
                 reads=[("acs",)], writes=[("Alast",)])
        p.op("scalar", lambda e: e.activation(decay[:], nacs[:], AF.Exp,
                                              bias=acs[:, CH - 1:CH]),
             reads=[("nacs",), ("acs",)], writes=[("decay",)])
        p.op("vector", lambda e: e.tensor_tensor(ddec[:], dtc[:], decay[:], op=ALU.mult),
             reads=[("dtc",), ("decay",)], writes=[("ddec",)])
        p.op("scalar", lambda e: e.activation(eacol[:], acs[:, CH - 1:CH], AF.Exp),
             reads=[("acs",)], writes=[("eacol",)])
        p.op("tensor", lambda e: e.transpose(psM[0:1, 384:384 + NHEADS], eacol[:],
                                             identf_sb[0:NHEADS, 0:NHEADS]),
             reads=[("eacol",), KC], writes=[KM])
        p.op("vector", lambda e: e.tensor_copy(earow[:], psM[0:1, 384:384 + NHEADS]),
             reads=[KM], writes=[("earow",)])
        p.op("tensor", lambda e: e.matmul(psM[:, 256:256 + NHEADS], ones_row_sb[:],
                                          earow[:], start=True, stop=True),
             reads=[("earow",), KC], writes=[KM])
        p.op("vector", lambda e: e.tensor_copy(EAb[:], psM[:, 256:256 + NHEADS]),
             reads=[KM], writes=[("EAb",)])
        trs = [(ddec, ddecT, ("ddec",), ("ddecT",))]
        if full:
            trs += [(dtc, dtT, ("dtc",), ("dtT",)), (nacs, nacsT, ("nacs",), ("nacsT",))]
        for tsrc, tdst, rk, wk in trs:
            p.op("tensor", lambda e, tsrc=tsrc: e.transpose(
                psM[:, 128:128 + NHEADS], tsrc[:], identf_sb[0:NHEADS, 0:NHEADS]),
                reads=[rk, KC], writes=[KM])
            p.op("vector", lambda e, tdst=tdst: e.tensor_copy(tdst[:], psM[:, 128:128 + NHEADS]),
                 reads=[KM], writes=[wk])

        for grp in range(4):
            for i8 in range(8):
                jx = grp * 8 + i8
                p.op("tensor", lambda e, i8=i8, jx=jx: e.transpose(
                    psMb[:, i8 * 128:(i8 + 1) * 128], xbcs[:, jx * CH:(jx + 1) * CH],
                    ident_sb[:]),
                    reads=[("xbcs",), KC], writes=[KMB])
            p.op("scalar", lambda e, grp=grp: e.copy(
                xhtm[:, grp * 1024:(grp + 1) * 1024], psMb[:, 0:1024]),
                reads=[KMB], writes=[("xhtm",)])
        if full:
            p.op("vector", lambda e: e.tensor_tensor(
                Xp[:], xhtm[:], rep_last(dtT[:], HEADDIM), op=ALU.mult),
                reads=[("xhtm",), ("dtT",)], writes=[("Xp",)])
        p.op("vector", lambda e: e.tensor_tensor(
            Xdec[:], xhtm[:], rep_last(ddecT[:], HEADDIM), op=ALU.mult),
            reads=[("xhtm",), ("ddecT",)], writes=[("Xdec",)])

        if full:
            p.op("tensor", lambda e, B_nm=B_nm, C_nm=C_nm: e.matmul(
                psM[:, 0:CH], B_nm, C_nm, start=True, stop=True),
                reads=[("xbcs",)], writes=[KM])
            p.op("vector", lambda e: e.tensor_copy(Gsb[:], psM[:, 0:CH]),
                 reads=[KM], writes=[("G",)])
        p.op("tensor", lambda e, B_nm=B_nm: e.transpose(psMb[:, 0:CH], B_nm, ident_sb[:]),
             reads=[("xbcs",), KC], writes=[KMB])
        p.op("vector", lambda e: e.tensor_copy(Btm[:], psMb[:, 0:CH]),
             reads=[KMB], writes=[("Btm",)])

        for hh in range(NHEADS // HHALF):
            h0 = hh * HHALF
            if full:
                Dm = Dm2[hh % 2]
                kdm = ("Dm", hh % 2)
                p.op("vector", lambda e, h0=h0, Dm=Dm: e.tensor_tensor(
                    Dm[:], rep_mid(masktri_sb[:], HHALF),
                    rep_last(nacsT[:, h0:h0 + HHALF], CH), op=ALU.add),
                    reads=[KC, ("nacsT",)], writes=[kdm])
                p.dma("gpsimd", f"c1_abc{hh % 2}", Dm[:],
                      acs_d.rearrange("h l -> (h l)").unsqueeze(0)[:, h0 * CH:(h0 + HHALF) * CH]
                      .to_broadcast((128, HHALF * CH)),
                      reads=[("d_acs",), kdm], writes=[kdm],
                      accum_op=ALU.add)
                p.dma("sync", "c1_eabc", eabc[:],
                      ea_d[c].rearrange("h l -> (h l)").unsqueeze(0)[:, h0 * CH:(h0 + HHALF) * CH]
                      .to_broadcast((128, HHALF * CH)),
                      reads=[("d_ea",)], writes=[("eabc",)])
                p.op("scalar", lambda e, Dm=Dm: e.activation(Lh[:], Dm[:], AF.Exp),
                     reads=[kdm], writes=[("Lh",)])
                p.op("vector", lambda e: e.tensor_tensor(
                    Mth[:], Lh[:], rep_mid(Gsb[:], HHALF), op=ALU.mult),
                    reads=[("Lh",), ("G",)], writes=[("Mth",)])
                p.op("gpsimd", lambda e, C_nm=C_nm: e.tensor_tensor(
                    Cexp[:], eabc[:], rep_mid(C_nm, HHALF), op=ALU.mult),
                    reads=[("eabc",), ("xbcs",)], writes=[("Cexp",)])

                for jj in range(HHALF // 2):
                    j = hh * (HHALF // 2) + jj
                    bank, kb = psY[jj % 2], ("psY", jj % 2)
                    for u in range(2):
                        h = 2 * j + u
                        lh = h - h0
                        rows = slice(64 * u, 64 * u + 64)
                        p.op("tensor", lambda e, bank=bank, rows=rows, h=h, lh=lh:
                             e.matmul(bank[rows, 0:CH], Xp[:, h * 64:h * 64 + 64],
                                      Mth[:, lh * CH:(lh + 1) * CH], start=True, stop=False),
                             reads=[("Xp",), ("Mth",)], writes=[kb])
                        p.op("tensor", lambda e, bank=bank, rows=rows, h=h, lh=lh:
                             e.matmul(bank[rows, 0:CH], Sb_sb[:, h * 64:h * 64 + 64],
                                      Cexp[:, lh * CH:(lh + 1) * CH], start=False, stop=True),
                             reads=[("Sb",), ("Cexp",)], writes=[kb])
                    p.op("vector", lambda e, j=j, bank=bank: e.scalar_tensor_tensor(
                        ysb[:, j * CH:(j + 1) * CH], xbcs[:, j * CH:(j + 1) * CH],
                        D_cols_sb[:, j:j + 1], bank[:, 0:CH], op0=ALU.mult, op1=ALU.add),
                        reads=[("xbcs",), kb, KC], writes=[("ysb",)])

            for g in range(HHALF // 8):
                gg = hh * (HHALF // 8) + g
                bank, kb = psS[gg % 2], ("psS", gg % 2)
                hbase = h0 + g * 8
                for i in range(8):
                    h = hbase + i
                    p.op("tensor", lambda e, bank=bank, i=i, h=h: e.matmul(
                        bank[:, i * 64:(i + 1) * 64], Btm[:],
                        Xdec[:, h * 64:(h + 1) * 64], start=True, stop=True),
                        reads=[("Btm",), ("Xdec",)], writes=[kb])
                base = hbase * 64
                p.op("vector", lambda e, base=base, hbase=hbase: e.tensor_tensor(
                    S_sb[:, base:base + 512], S_sb[:, base:base + 512],
                    rep_last(EAb[:, hbase:hbase + 8], 64), op=ALU.mult),
                    reads=[("S",), ("EAb",)], writes=[("S",)])
                p.op("vector", lambda e, base=base, bank=bank: e.tensor_tensor(
                    S_sb[:, base:base + 512], S_sb[:, base:base + 512],
                    bank[:, 0:512], op=ALU.add),
                    reads=[("S",), kb], writes=[("S",)])
        if full:
            p.op("vector", lambda e: e.tensor_copy(Sb_sb[:], S_sb[:]),
                 reads=[("S",)], writes=[("Sb",)])
            p.dma("scalar", "c1_y", y_d[c], ysb[:], reads=[("ysb",)], writes=[("d_y", c)])

    for kt in range(16):
        p.dma_tr("sync", "a2_lnf", ln1f[:, kt * TC:kt * TC + T],
                 ln1p_d[0:T, kt * 128:(kt + 1) * 128],
                 reads=[("d_ln1p",)], writes=[("ln1f", kt)])
    for nt in range(NT_Z, NT_IN):
        emit_a2_tile(nt, False, xbcp_d, ("d_xbcp",), dtp_sb, ("dtp_sb",), None)

    for kt in range(16):
        p.dma_tr("sync", "a2_lnf", ln1f[:, kt * TC:kt * TC + TC],
                 ln1_d[0:TC, kt * 128:(kt + 1) * 128],
                 reads=[("d_ln1",)], writes=[("ln1f", kt)])

    p.op("vector", lambda e: e.memset(S_sb[:], 0.0), writes=[("S",)])
    p.op("vector", lambda e: e.memset(
        xc[0][:].rearrange("p (nt w) -> p nt w", w=CHW)[:, :, 0:3], 0.0),
        writes=[("xc", 0)])
    ti = 0
    for c in range(NCH):
        quota = (NT_IN * (c + 1) + NCH - 1) // NCH
        while ti < min(quota, NT_IN):
            emit_a2_tile(ti, True, xbc_d, ("d_xbc",), dt_sb, ("dt_sb",), z_d)
            ti += 1
        emit_chunk(c, False, xbcp_d, ("d_xbcp",), dtp_sb, ("dtp_sb",))
    while ti < NT_IN:
        emit_a2_tile(ti, True, xbc_d, ("d_xbc",), dt_sb, ("dt_sb",), z_d)
        ti += 1
    p.op("vector", lambda e: e.tensor_copy(S0b_sb[:], S_sb[:]),
         reads=[("S",)], writes=[("S0b",)])

    p.barrier()
    a2s.close()

    c1x = ExitStack()
    Xp = sbuf(c1x, "Xp", (128, D_INNER), BF16)
    Dm2 = [sbuf(c1x, f"Dm{i}", (128, HHALF * CH), F32) for i in range(2)]
    Lh = sbuf(c1x, "Lh", (128, HHALF * CH), BF16)
    Mth = sbuf(c1x, "Mth", (128, HHALF * CH), BF16)
    eabc = sbuf(c1x, "eabc", (128, HHALF * CH), BF16)
    Cexp = sbuf(c1x, "Cexp", (128, HHALF * CH), BF16)
    ysb = sbuf(c1x, "ysb", (128, NT_Z * CH), BF16)
    Sb_sb = sbuf(c1x, "Sb_sb", (128, NHEADS * HEADDIM), BF16)
    psY = [c1x.enter_context(nc.psum_tensor(f"psY{i}", [128, 512], F32)) for i in range(2)]

    p.op("vector", lambda e: e.memset(S_sb[:], 0.0), reads=[("S0b",)], writes=[("S",)])
    p.op("vector", lambda e: e.memset(Sb_sb[:], 0.0), writes=[("Sb",)])
    p.op("vector", lambda e: e.tensor_copy(
        xc[0][:].rearrange("p (nt w) -> p nt w", w=CHW)[:, :, 0:3],
        ctx3_sb[:].rearrange("p (nt w) -> p nt w", w=3)),
        reads=[("ctx3",)], writes=[("xc", 0)])
    for c in range(NCH):
        emit_chunk(c, True, xbc_d, ("d_xbc",), dt_sb, ("dt_sb",))

    p.op("vector", lambda e: e.tensor_tensor_scan(
        cumprev_sb[:, 1:NCH], Alast_sb[:, 0:NCH - 1], Alast_sb[:, 0:NCH - 1], 0.0,
        op0=ALU.add, op1=ALU.bypass),
        reads=[("Alast",)], writes=[("cumprev",)])
    p.op("vector", lambda e: e.memset(cumprev_sb[:, 0:1], 0.0), writes=[("cumprev",)])

    p.barrier()
    c1x.close()
    c1s.close()
    ms1.close()

    ph = ExitStack()
    yn_sb = sbuf(ph, "yn_sb", (128, NT_Z * HALF_T), BF16)
    ez = sbuf(ph, "ez", (NHEADS, CH), BF16)
    edc = sbuf(ph, "edc", (NHEADS, 1), F32)
    scc = sbuf(ph, "scc", (NHEADS, CH), BF16)
    scbc = sbuf(ph, "scbc", (128, HHALF * CH), BF16)
    Cph = sbuf(ph, "Cph", (128, HHALF * CH), BF16)
    yc2 = sbuf(ph, "yc2", (128, NT_Z * CH), F32)
    zc2 = sbuf(ph, "zc2", (128, NT_Z * CH), BF16)
    slz = sbuf(ph, "slz", (128, NT_Z * CH), F32)
    sqz = sbuf(ph, "sqz", (128, NT_Z * CH), F32)
    rrow = sbuf(ph, "rrow", (1, CH), F32)
    psC = [ph.enter_context(nc.psum_tensor(f"psC{i}", [128, 512], F32)) for i in range(2)]
    psR = ph.enter_context(nc.psum_tensor("psR", [128, 1024], F32))
    wD = [sbuf(ph, f"wD{i}", (128, 32 * 128), BF16) for i in range(2)]
    x0D = sbuf(ph, "x0D", (128, HALF_T), F32)
    lnD = sbuf(ph, "lnD", (128, HALF_T), BF16)
    x1D = sbuf(ph, "x1D", (128, HALF_T), F32)
    psD = [ph.enter_context(nc.psum_tensor(f"psD{i}", [128, 512], F32)) for i in range(4)]
    DSL = slices(HALF_T, 512)

    for half in range(2):
        for c in range(half * (NCH // 2), (half + 1) * (NCH // 2)):
            zrel = c - half * (NCH // 2)
            p.dma("sync", "c2_ez", ez[:], ea_d[c], reads=[("d_ea",)], writes=[("ez",)])
            p.op("scalar", lambda e, c=c: e.activation(edc[:], cumprev_sb[:, c:c + 1], AF.Exp),
                 reads=[("cumprev",)], writes=[("edc",)])
            p.op("vector", lambda e: e.tensor_tensor(edc[:], edc[:], mask2_sb[0:NHEADS, :],
                                                     op=ALU.mult),
                 reads=[("edc",), KC], writes=[("edc",)])
            p.op("vector", lambda e: e.tensor_scalar(scc[:], ez[:], edc[:], None, op0=ALU.mult),
                 reads=[("ez",), ("edc",)], writes=[("scc",)])
            p.dma("scalar", "c2_sc", sc_d[:, :], scc[:], reads=[("scc",)], writes=[("d_sc",)])
            p.dma("gpsimd", "c2_y", yc2[:], y_d[c], reads=[("d_y", c)], writes=[("yc2",)])
            for hh in range(NHEADS // HHALF):
                h0 = hh * HHALF
                p.dma("sync", "c2_scbc", scbc[:],
                      sc_d.rearrange("h l -> (h l)").unsqueeze(0)[:, h0 * CH:(h0 + HHALF) * CH]
                      .to_broadcast((128, HHALF * CH)),
                      reads=[("d_sc",)], writes=[("scbc",)])
                p.op("vector", lambda e, c=c: e.tensor_tensor(
                    Cph[:], scbc[:], rep_mid(Cfull_sb[:, c * CH:(c + 1) * CH], HHALF),
                    op=ALU.mult), reads=[("scbc",), ("Cfull",)], writes=[("Cph",)])
                for jj in range(HHALF // 2):
                    j = hh * (HHALF // 2) + jj
                    bank, kb = psC[jj % 2], ("psC", jj % 2)
                    for u in range(2):
                        h = 2 * j + u
                        lh = h - h0
                        rows = slice(64 * u, 64 * u + 64)
                        p.op("tensor", lambda e, bank=bank, rows=rows, h=h, lh=lh:
                             e.matmul(bank[rows, 0:CH], S0b_sb[:, h * 64:h * 64 + 64],
                                      Cph[:, lh * CH:(lh + 1) * CH], start=True, stop=True),
                             reads=[("S0b",), ("Cph",)], writes=[kb])
                    p.op("vector", lambda e, j=j, bank=bank: e.tensor_tensor(
                        yc2[:, j * CH:(j + 1) * CH], yc2[:, j * CH:(j + 1) * CH],
                        bank[:, 0:CH], op=ALU.add),
                        reads=[("yc2",), kb], writes=[("yc2",)])
            p.dma("sync", "c2_z", zc2[:].rearrange("p (nt l) -> p nt l", l=CH),
                  z_d.rearrange("(nt p) t -> p nt t", p=128)[:, :, c * CH:(c + 1) * CH],
                  reads=[("d_z",)], writes=[("zc2",)])
            p.op("scalar", lambda e: e.activation(slz[:], zc2[:], AF.Silu),
                 reads=[("zc2",)], writes=[("slz",)])
            p.op("vector", lambda e: e.tensor_tensor(yc2[:], yc2[:], slz[:], op=ALU.mult),
                 reads=[("yc2",), ("slz",)], writes=[("yc2",)])
            p.op("scalar", lambda e: e.activation(sqz[:], yc2[:], AF.Square),
                 reads=[("yc2",)], writes=[("sqz",)])
            for nt in range(NT_Z):
                p.op("tensor", lambda e, nt=nt: e.matmul(
                    psR[0:1, 0:CH], ones_col_sb[:], sqz[:, nt * CH:(nt + 1) * CH],
                    start=(nt == 0), stop=(nt == NT_Z - 1)),
                    reads=[("sqz",), KC], writes=[("psRs",)])
            p.op("scalar", lambda e: e.activation(rrow[:], psR[0:1, 0:CH], AF.Sqrt,
                                                  bias=misc_sb[0:1, 0:1],
                                                  scale=misc_sb[0:1, 2:3]),
                 reads=[("psRs",), KC], writes=[("rrow",)])
            p.op("vector", lambda e: e.reciprocal(rrow[:], rrow[:]),
                 reads=[("rrow",)], writes=[("rrow",)])
            p.op("tensor", lambda e: e.matmul(psR[:, 512:512 + CH], ones_row_sb[:],
                                              rrow[:], start=True, stop=True),
                 reads=[("rrow",), KC], writes=[("psRb",)])
            for nt in range(NT_Z):
                p.op("vector", lambda e, nt=nt, zrel=zrel: e.scalar_tensor_tensor(
                    yn_sb[:, nt * HALF_T + zrel * CH:nt * HALF_T + (zrel + 1) * CH],
                    yc2[:, nt * CH:(nt + 1) * CH], normw_sb[:, nt:nt + 1],
                    psR[:, 512:512 + CH], op0=ALU.mult, op1=ALU.mult),
                    reads=[("yc2",), ("psRb",), KC], writes=[("yn",)])

        for dt in range(16):
            par = dt % 2
            wb = wD[par]
            kw = ("wD", par)
            p.dma("gpsimd", f"d_w{par}", wb[:], w_out_r[dt].rearrange("f k d -> f (k d)"),
                  writes=[kw])
            pset = psD[par * 2: par * 2 + 2]
            pk = [("psD", par * 2 + i) for i in range(2)]
            for kt in range(32):
                for s, (n0, n1) in enumerate(DSL):
                    p.op("tensor", lambda e, wb=wb, kt=kt, n0=n0, n1=n1, ps=pset[s]:
                         e.matmul(ps[:, 0:n1 - n0], wb[:, kt * 128:(kt + 1) * 128],
                                  yn_sb[:, kt * HALF_T + n0:kt * HALF_T + n1],
                                  start=(kt == 0), stop=(kt == 31)),
                         reads=[kw, ("yn",)], writes=[pk[s]])
            p.dma("sync", "d_x0", x0D[:],
                  x_fm[dt * 128:(dt + 1) * 128, half * HALF_T:(half + 1) * HALF_T],
                  writes=[("x0D",)])
            p.dma_tr("sync", "d_ln", lnD[:],
                     ln1_d[half * HALF_T:(half + 1) * HALF_T, dt * 128:(dt + 1) * 128],
                     reads=[("d_ln1",)], writes=[("lnD",)])
            p.op("vector", lambda e: e.tensor_tensor(x1D[:], x0D[:], lnD[:], op=ALU.add),
                 reads=[("x0D",), ("lnD",)], writes=[("x1D",)])
            for s, (n0, n1) in enumerate(DSL):
                p.op("vector", lambda e, n0=n0, n1=n1, ps=pset[s]: e.tensor_tensor(
                    x1D[:, n0:n1], x1D[:, n0:n1], ps[:, 0:n1 - n0], op=ALU.add),
                    reads=[("x1D",), pk[s]], writes=[("x1D",)])
            p.dma("scalar", "d_x1",
                  x1_d[dt][:, half * HALF_T:(half + 1) * HALF_T], x1D[:],
                  reads=[("x1D",)], writes=[("d_x1",)])

    p.barrier()
    ph.close()
    ms2.close()

    ph = ExitStack()
    x2acc = sbuf(ph, "x2acc", (128, 16 * LG), F32)
    ln2 = sbuf(ph, "ln2", (128, 16 * LG), BF16)
    hsb = sbuf(ph, "hsb", (128, 16 * LG), BF16)
    sqE = [sbuf(ph, f"sqE{i}", (128, LG), F32) for i in range(2)]
    gkE = sbuf(ph, "gkE", (128, 16 * 8), BF16)
    rowsE = sbuf(ph, "rowsE", (1, 3 * LG), F32)
    abE = sbuf(ph, "abE", (128, 2 * LG), F32)
    g8 = sbuf(ph, "g8", (128, 8), F32)
    m8 = sbuf(ph, "m8", (128, 8), F32)
    maskE = sbuf(ph, "maskE", (128, 8), F32)
    eallE = sbuf(ph, "eallE", (128, 8), F32)
    em2E = sbuf(ph, "em2E", (128, 1), F32)
    denE = sbuf(ph, "denE", (128, 1), F32)
    negm1E = sbuf(ph, "negm1E", (128, 1), F32)
    cwE = sbuf(ph, "cwE", (128, 8), F32)
    cwTE = sbuf(ph, "cwTE", (8, LG), F32)
    cwrow = sbuf(ph, "cwrow", (1, LG), F32)
    cwb = sbuf(ph, "cwb", (128, LG), F32)
    usb = sbuf(ph, "usb", (128, LG), F32)
    usb2 = sbuf(ph, "usb2", (128, LG), F32)
    ssb = sbuf(ph, "ssb", (128, LG), BF16)
    ssb2 = sbuf(ph, "ssb2", (128, LG), BF16)
    wiA = [sbuf(ph, f"wiA{i}", (128, 16 * 128), BF16) for i in range(2)]
    wiB = [sbuf(ph, f"wiB{i}", (128, 16 * 128), BF16) for i in range(2)]
    woE = [sbuf(ph, f"woE{i}", (128, 16 * 128), BF16) for i in range(2)]
    psH0 = ph.enter_context(nc.psum_tensor("psH0", [128, 1024], F32))
    psH1 = ph.enter_context(nc.psum_tensor("psH1", [128, 1024], F32))
    psW = ph.enter_context(nc.psum_tensor("psW", [128, 1024], F32))
    psE = ph.enter_context(nc.psum_tensor("psE", [128, 1024], F32))
    KH0, KH1, KW_, KE = ("psH0",), ("psH1",), ("psW",), ("psE",)
    LSL = slices(LG, 512)

    p.dma("gpsimd", "e_gk", gkE[:], gk_r.rearrange("p k e -> p (k e)"), writes=[("gkE",)])

    for lg in range(NLG):
        t0 = lg * LG
        for dt in range(16):
            p.dma("sync", "e_x1", x2acc[:, dt * LG:(dt + 1) * LG],
                  x1_d[dt][:, t0:t0 + LG], reads=[("d_x1",)], writes=[("x2a", dt)])
        for dt in range(16):
            par = dt % 2
            p.op("scalar", lambda e, dt=dt, par=par: e.activation(
                sqE[par][:], x2acc[:, dt * LG:(dt + 1) * LG], AF.Square),
                reads=[("x2a", dt)], writes=[("sqE", par)])
            for (n0, n1) in LSL:
                p.op("tensor", lambda e, dt=dt, n0=n0, n1=n1: e.matmul(
                    psE[0:1, n0:n1], ones_col_sb[:], x2acc[:, dt * LG + n0:dt * LG + n1],
                    start=(dt == 0), stop=(dt == 15)),
                    reads=[("x2a", dt), KC], writes=[KE])
                p.op("tensor", lambda e, par=par, n0=n0, n1=n1, dt=dt: e.matmul(
                    psW[0:1, n0:n1], ones_col_sb[:], sqE[par][:, n0:n1],
                    start=(dt == 0), stop=(dt == 15)),
                    reads=[("sqE", par), KC], writes=[KW_])
        p.op("vector", lambda e: e.tensor_scalar(
            rowsE[0:1, 0:LG], psE[0:1, 0:LG], 1.0 / D_MODEL, None, op0=ALU.mult),
            reads=[KE], writes=[("rowsE",)])
        p.op("vector", lambda e: e.tensor_scalar(
            rowsE[0:1, LG:2 * LG], psW[0:1, 0:LG], 1.0 / D_MODEL, None, op0=ALU.mult),
            reads=[KW_, ("rowsE",)], writes=[("rowsE",)])
        p.op("vector", lambda e: e.tensor_tensor(
            rowsE[0:1, 2 * LG:3 * LG], rowsE[0:1, 0:LG], rowsE[0:1, 0:LG], op=ALU.mult),
            reads=[("rowsE",)], writes=[("rowsE",)])
        p.op("vector", lambda e: e.tensor_tensor(
            rowsE[0:1, LG:2 * LG], rowsE[0:1, LG:2 * LG], rowsE[0:1, 2 * LG:3 * LG],
            op=ALU.subtract), reads=[("rowsE",)], writes=[("rowsE",)])
        p.op("scalar", lambda e: e.activation(rowsE[0:1, LG:2 * LG],
                                              rowsE[0:1, LG:2 * LG], AF.Sqrt,
                                              bias=misc_sb[0:1, 0:1]),
             reads=[("rowsE",), KC], writes=[("rowsE",)])
        p.op("vector", lambda e: e.reciprocal(rowsE[0:1, LG:2 * LG], rowsE[0:1, LG:2 * LG]),
             reads=[("rowsE",)], writes=[("rowsE",)])
        p.op("vector", lambda e: e.tensor_tensor(
            rowsE[0:1, 0:LG], rowsE[0:1, 0:LG], rowsE[0:1, LG:2 * LG], op=ALU.mult),
            reads=[("rowsE",)], writes=[("rowsE",)])
        for (n0, n1) in LSL:
            p.op("tensor", lambda e, n0=n0, n1=n1: e.matmul(
                psE[:, n0:n1], ones_row_sb[:], rowsE[0:1, LG + n0:LG + n1],
                start=True, stop=True), reads=[("rowsE",), KC], writes=[KE])
            p.op("vector", lambda e, n0=n0, n1=n1: e.tensor_copy(
                abE[:, n0:n1], psE[:, n0:n1]), reads=[KE], writes=[("abE",)])
            p.op("tensor", lambda e, n0=n0, n1=n1: e.matmul(
                psW[:, n0:n1], ones_row_sb[:], rowsE[0:1, n0:n1],
                start=True, stop=True), reads=[("rowsE",), KC], writes=[KW_])
            p.op("vector", lambda e, n0=n0, n1=n1: e.tensor_copy(
                abE[:, LG + n0:LG + n1], psW[:, n0:n1]), reads=[KW_], writes=[("abE",)])
        for dt in range(16):
            par = dt % 2
            p.op("vector", lambda e, dt=dt, par=par: e.tensor_tensor(
                sqE[par][:], x2acc[:, dt * LG:(dt + 1) * LG], abE[:, 0:LG], op=ALU.mult),
                reads=[("x2a", dt), ("abE",)], writes=[("sqE", par)])
            p.op("vector", lambda e, dt=dt, par=par: e.tensor_tensor(
                ln2[:, dt * LG:(dt + 1) * LG], sqE[par][:], abE[:, LG:2 * LG],
                op=ALU.subtract),
                reads=[("sqE", par), ("abE",)], writes=[("ln2", dt)])
        KLN2 = [("ln2", dt) for dt in range(16)]
        for st in range(LG // 128):
            for kt in range(16):
                p.op("tensor", lambda e, kt=kt, st=st: e.matmul(
                    psE[:, 0:8], ln2[:, kt * LG + st * 128:kt * LG + (st + 1) * 128],
                    gkE[:, kt * 8:(kt + 1) * 8], start=(kt == 0), stop=(kt == 15)),
                    reads=[KLN2[kt], ("gkE",)], writes=[KE])
            p.op("vector", lambda e: e.tensor_copy(g8[:], psE[:, 0:8]),
                 reads=[KE], writes=[("g8",)])
            p.op("vector", lambda e: e.max(out=m8[:], in_=g8[:]),
                 reads=[("g8",)], writes=[("m8",)])
            p.op("vector", lambda e: e.tensor_scalar(maskE[:], g8[:], m8[:, 1:2], None,
                                                     op0=ALU.is_ge),
                 reads=[("g8",), ("m8",)], writes=[("maskE",)])
            p.op("vector", lambda e: e.tensor_scalar(negm1E[:], m8[:, 0:1], -1.0, None,
                                                     op0=ALU.mult),
                 reads=[("m8",)], writes=[("negm1",)])
            p.op("scalar", lambda e: e.activation(eallE[:], g8[:], AF.Exp, bias=negm1E[:]),
                 reads=[("g8",), ("negm1",)], writes=[("eall",)])
            p.op("scalar", lambda e: e.activation(em2E[:], m8[:, 1:2], AF.Exp, bias=negm1E[:]),
                 reads=[("m8",), ("negm1",)], writes=[("em2",)])
            p.op("vector", lambda e: e.tensor_scalar(denE[:], em2E[:], 1.0, None, op0=ALU.add),
                 reads=[("em2",)], writes=[("den",)])
            p.op("vector", lambda e: e.reciprocal(denE[:], denE[:]),
                 reads=[("den",)], writes=[("den",)])
            p.op("vector", lambda e: e.scalar_tensor_tensor(
                cwE[:], eallE[:], denE[:], maskE[:], op0=ALU.mult, op1=ALU.mult),
                reads=[("eall",), ("den",), ("maskE",)], writes=[("cwE",)])
            p.op("tensor", lambda e: e.transpose(psE[0:8, 512:512 + 128], cwE[:],
                                                 identf_sb[:]),
                 reads=[("cwE",), KC], writes=[KE])
            p.op("vector", lambda e, st=st: e.tensor_copy(
                cwTE[:, st * 128:(st + 1) * 128], psE[0:8, 512:512 + 128]),
                reads=[KE], writes=[("cwT",)])
        p.dma("scalar", "e_cw", cw_d[:, t0:t0 + LG], cwTE[:],
              reads=[("cwT",)], writes=[("d_cw",)])

        for ex in range(8):
            p.dma("sync", "e_cwrow", cwrow[:], cw_d[ex:ex + 1, t0:t0 + LG],
                  reads=[("d_cw",)], writes=[("cwrow",)])
            for (n0, n1) in LSL:
                p.op("tensor", lambda e, n0=n0, n1=n1: e.matmul(
                    psE[:, n0:n1], ones_row_sb[:], cwrow[0:1, n0:n1],
                    start=True, stop=True), reads=[("cwrow",), KC], writes=[KE])
                p.op("vector", lambda e, n0=n0, n1=n1: e.tensor_copy(
                    cwb[:, n0:n1], psE[:, n0:n1]), reads=[KE], writes=[("cwb",)])
            for mt in range(16):
                par = mt % 2
                ph0, kh0 = (psH0, KH0) if par == 0 else (psW, KW_)
                ph1, kh1 = (psH1, KH1) if par == 0 else (psE, KE)
                p.dma("gpsimd", f"e_wi0_{par}", wiA[par][:],
                      wi0_r[ex, mt].rearrange("d k m -> d (k m)"), writes=[("wiA", par)])
                p.dma("gpsimd", f"e_wi1_{par}", wiB[par][:],
                      wi1_r[ex, mt].rearrange("d k m -> d (k m)"), writes=[("wiB", par)])
                for kt in range(16):
                    for (n0, n1) in LSL:
                        p.op("tensor", lambda e, par=par, kt=kt, n0=n0, n1=n1, ph0=ph0:
                             e.matmul(ph0[:, n0:n1], wiA[par][:, kt * 128:(kt + 1) * 128],
                                      ln2[:, kt * LG + n0:kt * LG + n1],
                                      start=(kt == 0), stop=(kt == 15)),
                             reads=[("wiA", par), KLN2[kt]], writes=[kh0])
                        p.op("tensor", lambda e, par=par, kt=kt, n0=n0, n1=n1, ph1=ph1:
                             e.matmul(ph1[:, n0:n1], wiB[par][:, kt * 128:(kt + 1) * 128],
                                      ln2[:, kt * LG + n0:kt * LG + n1],
                                      start=(kt == 0), stop=(kt == 15)),
                             reads=[("wiB", par), KLN2[kt]], writes=[kh1])
                usbp, kusb = (usb, ("usb", 0)) if par == 0 else (usb2, ("usb", 1))
                ssbp, kssb = (ssb, ("ssb", 0)) if par == 0 else (ssb2, ("ssb", 1))
                p.op("vector", lambda e, ph1=ph1, usbp=usbp: e.tensor_tensor(
                    usbp[:], ph1[:, 0:LG], cwb[:], op=ALU.mult),
                    reads=[kh1, ("cwb",)], writes=[kusb])
                p.op("scalar", lambda e, ph0=ph0, ssbp=ssbp: e.activation(
                    ssbp[:], ph0[:, 0:LG], AF.Silu),
                    reads=[kh0], writes=[kssb])
                p.op("vector", lambda e, mt=mt, usbp=usbp, ssbp=ssbp: e.tensor_tensor(
                    hsb[:, mt * LG:(mt + 1) * LG], usbp[:], ssbp[:], op=ALU.mult),
                    reads=[kusb, kssb], writes=[("hsb", mt)])
            for dt in range(16):
                par = dt % 2
                pw, kw2 = (psH0, KH0) if par == 0 else (psW, KW_)
                p.dma("gpsimd", f"e_wo_{par}", woE[par][:],
                      wo_r[ex, dt].rearrange("m t d -> m (t d)"), writes=[("woE", par)])
                for mt in range(16):
                    for (n0, n1) in LSL:
                        p.op("tensor", lambda e, par=par, mt=mt, n0=n0, n1=n1, pw=pw:
                             e.matmul(pw[:, n0:n1], woE[par][:, mt * 128:(mt + 1) * 128],
                                      hsb[:, mt * LG + n0:mt * LG + n1],
                                      start=(mt == 0), stop=(mt == 15)),
                             reads=[("woE", par), ("hsb", mt)], writes=[kw2])
                for (n0, n1) in LSL:
                    p.op("vector", lambda e, dt=dt, n0=n0, n1=n1, pw=pw: e.tensor_tensor(
                        x2acc[:, dt * LG + n0:dt * LG + n1],
                        x2acc[:, dt * LG + n0:dt * LG + n1], pw[:, n0:n1], op=ALU.add),
                        reads=[("x2a", dt), kw2], writes=[("x2a", dt)])
        for dt in range(16):
            p.dma("scalar", "e_out", out_d[dt][:, t0:t0 + LG],
                  x2acc[:, dt * LG:(dt + 1) * LG],
                  reads=[("x2a", dt)], writes=[("d_out",)])

    p.fence("gpsimd", [("d_out",)])
    p.build()
    return nc


def kernel(**inputs):
    T = 2048
    in_maps = host_layouts(inputs, T)
    nc = build_program(T)
    res = run_bass_kernel_spmd(nc, in_maps, core_ids=list(range(8)))
    out = np.zeros((4, 4096, 2048), np.float32)
    for core in range(8):
        b, half = core // 2, core % 2
        o = np.asarray(res.results[core]["out_fm"])
        out[b, half * T:(half + 1) * T, :] = o.reshape(2048, T).T
    return out


# version 15
# speedup vs baseline: 1.1958x; 1.0099x over previous
import numpy as np
import ml_dtypes
from contextlib import ExitStack

import concourse.bass as bass
import concourse.mybir as mybir
from concourse.bass_utils import run_bass_kernel_spmd

F32 = mybir.dt.float32
BF16 = mybir.dt.bfloat16
AF = mybir.ActivationFunctionType
ALU = mybir.AluOpType
bfloat16 = ml_dtypes.bfloat16

D_MODEL = 2048
HEADDIM = 64
CH = 128
D_INNER = 4096
NHEADS = 64
NPAIR = 32
CONV_DIM = 4352
D_IN_PROJ = 8512
NT_Z = 32
NT_XBC = 34
NT_IN = 67
EPS = 1e-5
NEG_BIG = -1e9
HHALF = 16

ENGS = ("tensor", "vector", "scalar", "gpsimd", "sync")


def rep_mid(ap2d, reps):
    return ap2d.unsqueeze(1).to_broadcast((ap2d.shape[0], reps, ap2d.shape[1]))


def rep_last(ap2d, reps):
    return ap2d.to_broadcast((ap2d.shape[0], ap2d.shape[1], reps))


def slices(total, step):
    out = []
    o = 0
    while o < total:
        out.append((o, min(total, o + step)))
        o += step
    return out


class Prog:
    def __init__(self, nc):
        self.nc = nc
        self.es = ExitStack()
        self.sems = {}
        self.count = {}
        self.watermark = {}
        self.last_write = {}
        self.readers = {}
        self.ops = []
        for e in ENGS:
            self._new_sem(e)

    def _new_sem(self, name):
        self.sems[name] = self.es.enter_context(self.nc.semaphore("sem_" + name))
        self.count[name] = 0

    def dma_group(self, name):
        if name not in self.sems:
            self._new_sem(name)
        return name

    def _need_waits(self, eng, reads, writes):
        need = {}

        def add(sem, val):
            if val > need.get(sem, 0):
                need[sem] = val

        for r in reads:
            lw = self.last_write.get(r)
            if lw:
                add(*lw)
        for w in writes:
            lw = self.last_write.get(w)
            if lw:
                add(*lw)
            for sem, val in self.readers.get(w, {}).items():
                add(sem, val)
        out = []
        for sem, val in need.items():
            if sem == eng == "tensor":
                continue
            if sem not in ENGS:
                val = self.count[sem]
            wm = self.watermark.get((eng, sem), 0)
            if val > wm:
                out.append((sem, val))
                self.watermark[(eng, sem)] = val
        return out

    def op(self, eng, emit_fn, reads=(), writes=(), tag=None):
        waits = self._need_waits(eng, reads, writes)
        self.count[eng] += 1
        v = self.count[eng]
        self.ops.append((eng, waits, emit_fn, eng, 1, tag))
        for w in writes:
            self.last_write[w] = (eng, v)
            self.readers[w] = {}
        for r in reads:
            self.readers.setdefault(r, {})[eng] = v

    def _dma_common(self, q, group, emit, reads, writes, tag):
        g = self.dma_group(group)
        waits = self._need_waits(q, reads, writes)
        self.count[g] += 16
        v = self.count[g]
        self.ops.append((q, waits, emit, g, 16, tag))
        for w in writes:
            self.last_write[w] = (g, v)
            self.readers[w] = {}
        for r in reads:
            self.readers.setdefault(r, {})[g] = v

    def dma(self, q, group, out, in_, reads=(), writes=(), tag=None, **kw):
        def emit(e, out=out, in_=in_, kw=kw):
            return e.dma_start(out=out, in_=in_, **kw)
        self._dma_common(q, group, emit, reads, writes, tag)

    def dma_tr(self, q, group, out, in_, reads=(), writes=(), tag=None):
        def emit(e, out=out, in_=in_):
            return e.dma_start_transpose(out=out, in_=in_)
        self._dma_common(q, group, emit, reads, writes, tag)

    def collective(self, emit_fn, reads=(), writes=(), tag="cc"):
        self._dma_common("gpsimd", "cc", emit_fn, reads, writes, tag)

    def nop_marker(self, eng, reads, writes):
        self.op(eng, lambda e: e.engine_nop(), reads=reads, writes=writes)

    def barrier(self):
        snap = dict(self.count)
        for e in ENGS:
            waits = []
            for sem, val in snap.items():
                if sem == e or val <= 0:
                    continue
                wm = self.watermark.get((e, sem), 0)
                if val > wm:
                    waits.append((sem, val))
                    self.watermark[(e, sem)] = val
            if waits:
                self.ops.append((e, waits, None, None, 0, "barrier"))

    def fence(self, eng, keys):
        waits = self._need_waits(eng, keys, ())
        if waits:
            self.ops.append((eng, waits, None, None, 0, "fence"))

    def build(self):
        per_eng = {}
        for rec in self.ops:
            per_eng.setdefault(rec[0], []).append(rec)
        block = self.es.enter_context(self.nc.Block())
        for eng_name in ENGS:
            recs = per_eng.get(eng_name, [])
            if not recs:
                continue
            section = getattr(block, eng_name)

            def body(engine, recs=recs):
                for (_e, waits, emit_fn, inc_sem, inc_val, _tag) in recs:
                    for sem, val in waits:
                        engine.wait_ge(self.sems[sem], val)
                    if emit_fn is not None:
                        inst = emit_fn(engine)
                        if inc_sem is not None:
                            inst.then_inc(self.sems[inc_sem], inc_val)

            section(body)



def host_layouts(inputs, T=2048):
    f32 = np.float32
    x = np.ascontiguousarray(np.asarray(inputs["L0_attn_x"], f32))
    w_in = np.asarray(inputs["L0_attn_w_in"], f32)
    conv_w = np.asarray(inputs["L0_attn_conv_w"], f32)
    conv_b = np.asarray(inputs["L0_attn_conv_b"], f32)
    dt_bias = np.asarray(inputs["L0_attn_dt_bias"], f32)
    A_log = np.asarray(inputs["L0_attn_A_log"], f32)
    Dv = np.asarray(inputs["L0_attn_D"], f32)
    norm_w = np.asarray(inputs["L0_attn_norm_w"], f32)
    w_out = np.asarray(inputs["L0_attn_w_out"], f32)
    gk = np.asarray(inputs["L0_ffn_gate_kernel"], f32)
    wi0 = np.asarray(inputs["L0_ffn_wi_0"], f32)
    wi1 = np.asarray(inputs["L0_ffn_wi_1"], f32)
    wo = np.asarray(inputs["L0_ffn_wo"], f32)

    w_in_p = np.zeros((2048, NT_IN * 128), f32)
    w_in_p[:, :D_IN_PROJ] = w_in
    w_in_r = np.ascontiguousarray(
        w_in_p.reshape(16, 128, NT_IN, 128).transpose(2, 1, 0, 3))
    w_out_r = np.ascontiguousarray(
        w_out.reshape(32, 128, 16, 128).transpose(2, 1, 0, 3))
    wi0_r = np.ascontiguousarray(
        wi0.reshape(8, 16, 128, 16, 128).transpose(0, 3, 2, 1, 4))
    wi1_r = np.ascontiguousarray(
        wi1.reshape(8, 16, 128, 16, 128).transpose(0, 3, 2, 1, 4))
    wo_r = np.ascontiguousarray(
        wo.reshape(8, 16, 128, 16, 128).transpose(0, 3, 2, 1, 4))
    gk_r = np.ascontiguousarray(gk.reshape(16, 128, 8).transpose(1, 0, 2))
    conv_w_r = np.ascontiguousarray(conv_w.reshape(NT_XBC, 128, 4).transpose(1, 0, 2))
    conv_b_r = np.ascontiguousarray(conv_b.reshape(NT_XBC, 128).transpose(1, 0))
    D_cols = np.ascontiguousarray(
        Dv.reshape(NPAIR, 2)[:, :, None].repeat(HEADDIM, 2)
        .transpose(1, 2, 0).reshape(128, NPAIR))
    normw_cols = np.ascontiguousarray(norm_w.reshape(NT_Z, 128).transpose(1, 0))
    dtb = np.ascontiguousarray(dt_bias.reshape(NHEADS, 1))
    alog = np.ascontiguousarray(A_log.reshape(NHEADS, 1))

    li = np.arange(CH)
    shared = dict(
        w_in_r=w_in_r, w_out_r=w_out_r, wi0_r=wi0_r, wi1_r=wi1_r, wo_r=wo_r,
        gk_r=gk_r, conv_w_r=conv_w_r, conv_b_r=conv_b_r, D_cols=D_cols,
        normw_cols=normw_cols, dt_bias_c=dtb, A_log_c=alog,
        ident_bf=np.eye(128, dtype=bfloat16),
        ident_f32=np.eye(128, dtype=f32),
        ones_row_f32=np.ones((1, 128), f32),
        ones_col_f32=np.ones((128, 1), f32),
        masktri=np.where(li[None, :] >= li[:, None], 0.0, NEG_BIG).astype(f32),
        misc_cols=np.tile(np.array([[EPS, -1.0, 1.0 / D_INNER]], f32), (128, 1)),
    )

    in_maps = []
    for core in range(8):
        b, half = core // 2, core % 2
        x_tm = np.ascontiguousarray(x[b, half * T:(half + 1) * T])
        x_fm = np.ascontiguousarray(x_tm.T)
        x_ctx = np.zeros((128, 2048), f32)
        x_ptm = np.zeros((T, 2048), f32)
        if half == 1:
            x_ctx[0:3] = x[b, T - 3:T]
            x_ptm[:] = x[b, 0:T]
        m = dict(shared)
        m.update(x_tm=x_tm, x_fm=x_fm, x_ctx=x_ctx, x_ptm=x_ptm,
                 mask_first=np.full((128, 1), 0.0 if half else 1.0, f32),
                 mask_second=np.full((128, 1), 1.0 if half else 0.0, f32))
        in_maps.append(m)
    return in_maps



def build_program(T=2048, debug_taps=()):
    nc = bass.Bass("TRN2", target_bir_lowering=False)
    NCH = T // CH
    NST = T // 128
    LG = min(1024, T)
    NLG = T // LG
    HALF_T = T // 2
    p = Prog(nc)
    es = p.es

    def din(name, shape, dtype=F32):
        return nc.dram_tensor(name, list(shape), dtype, kind="ExternalInput").ap()

    def dint(name, shape, dtype):
        kind = "ExternalOutput" if name in debug_taps else "Internal"
        return nc.dram_tensor(name, list(shape), dtype, kind=kind).ap()

    x_tm = din("x_tm", (T, 2048))
    x_fm = din("x_fm", (2048, T))
    x_ctx = din("x_ctx", (128, 2048))
    x_ptm = din("x_ptm", (T, 2048))
    w_in_r = din("w_in_r", (NT_IN, 128, 16, 128))
    w_out_r = din("w_out_r", (16, 128, 32, 128))
    wi0_r = din("wi0_r", (8, 16, 128, 16, 128))
    wi1_r = din("wi1_r", (8, 16, 128, 16, 128))
    wo_r = din("wo_r", (8, 16, 128, 16, 128))
    gk_r = din("gk_r", (128, 16, 8))
    conv_w_d = din("conv_w_r", (128, NT_XBC, 4))
    conv_b_d = din("conv_b_r", (128, NT_XBC))
    D_cols_d = din("D_cols", (128, NPAIR))
    normw_d = din("normw_cols", (128, NT_Z))
    dtb_d = din("dt_bias_c", (NHEADS, 1))
    alog_d = din("A_log_c", (NHEADS, 1))
    ident_bf_d = din("ident_bf", (128, 128), BF16)
    ident_f32_d = din("ident_f32", (128, 128))
    ones_row_f32_d = din("ones_row_f32", (1, 128))
    ones_col_f32_d = din("ones_col_f32", (128, 1))
    masktri_d = din("masktri", (CH, CH))
    mask1_d = din("mask_first", (128, 1))
    mask2_d = din("mask_second", (128, 1))
    misc_d = din("misc_cols", (128, 3))

    out_d = nc.dram_tensor("out_fm", [16, 128, T], F32, kind="ExternalOutput").ap()

    ln1_d = dint("ln1_d", (T + 128, 2048), BF16)
    z_d = dint("z_d", (D_INNER, T), BF16)
    xbc_d = dint("xbc_d", (CONV_DIM, T), BF16)
    ln1p_d = dint("ln1p_d", (T, 2048), BF16)
    xbcp_d = dint("xbcp_d", (CONV_DIM, T), BF16)
    xh_d = dint("xh_d", (D_INNER, CH), BF16)
    acs_d = dint("acs_d", (NHEADS, CH), F32)
    ea_d = dint("ea_d", (NCH, NHEADS, CH), BF16)
    sc_d = dint("sc_d", (NHEADS, CH), BF16)
    y_d = dint("y_d", (NCH, 128, NT_Z * CH), F32)
    x1_d = dint("x1_d", (16, 128, T), F32)
    cw_d = dint("cw_d", (8, T), F32)

    def sbuf(stack, name, shape, dtype, side=None):
        return stack.enter_context(nc.sbuf_tensor(name, list(shape), dtype, side=side))

    conv_w_sb = sbuf(es, "conv_w_sb", (128, NT_XBC * 4), F32)
    conv_b_sb = sbuf(es, "conv_b_sb", (128, NT_XBC), F32)
    D_cols_sb = sbuf(es, "D_cols_sb", (128, NPAIR), F32)
    normw_sb = sbuf(es, "normw_sb", (128, NT_Z), F32)
    dtb_sb = sbuf(es, "dtb_sb", (NHEADS, 1), F32)
    alog_sb = sbuf(es, "alog_sb", (NHEADS, 1), F32)
    expalog_sb = sbuf(es, "expalog_sb", (NHEADS, 1), F32)
    ident_sb = sbuf(es, "ident_sb", (128, 128), BF16)
    identf_sb = sbuf(es, "identf_sb", (128, 128), F32)
    ones_row_sb = sbuf(es, "ones_row_sb", (1, 128), F32)
    ones_col_sb = sbuf(es, "ones_col_sb", (128, 1), F32)
    masktri_sb = sbuf(es, "masktri_sb", (CH, CH), F32)
    mask1_sb = sbuf(es, "mask1_sb", (128, 1), F32)
    mask2_sb = sbuf(es, "mask2_sb", (128, 1), F32)
    misc_sb = sbuf(es, "misc_sb", (128, 3), F32)
    ms1 = ExitStack()
    dt_sb = sbuf(ms1, "dt_sb", (NHEADS, T), F32, side="right")
    dtp_sb = sbuf(ms1, "dtp_sb", (NHEADS, T), F32, side="right")
    ctx3_sb = sbuf(ms1, "ctx3_sb", (128, NT_XBC * 3), F32, side="right")
    xc = [sbuf(ms1, f"xc{i}", (128, NT_XBC * (CH + 3)), BF16, side="right") for i in range(2)]
    ms2 = ExitStack()
    S0b_sb = sbuf(ms2, "S0b_sb", (128, NHEADS * HEADDIM), BF16)
    Cfull_sb = sbuf(ms2, "Cfull_sb", (128, NCH * CH), BF16)
    Alast_sb = sbuf(ms2, "Alast_sb", (NHEADS, NCH), F32)
    cumprev_sb = sbuf(ms2, "cumprev_sb", (NHEADS, NCH), F32)

    KC = ("consts",)
    for dst, src in [
        (conv_w_sb, conv_w_d.rearrange("p a b -> p (a b)")), (conv_b_sb, conv_b_d),
        (D_cols_sb, D_cols_d), (normw_sb, normw_d), (dtb_sb, dtb_d),
        (alog_sb, alog_d), (ident_sb, ident_bf_d), (identf_sb, ident_f32_d),
        (ones_row_sb, ones_row_f32_d), (ones_col_sb, ones_col_f32_d),
        (masktri_sb, masktri_d), (mask1_sb, mask1_d), (mask2_sb, mask2_d),
        (misc_sb, misc_d),
    ]:
        p.dma("sync", "consts", dst[:], src, writes=[KC])
    p.op("scalar", lambda e: e.activation(expalog_sb[:], alog_sb[:], AF.Exp),
         reads=[KC], writes=[KC])

    ph = ExitStack()
    xst = [sbuf(ph, f"a1_x{i}", (128, 2048), F32) for i in range(2)]
    lnb = [sbuf(ph, f"a1_ln{i}", (128, 2048), BF16) for i in range(2)]
    st6 = sbuf(ph, "a1_st6", (128, 24), F32)
    mv_sb = sbuf(ph, "a1_mv", (128, 2), F32)
    rstd_sb = sbuf(ph, "a1_rstd", (128, 1), F32)
    mr_sb = sbuf(ph, "a1_mr", (128, 1), F32)

    for st in range(2 * NST + 1):
        par = st % 2
        xb, lb = xst[par], lnb[par]
        kx, kl = ("a1x", par), ("a1l", par)
        if st < NST:
            src = x_tm[st * 128:(st + 1) * 128, :]
        elif st == NST:
            src = x_ctx[:, :]
        else:
            sp = st - NST - 1
            src = x_ptm[sp * 128:(sp + 1) * 128, :]
        p.dma("scalar", f"a1_in{par}", xb[:], src, writes=[kx])
        for q in range(4):
            p.op("vector", lambda e, xb=xb, q=q: e.bn_stats(
                out=st6[:, q * 6:(q + 1) * 6], in_=xb[:, q * 512:(q + 1) * 512]),
                reads=[kx], writes=[("a1st6",)])
        p.op("vector", lambda e: e.bn_aggr(out=mv_sb[:], in_=st6[:]),
             reads=[("a1st6",)], writes=[("a1mv",)])
        p.op("scalar", lambda e: e.activation(rstd_sb[:], mv_sb[:, 1:2], AF.Sqrt, bias=misc_sb[:, 0:1]),
             reads=[("a1mv",), KC], writes=[("a1rstd",)])
        p.op("vector", lambda e: e.reciprocal(rstd_sb[:], rstd_sb[:]),
             reads=[("a1rstd",)], writes=[("a1rstd",)])
        p.op("vector", lambda e: e.tensor_tensor(mr_sb[:], mv_sb[:, 0:1], rstd_sb[:], op=ALU.mult),
             reads=[("a1mv",), ("a1rstd",)], writes=[("a1mr",)])
        p.op("vector", lambda e, xb=xb, lb=lb: e.tensor_scalar(
            lb[:], xb[:], rstd_sb[:], mr_sb[:], op0=ALU.mult, op1=ALU.subtract),
            reads=[kx, ("a1rstd",), ("a1mr",)], writes=[kl])
        if st <= NST:
            p.dma("scalar", "a1_out", ln1_d[st * 128:(st + 1) * 128, :], lb[:],
                  reads=[kl], writes=[("d_ln1",)])
        else:
            sp = st - NST - 1
            p.dma("scalar", "a1_out", ln1p_d[sp * 128:(sp + 1) * 128, :], lb[:],
                  reads=[kl], writes=[("d_ln1p",)])

    p.barrier()
    ph.close()

    c1s = ExitStack()
    convo = sbuf(c1s, "convo", (128, NT_XBC * CH), F32)
    xbcs = sbuf(c1s, "xbcs", (128, NT_XBC * CH), BF16)
    dtc = sbuf(c1s, "dtc", (NHEADS, CH), F32)
    acs = sbuf(c1s, "acs", (NHEADS, CH), F32)
    nacs = sbuf(c1s, "nacs", (NHEADS, CH), F32)
    expa = sbuf(c1s, "expa", (NHEADS, CH), BF16)
    decay = sbuf(c1s, "decay", (NHEADS, CH), F32)
    ddec = sbuf(c1s, "ddec", (NHEADS, CH), F32)
    t64a = sbuf(c1s, "t64a", (NHEADS, CH), F32)
    t64b = sbuf(c1s, "t64b", (NHEADS, CH), F32)
    eacol = sbuf(c1s, "eacol", (NHEADS, 1), F32)
    dtT = sbuf(c1s, "dtT", (128, NHEADS), F32)
    ddecT = sbuf(c1s, "ddecT", (128, NHEADS), F32)
    nacsT = sbuf(c1s, "nacsT", (128, NHEADS), F32)
    EAb = sbuf(c1s, "EAb", (128, NHEADS), F32)
    earow = sbuf(c1s, "earow", (1, NHEADS), F32)
    xhtm = sbuf(c1s, "xhtm", (128, D_INNER), BF16)
    Xdec = sbuf(c1s, "Xdec", (128, D_INNER), BF16)
    Btm = sbuf(c1s, "Btm", (128, CH), BF16)
    Gsb = sbuf(c1s, "Gsb", (128, CH), BF16)
    t128 = sbuf(c1s, "t128", (128, CH), F32)
    S_sb = sbuf(c1s, "S_sb", (128, NHEADS * HEADDIM), F32)
    psS = [c1s.enter_context(nc.psum_tensor(f"psS{i}", [128, 512], F32)) for i in range(2)]
    psM = c1s.enter_context(nc.psum_tensor("psM", [128, 512], F32))
    psMb = c1s.enter_context(nc.psum_tensor("psMb", [128, 1024], BF16))
    KM, KMB = ("psM",), ("psMb",)
    CHW = CH + 3

    a2s = ExitStack()
    TC = T + 128
    ln1f = sbuf(a2s, "ln1f", (128, 16 * TC), BF16, side="right")
    w2 = [sbuf(a2s, f"a2_w{i}", (128, 16 * 128), BF16, side="right") for i in range(2)]
    ev_b = [sbuf(a2s, f"a2_eb{i}", (128, T), BF16, side="right") for i in range(2)]
    ev_f = sbuf(a2s, "a2_ef", (128, T), F32, side="right")
    ctx_ev = sbuf(a2s, "a2_ctxev", (128, 128), F32, side="right")
    ps_a2 = [a2s.enter_context(nc.psum_tensor(f"psA{i}", [128, 512], F32))
             for i in range(4)]
    TSL = slices(T, 512)
    KLN1F = [("ln1f", kt) for kt in range(16)]

    def emit_a2_tile(nt, need_ctx, xbc_dst, kxbc_dst, dt_dst, kdt_dst, z_out):
        if True:
            par = nt % 2
            wb = w2[par]
            kw = ("a2w", par)
            p.dma("gpsimd", f"a2_w{par}", wb[:], w_in_r[nt].rearrange("d k n -> d (k n)"),
                  writes=[kw], tag=f"w_in {nt}")
            pset = [ps_a2[i] for i in range(len(TSL))]
            pk = [("psA", i) for i in range(len(TSL))]
            for kt in range(16):
                for s, (n0, n1) in enumerate(TSL):
                    p.op("tensor",
                         lambda e, wb=wb, kt=kt, n0=n0, n1=n1, ps=pset[s]:
                         e.matmul(ps[:, 0:n1 - n0], wb[:, kt * 128:(kt + 1) * 128],
                                  ln1f[:, kt * TC + n0:kt * TC + n1],
                                  start=(kt == 0), stop=(kt == 15)),
                         reads=[kw, KLN1F[kt]], writes=[pk[s]])
            is_z = nt < NT_Z
            is_dt = nt >= NT_Z + NT_XBC
            eb = ev_b[par]
            keb = ("a2eb", par)
            for s, (n0, n1) in enumerate(TSL):
                if is_dt:
                    p.op("vector", lambda e, ps=pset[s], n0=n0, n1=n1:
                         e.tensor_copy(ev_f[:, n0:n1], ps[:, 0:n1 - n0]),
                         reads=[pk[s]], writes=[("a2ef",)])
                elif s % 2 == 0:
                    p.op("vector", lambda e, ps=pset[s], n0=n0, n1=n1, eb=eb:
                         e.tensor_copy(eb[:, n0:n1], ps[:, 0:n1 - n0]),
                         reads=[pk[s]], writes=[keb])
                else:
                    p.op("scalar", lambda e, ps=pset[s], n0=n0, n1=n1, eb=eb:
                         e.copy(eb[:, n0:n1], ps[:, 0:n1 - n0]),
                         reads=[pk[s]], writes=[keb])
            if is_z:
                p.dma("scalar", "a2_zout", z_out[nt * 128:(nt + 1) * 128, :], eb[:, 0:T],
                      reads=[keb], writes=[("d_z",)])
            elif not is_dt:
                j = nt - NT_Z
                p.dma("scalar", "a2_xout", xbc_dst[j * 128:(j + 1) * 128, :], eb[:, 0:T],
                      reads=[keb], writes=[kxbc_dst])
                if need_ctx:
                    pc, kpc = pset[0], pk[0]
                    for kt in range(16):
                        p.op("tensor", lambda e, wb=wb, kt=kt, pc=pc:
                             e.matmul(pc[:, 0:128], wb[:, kt * 128:(kt + 1) * 128],
                                      ln1f[:, kt * TC + T:kt * TC + T + 128],
                                      start=(kt == 0), stop=(kt == 15)),
                             reads=[kw, KLN1F[kt]], writes=[kpc])
                    p.op("vector", lambda e, pc=pc: e.tensor_copy(ctx_ev[:], pc[:, 0:128]),
                         reads=[kpc], writes=[("a2ctx",)])
                    p.op("vector", lambda e, j=j: e.tensor_copy(
                        ctx3_sb[:, j * 3:(j + 1) * 3], ctx_ev[:, 0:3]),
                        reads=[("a2ctx",)], writes=[("ctx3",)])
            else:
                p.op("vector", lambda e, dt_dst=dt_dst: e.tensor_copy(
                    dt_dst[:, 0:T], ev_f[0:64, 0:T]),
                    reads=[("a2ef",)], writes=[kdt_dst])

    def emit_chunk(c, full, xbcd, kxbcd, dtbuf, kdtbuf):
        par = c % 2
        xcb = xc[par]
        kxc = ("xc", par)
        p.dma("sync", f"c1_xbc{par}",
              xcb[:].rearrange("p (nt w) -> p nt w", w=CHW)[:, :, 3:3 + CH],
              xbcd.rearrange("(nt p) t -> p nt t", p=128)[:, :, c * CH:(c + 1) * CH],
              reads=[kxbcd], writes=[kxc])
        for nt in range(NT_XBC):
            o = nt * CHW
            co = nt * CH
            if nt % 2 == 0:
                p.op("vector", lambda e, o=o, co=co, nt=nt, xcb=xcb: e.tensor_scalar(
                    convo[:, co:co + CH], xcb[:, o:o + CH],
                    conv_w_sb[:, 4 * nt:4 * nt + 1], conv_b_sb[:, nt:nt + 1],
                    op0=ALU.mult, op1=ALU.add),
                    reads=[kxc, KC], writes=[("convo", nt)])
                for tap in range(1, 4):
                    p.op("vector", lambda e, o=o, co=co, nt=nt, tap=tap, xcb=xcb:
                         e.scalar_tensor_tensor(
                             convo[:, co:co + CH], xcb[:, o + tap:o + tap + CH],
                             conv_w_sb[:, 4 * nt + tap:4 * nt + tap + 1],
                             convo[:, co:co + CH], op0=ALU.mult, op1=ALU.add),
                         reads=[kxc, KC, ("convo", nt)], writes=[("convo", nt)])
            else:
                for tap in range(4):
                    wv = conv_w_sb[:, 4 * nt + tap:4 * nt + tap + 1].to_broadcast((128, 1, CH))
                    if tap == 0:
                        p.op("gpsimd", lambda e, o=o, co=co, wv=wv, xcb=xcb: e.tensor_tensor(
                            convo[:, co:co + CH], xcb[:, o:o + CH], wv, op=ALU.mult),
                            reads=[kxc, KC], writes=[("convo", nt)])
                    else:
                        p.op("gpsimd", lambda e, o=o, tap=tap, wv=wv, xcb=xcb:
                             e.tensor_tensor(t128[:, 0:CH], xcb[:, o + tap:o + tap + CH],
                                             wv, op=ALU.mult),
                             reads=[kxc, KC], writes=[("t128",)])
                        p.op("gpsimd", lambda e, co=co, nt=nt: e.tensor_tensor(
                            convo[:, co:co + CH], convo[:, co:co + CH], t128[:, 0:CH],
                            op=ALU.add),
                            reads=[("t128",), ("convo", nt)], writes=[("convo", nt)])
                p.op("gpsimd", lambda e, co=co, nt=nt: e.tensor_tensor(
                    convo[:, co:co + CH], convo[:, co:co + CH],
                    conv_b_sb[:, nt:nt + 1].to_broadcast((128, 1, CH)), op=ALU.add),
                    reads=[KC, ("convo", nt)], writes=[("convo", nt)])
            eng = "vector" if nt % 2 == 0 else "gpsimd"
            p.op(eng, lambda e, o=o, opar=1 - par, xcb=xcb: e.tensor_copy(
                xc[opar][:, o:o + 3], xcb[:, o + CH:o + CH + 3]),
                reads=[kxc], writes=[("xc", 1 - par)])
        KCV = [("convo", nt) for nt in range(NT_XBC)]
        if full:
            for nt in range(0, NT_XBC, 6):
                p.op("tensor", lambda e: e.matmul(
                    psM[:, 448:448 + 64], ones_row_sb[:], earow[:],
                    start=True, stop=True),
                    reads=[("convo", nt), ("earow",), KC], writes=[KM])
        p.op("scalar", lambda e: e.activation(xbcs[:], convo[:], AF.Silu),
             reads=KCV, writes=[("xbcs",)])
        B_nm = xbcs[:, NT_Z * CH:(NT_Z + 1) * CH]
        C_nm = xbcs[:, (NT_Z + 1) * CH:(NT_Z + 2) * CH]
        if full:
            p.op("vector", lambda e, c=c, C_nm=C_nm: e.tensor_copy(
                Cfull_sb[:, c * CH:(c + 1) * CH], C_nm),
                reads=[("xbcs",)], writes=[("Cfull",)])

        cs0 = c * CH
        dtsl = dtbuf[:, cs0:cs0 + CH]
        nacsv = nacs[:]
        KNC = ("nacs",)
        p.op("vector", lambda e, dtsl=dtsl: e.tensor_scalar(
            t64a[:], dtsl, expalog_sb[:], None, op0=ALU.mult),
            reads=[kdtbuf, KC], writes=[("t64a",)])
        p.op("vector", lambda e: e.tensor_tensor_scan(
            nacs[:], t64a[:], t64a[:], 0.0, op0=ALU.add, op1=ALU.bypass),
            reads=[("t64a",)], writes=[("nacs",)])
        p.op("vector", lambda e: e.tensor_scalar(acs[:], nacs[:], -1.0, None, op0=ALU.mult),
             reads=[("nacs",)], writes=[("acs",)])
        if full:
            p.dma("scalar", "c1_acs", acs_d[:, :], acs[:], reads=[("acs",)], writes=[("d_acs",)])
            p.op("scalar", lambda e: e.activation(expa[:], acs[:], AF.Exp),
                 reads=[("acs",)], writes=[("expa",)])
            p.dma("scalar", "c1_ea", ea_d[c], expa[:], reads=[("expa",)], writes=[("d_ea",)])
            p.op("vector", lambda e, c=c: e.tensor_copy(Alast_sb[:, c:c + 1], acs[:, CH - 1:CH]),
                 reads=[("acs",)], writes=[("Alast",)])
        p.op("scalar", lambda e, nacsv=nacsv: e.activation(decay[:], nacsv, AF.Exp,
                                              bias=acs[:, CH - 1:CH]),
             reads=[KNC, ("acs",)], writes=[("decay",)])
        p.op("vector", lambda e, dtsl=dtsl: e.tensor_tensor(ddec[:], dtsl, decay[:],
                                                            op=ALU.mult),
             reads=[kdtbuf, ("decay",)], writes=[("ddec",)])
        p.op("scalar", lambda e: e.activation(eacol[:], acs[:, CH - 1:CH], AF.Exp),
             reads=[("acs",)], writes=[("eacol",)])
        p.op("tensor", lambda e: e.transpose(psM[0:1, 384:384 + NHEADS], eacol[:],
                                             identf_sb[0:NHEADS, 0:NHEADS]),
             reads=[("eacol",), KC], writes=[KM])
        p.op("vector", lambda e: e.tensor_copy(earow[:], psM[0:1, 384:384 + NHEADS]),
             reads=[KM], writes=[("earow",)])
        p.op("tensor", lambda e: e.matmul(psM[:, 256:256 + NHEADS], ones_row_sb[:],
                                          earow[:], start=True, stop=True),
             reads=[("earow",), KC], writes=[KM])
        p.op("vector", lambda e: e.tensor_copy(EAb[:], psM[:, 256:256 + NHEADS]),
             reads=[KM], writes=[("EAb",)])
        trs = [(ddec[:], ddecT, ("ddec",), ("ddecT",), 0)]
        if full:
            trs += [(dtsl, dtT, kdtbuf, ("dtT",), 0), (nacsv, nacsT, KNC, ("nacsT",), 0)]
        for tsrc, tdst, rk, wk, tb in trs:
            p.op("tensor", lambda e, tsrc=tsrc, tb=tb: e.transpose(
                psM[:, 128:128 + NHEADS], tsrc,
                identf_sb[tb:tb + NHEADS, tb:tb + NHEADS]),
                reads=[rk, KC], writes=[KM])
            p.op("vector", lambda e, tdst=tdst: e.tensor_copy(tdst[:], psM[:, 128:128 + NHEADS]),
                 reads=[KM], writes=[wk])

        for grp in range(4):
            for i8 in range(8):
                jx = grp * 8 + i8
                p.op("tensor", lambda e, i8=i8, jx=jx: e.transpose(
                    psMb[:, i8 * 128:(i8 + 1) * 128], xbcs[:, jx * CH:(jx + 1) * CH],
                    ident_sb[:]),
                    reads=[("xbcs",), KC], writes=[KMB])
            p.op("scalar", lambda e, grp=grp: e.copy(
                xhtm[:, grp * 1024:(grp + 1) * 1024], psMb[:, 0:1024]),
                reads=[KMB], writes=[("xhtm",)])
        if full:
            p.op("vector", lambda e: e.tensor_tensor(
                Xp[:], xhtm[:], rep_last(dtT[:], HEADDIM), op=ALU.mult),
                reads=[("xhtm",), ("dtT",)], writes=[("Xp",)])
        p.op("vector", lambda e: e.tensor_tensor(
            Xdec[:], xhtm[:], rep_last(ddecT[:], HEADDIM), op=ALU.mult),
            reads=[("xhtm",), ("ddecT",)], writes=[("Xdec",)])

        if full:
            p.op("tensor", lambda e, B_nm=B_nm, C_nm=C_nm: e.matmul(
                psM[:, 0:CH], B_nm, C_nm, start=True, stop=True),
                reads=[("xbcs",)], writes=[KM])
            p.op("vector", lambda e: e.tensor_copy(Gsb[:], psM[:, 0:CH]),
                 reads=[KM], writes=[("G",)])
        p.op("tensor", lambda e, B_nm=B_nm: e.transpose(psMb[:, 0:CH], B_nm, ident_sb[:]),
             reads=[("xbcs",), KC], writes=[KMB])
        p.op("vector", lambda e: e.tensor_copy(Btm[:], psMb[:, 0:CH]),
             reads=[KMB], writes=[("Btm",)])

        for hh in range(NHEADS // HHALF):
            h0 = hh * HHALF
            if full:
                p.op("vector", lambda e, h0=h0: e.tensor_tensor(
                    Dm[:], rep_mid(masktri_sb[:], HHALF),
                    rep_last(nacsT[:, h0:h0 + HHALF], CH), op=ALU.add),
                    reads=[KC, ("nacsT",)], writes=[("Dm",)])
                p.dma("gpsimd", "c1_abc", Dm[:],
                      acs_d.rearrange("h l -> (h l)").unsqueeze(0)[:, h0 * CH:(h0 + HHALF) * CH]
                      .to_broadcast((128, HHALF * CH)),
                      reads=[("d_acs",), ("Dm",)], writes=[("Dm",)],
                      accum_op=ALU.add)
                p.dma("sync", "c1_eabc", eabc[:],
                      ea_d[c].rearrange("h l -> (h l)").unsqueeze(0)[:, h0 * CH:(h0 + HHALF) * CH]
                      .to_broadcast((128, HHALF * CH)),
                      reads=[("d_ea",)], writes=[("eabc",)])
                p.op("scalar", lambda e: e.activation(Lh[:], Dm[:], AF.Exp),
                     reads=[("Dm",)], writes=[("Lh",)])
                p.op("vector", lambda e: e.tensor_tensor(
                    Mth[:], Lh[:], rep_mid(Gsb[:], HHALF), op=ALU.mult),
                    reads=[("Lh",), ("G",)], writes=[("Mth",)])
                p.op("gpsimd", lambda e, C_nm=C_nm: e.tensor_tensor(
                    Cexp[:], eabc[:], rep_mid(C_nm, HHALF), op=ALU.mult),
                    reads=[("eabc",), ("xbcs",)], writes=[("Cexp",)])

                for jj in range(HHALF // 2):
                    j = hh * (HHALF // 2) + jj
                    bank, kb = psY[jj % 2], ("psY", jj % 2)
                    for u in range(2):
                        h = 2 * j + u
                        lh = h - h0
                        rows = slice(64 * u, 64 * u + 64)
                        p.op("tensor", lambda e, bank=bank, rows=rows, h=h, lh=lh:
                             e.matmul(bank[rows, 0:CH], Xp[:, h * 64:h * 64 + 64],
                                      Mth[:, lh * CH:(lh + 1) * CH], start=True, stop=False),
                             reads=[("Xp",), ("Mth",)], writes=[kb])
                        p.op("tensor", lambda e, bank=bank, rows=rows, h=h, lh=lh:
                             e.matmul(bank[rows, 0:CH], Sb_sb[:, h * 64:h * 64 + 64],
                                      Cexp[:, lh * CH:(lh + 1) * CH], start=False, stop=True),
                             reads=[("Sb",), ("Cexp",)], writes=[kb])
                    p.op("vector", lambda e, j=j, bank=bank: e.scalar_tensor_tensor(
                        ysb[:, j * CH:(j + 1) * CH], xbcs[:, j * CH:(j + 1) * CH],
                        D_cols_sb[:, j:j + 1], bank[:, 0:CH], op0=ALU.mult, op1=ALU.add),
                        reads=[("xbcs",), kb, KC], writes=[("ysb",)])

            for g in range(HHALF // 8):
                gg = hh * (HHALF // 8) + g
                bank, kb = psS[gg % 2], ("psS", gg % 2)
                hbase = h0 + g * 8
                for i in range(8):
                    h = hbase + i
                    p.op("tensor", lambda e, bank=bank, i=i, h=h: e.matmul(
                        bank[:, i * 64:(i + 1) * 64], Btm[:],
                        Xdec[:, h * 64:(h + 1) * 64], start=True, stop=True),
                        reads=[("Btm",), ("Xdec",)], writes=[kb])
                base = hbase * 64
                p.op("vector", lambda e, base=base, hbase=hbase: e.tensor_tensor(
                    S_sb[:, base:base + 512], S_sb[:, base:base + 512],
                    rep_last(EAb[:, hbase:hbase + 8], 64), op=ALU.mult),
                    reads=[("S",), ("EAb",)], writes=[("S",)])
                p.op("vector", lambda e, base=base, bank=bank: e.tensor_tensor(
                    S_sb[:, base:base + 512], S_sb[:, base:base + 512],
                    bank[:, 0:512], op=ALU.add),
                    reads=[("S",), kb], writes=[("S",)])
        if full:
            p.op("vector", lambda e: e.tensor_copy(Sb_sb[:], S_sb[:]),
                 reads=[("S",)], writes=[("Sb",)])
            p.dma("scalar", "c1_y", y_d[c], ysb[:], reads=[("ysb",)], writes=[("d_y", c)])

    NTH = T // CH

    def emit_dt_hoist(dtbuf, kdtbuf):
        KSC = [("convo", nt) for nt in range(NTH)]
        sa = convo[0:64, 0:T]
        p.op("vector", lambda e: e.tensor_scalar(
            dtbuf[:, 0:T], dtbuf[:, 0:T], dtb_sb[:], None, op0=ALU.add),
            reads=[kdtbuf, KC], writes=[kdtbuf])
        p.op("scalar", lambda e: e.activation(sa, dtbuf[:, 0:T], AF.Abs),
             reads=[kdtbuf], writes=KSC)
        p.op("scalar", lambda e: e.activation(sa, sa, AF.Exp,
                                              scale=misc_sb[0:NHEADS, 1:2]),
             reads=KSC + [KC], writes=KSC)
        p.op("scalar", lambda e: e.activation(sa, sa, AF.Ln, bias=1.0),
             reads=KSC, writes=KSC)
        p.op("vector", lambda e: e.tensor_scalar_max(dtbuf[:, 0:T], dtbuf[:, 0:T], 0.0),
             reads=[kdtbuf] + KSC, writes=[kdtbuf])
        p.op("vector", lambda e: e.tensor_tensor(dtbuf[:, 0:T], dtbuf[:, 0:T], sa,
                                                 op=ALU.add),
             reads=[kdtbuf] + KSC, writes=[kdtbuf])

    for kt in range(16):
        p.dma_tr("sync", "a2_lnf", ln1f[:, kt * TC:kt * TC + T],
                 ln1p_d[0:T, kt * 128:(kt + 1) * 128],
                 reads=[("d_ln1p",)], writes=[("ln1f", kt)])
    for nt in range(NT_Z, NT_IN):
        emit_a2_tile(nt, False, xbcp_d, ("d_xbcp",), dtp_sb, ("dtp_sb",), None)

    for kt in range(16):
        p.dma_tr("sync", "a2_lnf", ln1f[:, kt * TC:kt * TC + TC],
                 ln1_d[0:TC, kt * 128:(kt + 1) * 128],
                 reads=[("d_ln1",)], writes=[("ln1f", kt)])

    p.op("vector", lambda e: e.memset(S_sb[:], 0.0), writes=[("S",)])
    p.op("vector", lambda e: e.memset(
        xc[0][:].rearrange("p (nt w) -> p nt w", w=CHW)[:, :, 0:3], 0.0),
        writes=[("xc", 0)])
    emit_dt_hoist(dtp_sb, ("dtp_sb",))
    ti = 0
    for c in range(NCH):
        quota = (NT_IN * (c + 1) + NCH - 1) // NCH
        while ti < min(quota, NT_IN):
            emit_a2_tile(ti, True, xbc_d, ("d_xbc",), dt_sb, ("dt_sb",), z_d)
            ti += 1
        emit_chunk(c, False, xbcp_d, ("d_xbcp",), dtp_sb, ("dtp_sb",))
    while ti < NT_IN:
        emit_a2_tile(ti, True, xbc_d, ("d_xbc",), dt_sb, ("dt_sb",), z_d)
        ti += 1
    p.op("vector", lambda e: e.tensor_copy(S0b_sb[:], S_sb[:]),
         reads=[("S",)], writes=[("S0b",)])

    p.barrier()
    a2s.close()

    c1x = ExitStack()
    Xp = sbuf(c1x, "Xp", (128, D_INNER), BF16)
    Dm = sbuf(c1x, "Dm", (128, HHALF * CH), F32)
    Lh = sbuf(c1x, "Lh", (128, HHALF * CH), BF16)
    Mth = sbuf(c1x, "Mth", (128, HHALF * CH), BF16)
    eabc = sbuf(c1x, "eabc", (128, HHALF * CH), BF16)
    Cexp = sbuf(c1x, "Cexp", (128, HHALF * CH), BF16)
    ysb = sbuf(c1x, "ysb", (128, NT_Z * CH), F32)
    Sb_sb = sbuf(c1x, "Sb_sb", (128, NHEADS * HEADDIM), BF16)
    psY = [c1x.enter_context(nc.psum_tensor(f"psY{i}", [128, 512], F32)) for i in range(2)]

    p.op("vector", lambda e: e.memset(S_sb[:], 0.0), reads=[("S0b",)], writes=[("S",)])
    p.op("vector", lambda e: e.memset(Sb_sb[:], 0.0), writes=[("Sb",)])
    p.op("vector", lambda e: e.tensor_copy(
        xc[0][:].rearrange("p (nt w) -> p nt w", w=CHW)[:, :, 0:3],
        ctx3_sb[:].rearrange("p (nt w) -> p nt w", w=3)),
        reads=[("ctx3",)], writes=[("xc", 0)])
    emit_dt_hoist(dt_sb, ("dt_sb",))
    for c in range(NCH):
        emit_chunk(c, True, xbc_d, ("d_xbc",), dt_sb, ("dt_sb",))

    p.op("vector", lambda e: e.tensor_tensor_scan(
        cumprev_sb[:, 1:NCH], Alast_sb[:, 0:NCH - 1], Alast_sb[:, 0:NCH - 1], 0.0,
        op0=ALU.add, op1=ALU.bypass),
        reads=[("Alast",)], writes=[("cumprev",)])
    p.op("vector", lambda e: e.memset(cumprev_sb[:, 0:1], 0.0), writes=[("cumprev",)])

    p.barrier()
    c1x.close()
    c1s.close()
    ms1.close()

    ph = ExitStack()
    yn_sb = sbuf(ph, "yn_sb", (128, NT_Z * HALF_T), BF16)
    ez = sbuf(ph, "ez", (NHEADS, CH), BF16)
    edc = sbuf(ph, "edc", (NHEADS, 1), F32)
    scc = sbuf(ph, "scc", (NHEADS, CH), BF16)
    scbc = sbuf(ph, "scbc", (128, HHALF * CH), BF16)
    Cph = sbuf(ph, "Cph", (128, HHALF * CH), BF16)
    yc2 = sbuf(ph, "yc2", (128, NT_Z * CH), F32)
    zc2 = sbuf(ph, "zc2", (128, NT_Z * CH), BF16)
    slz = sbuf(ph, "slz", (128, NT_Z * CH), F32)
    sqz = sbuf(ph, "sqz", (128, NT_Z * CH), F32)
    rrow = sbuf(ph, "rrow", (1, CH), F32)
    psC = [ph.enter_context(nc.psum_tensor(f"psC{i}", [128, 512], F32)) for i in range(2)]
    psR = ph.enter_context(nc.psum_tensor("psR", [128, 1024], F32))
    wD = [sbuf(ph, f"wD{i}", (128, 32 * 128), BF16) for i in range(2)]
    x0D = sbuf(ph, "x0D", (128, HALF_T), F32)
    lnD = sbuf(ph, "lnD", (128, HALF_T), BF16)
    x1D = sbuf(ph, "x1D", (128, HALF_T), F32)
    psD = [ph.enter_context(nc.psum_tensor(f"psD{i}", [128, 512], F32)) for i in range(4)]
    DSL = slices(HALF_T, 512)

    for half in range(2):
        for c in range(half * (NCH // 2), (half + 1) * (NCH // 2)):
            zrel = c - half * (NCH // 2)
            p.dma("sync", "c2_ez", ez[:], ea_d[c], reads=[("d_ea",)], writes=[("ez",)])
            p.op("scalar", lambda e, c=c: e.activation(edc[:], cumprev_sb[:, c:c + 1], AF.Exp),
                 reads=[("cumprev",)], writes=[("edc",)])
            p.op("vector", lambda e: e.tensor_tensor(edc[:], edc[:], mask2_sb[0:NHEADS, :],
                                                     op=ALU.mult),
                 reads=[("edc",), KC], writes=[("edc",)])
            p.op("vector", lambda e: e.tensor_scalar(scc[:], ez[:], edc[:], None, op0=ALU.mult),
                 reads=[("ez",), ("edc",)], writes=[("scc",)])
            p.dma("scalar", "c2_sc", sc_d[:, :], scc[:], reads=[("scc",)], writes=[("d_sc",)])
            p.dma("sync", "c2_y", yc2[:], y_d[c], reads=[("d_y", c)], writes=[("yc2",)])
            for hh in range(NHEADS // HHALF):
                h0 = hh * HHALF
                p.dma("sync", "c2_scbc", scbc[:],
                      sc_d.rearrange("h l -> (h l)").unsqueeze(0)[:, h0 * CH:(h0 + HHALF) * CH]
                      .to_broadcast((128, HHALF * CH)),
                      reads=[("d_sc",)], writes=[("scbc",)])
                p.op("vector", lambda e, c=c: e.tensor_tensor(
                    Cph[:], scbc[:], rep_mid(Cfull_sb[:, c * CH:(c + 1) * CH], HHALF),
                    op=ALU.mult), reads=[("scbc",), ("Cfull",)], writes=[("Cph",)])
                for jj in range(HHALF // 2):
                    j = hh * (HHALF // 2) + jj
                    bank, kb = psC[jj % 2], ("psC", jj % 2)
                    for u in range(2):
                        h = 2 * j + u
                        lh = h - h0
                        rows = slice(64 * u, 64 * u + 64)
                        p.op("tensor", lambda e, bank=bank, rows=rows, h=h, lh=lh:
                             e.matmul(bank[rows, 0:CH], S0b_sb[:, h * 64:h * 64 + 64],
                                      Cph[:, lh * CH:(lh + 1) * CH], start=True, stop=True),
                             reads=[("S0b",), ("Cph",)], writes=[kb])
                    p.op("vector", lambda e, j=j, bank=bank: e.tensor_tensor(
                        yc2[:, j * CH:(j + 1) * CH], yc2[:, j * CH:(j + 1) * CH],
                        bank[:, 0:CH], op=ALU.add),
                        reads=[("yc2",), kb], writes=[("yc2",)])
            p.dma("sync", "c2_z", zc2[:].rearrange("p (nt l) -> p nt l", l=CH),
                  z_d.rearrange("(nt p) t -> p nt t", p=128)[:, :, c * CH:(c + 1) * CH],
                  reads=[("d_z",)], writes=[("zc2",)])
            p.op("scalar", lambda e: e.activation(slz[:], zc2[:], AF.Silu),
                 reads=[("zc2",)], writes=[("slz",)])
            p.op("vector", lambda e: e.tensor_tensor(yc2[:], yc2[:], slz[:], op=ALU.mult),
                 reads=[("yc2",), ("slz",)], writes=[("yc2",)])
            p.op("scalar", lambda e: e.activation(sqz[:], yc2[:], AF.Square),
                 reads=[("yc2",)], writes=[("sqz",)])
            for nt in range(NT_Z):
                p.op("tensor", lambda e, nt=nt: e.matmul(
                    psR[0:1, 0:CH], ones_col_sb[:], sqz[:, nt * CH:(nt + 1) * CH],
                    start=(nt == 0), stop=(nt == NT_Z - 1)),
                    reads=[("sqz",), KC], writes=[("psRs",)])
            p.op("scalar", lambda e: e.activation(rrow[:], psR[0:1, 0:CH], AF.Sqrt,
                                                  bias=misc_sb[0:1, 0:1],
                                                  scale=misc_sb[0:1, 2:3]),
                 reads=[("psRs",), KC], writes=[("rrow",)])
            p.op("vector", lambda e: e.reciprocal(rrow[:], rrow[:]),
                 reads=[("rrow",)], writes=[("rrow",)])
            p.op("tensor", lambda e: e.matmul(psR[:, 512:512 + CH], ones_row_sb[:],
                                              rrow[:], start=True, stop=True),
                 reads=[("rrow",), KC], writes=[("psRb",)])
            for nt in range(NT_Z):
                p.op("vector", lambda e, nt=nt, zrel=zrel: e.scalar_tensor_tensor(
                    yn_sb[:, nt * HALF_T + zrel * CH:nt * HALF_T + (zrel + 1) * CH],
                    yc2[:, nt * CH:(nt + 1) * CH], normw_sb[:, nt:nt + 1],
                    psR[:, 512:512 + CH], op0=ALU.mult, op1=ALU.mult),
                    reads=[("yc2",), ("psRb",), KC], writes=[("yn",)])

        for dt in range(16):
            par = dt % 2
            wb = wD[par]
            kw = ("wD", par)
            p.dma("gpsimd", f"d_w{par}", wb[:], w_out_r[dt].rearrange("f k d -> f (k d)"),
                  writes=[kw])
            pset = psD[par * 2: par * 2 + 2]
            pk = [("psD", par * 2 + i) for i in range(2)]
            for kt in range(32):
                for s, (n0, n1) in enumerate(DSL):
                    p.op("tensor", lambda e, wb=wb, kt=kt, n0=n0, n1=n1, ps=pset[s]:
                         e.matmul(ps[:, 0:n1 - n0], wb[:, kt * 128:(kt + 1) * 128],
                                  yn_sb[:, kt * HALF_T + n0:kt * HALF_T + n1],
                                  start=(kt == 0), stop=(kt == 31)),
                         reads=[kw, ("yn",)], writes=[pk[s]])
            p.dma("sync", "d_x0", x0D[:],
                  x_fm[dt * 128:(dt + 1) * 128, half * HALF_T:(half + 1) * HALF_T],
                  writes=[("x0D",)])
            p.dma_tr("sync", "d_ln", lnD[:],
                     ln1_d[half * HALF_T:(half + 1) * HALF_T, dt * 128:(dt + 1) * 128],
                     reads=[("d_ln1",)], writes=[("lnD",)])
            p.op("vector", lambda e: e.tensor_tensor(x1D[:], x0D[:], lnD[:], op=ALU.add),
                 reads=[("x0D",), ("lnD",)], writes=[("x1D",)])
            for s, (n0, n1) in enumerate(DSL):
                p.op("vector", lambda e, n0=n0, n1=n1, ps=pset[s]: e.tensor_tensor(
                    x1D[:, n0:n1], x1D[:, n0:n1], ps[:, 0:n1 - n0], op=ALU.add),
                    reads=[("x1D",), pk[s]], writes=[("x1D",)])
            p.dma("scalar", "d_x1",
                  x1_d[dt][:, half * HALF_T:(half + 1) * HALF_T], x1D[:],
                  reads=[("x1D",)], writes=[("d_x1",)])

    p.barrier()
    ph.close()
    ms2.close()

    ph = ExitStack()
    x2acc = sbuf(ph, "x2acc", (128, 16 * LG), F32)
    ln2 = sbuf(ph, "ln2", (128, 16 * LG), BF16)
    hsb = sbuf(ph, "hsb", (128, 16 * LG), BF16)
    sqE = [sbuf(ph, f"sqE{i}", (128, LG), F32) for i in range(2)]
    gkE = sbuf(ph, "gkE", (128, 16 * 8), BF16)
    rowsE = sbuf(ph, "rowsE", (1, 3 * LG), F32)
    abE = sbuf(ph, "abE", (128, 2 * LG), F32)
    g8 = sbuf(ph, "g8", (128, 8), F32)
    m8 = sbuf(ph, "m8", (128, 8), F32)
    maskE = sbuf(ph, "maskE", (128, 8), F32)
    eallE = sbuf(ph, "eallE", (128, 8), F32)
    em2E = sbuf(ph, "em2E", (128, 1), F32)
    denE = sbuf(ph, "denE", (128, 1), F32)
    negm1E = sbuf(ph, "negm1E", (128, 1), F32)
    cwE = sbuf(ph, "cwE", (128, 8), F32)
    cwTE = sbuf(ph, "cwTE", (8, LG), F32)
    cwrow = sbuf(ph, "cwrow", (1, LG), F32)
    cwb = sbuf(ph, "cwb", (128, LG), F32)
    usb = sbuf(ph, "usb", (128, LG), F32)
    usb2 = sbuf(ph, "usb2", (128, LG), F32)
    ssb = sbuf(ph, "ssb", (128, LG), BF16)
    ssb2 = sbuf(ph, "ssb2", (128, LG), BF16)
    wiA = [sbuf(ph, f"wiA{i}", (128, 16 * 128), BF16) for i in range(2)]
    wiB = [sbuf(ph, f"wiB{i}", (128, 16 * 128), BF16) for i in range(2)]
    woE = [sbuf(ph, f"woE{i}", (128, 16 * 128), BF16) for i in range(2)]
    psH0 = ph.enter_context(nc.psum_tensor("psH0", [128, 1024], F32))
    psH1 = ph.enter_context(nc.psum_tensor("psH1", [128, 1024], F32))
    psW = ph.enter_context(nc.psum_tensor("psW", [128, 1024], F32))
    psE = ph.enter_context(nc.psum_tensor("psE", [128, 1024], F32))
    KH0, KH1, KW_, KE = ("psH0",), ("psH1",), ("psW",), ("psE",)
    LSL = slices(LG, 512)

    p.dma("gpsimd", "e_gk", gkE[:], gk_r.rearrange("p k e -> p (k e)"), writes=[("gkE",)])

    for lg in range(NLG):
        t0 = lg * LG
        for dt in range(16):
            p.dma("sync", "e_x1", x2acc[:, dt * LG:(dt + 1) * LG],
                  x1_d[dt][:, t0:t0 + LG], reads=[("d_x1",)], writes=[("x2a", dt)])
        for dt in range(16):
            par = dt % 2
            p.op("scalar", lambda e, dt=dt, par=par: e.activation(
                sqE[par][:], x2acc[:, dt * LG:(dt + 1) * LG], AF.Square),
                reads=[("x2a", dt)], writes=[("sqE", par)])
            for (n0, n1) in LSL:
                p.op("tensor", lambda e, dt=dt, n0=n0, n1=n1: e.matmul(
                    psE[0:1, n0:n1], ones_col_sb[:], x2acc[:, dt * LG + n0:dt * LG + n1],
                    start=(dt == 0), stop=(dt == 15)),
                    reads=[("x2a", dt), KC], writes=[KE])
                p.op("tensor", lambda e, par=par, n0=n0, n1=n1, dt=dt: e.matmul(
                    psW[0:1, n0:n1], ones_col_sb[:], sqE[par][:, n0:n1],
                    start=(dt == 0), stop=(dt == 15)),
                    reads=[("sqE", par), KC], writes=[KW_])
        p.op("vector", lambda e: e.tensor_scalar(
            rowsE[0:1, 0:LG], psE[0:1, 0:LG], 1.0 / D_MODEL, None, op0=ALU.mult),
            reads=[KE], writes=[("rowsE",)])
        p.op("vector", lambda e: e.tensor_scalar(
            rowsE[0:1, LG:2 * LG], psW[0:1, 0:LG], 1.0 / D_MODEL, None, op0=ALU.mult),
            reads=[KW_, ("rowsE",)], writes=[("rowsE",)])
        p.op("vector", lambda e: e.tensor_tensor(
            rowsE[0:1, 2 * LG:3 * LG], rowsE[0:1, 0:LG], rowsE[0:1, 0:LG], op=ALU.mult),
            reads=[("rowsE",)], writes=[("rowsE",)])
        p.op("vector", lambda e: e.tensor_tensor(
            rowsE[0:1, LG:2 * LG], rowsE[0:1, LG:2 * LG], rowsE[0:1, 2 * LG:3 * LG],
            op=ALU.subtract), reads=[("rowsE",)], writes=[("rowsE",)])
        p.op("scalar", lambda e: e.activation(rowsE[0:1, LG:2 * LG],
                                              rowsE[0:1, LG:2 * LG], AF.Sqrt,
                                              bias=misc_sb[0:1, 0:1]),
             reads=[("rowsE",), KC], writes=[("rowsE",)])
        p.op("vector", lambda e: e.reciprocal(rowsE[0:1, LG:2 * LG], rowsE[0:1, LG:2 * LG]),
             reads=[("rowsE",)], writes=[("rowsE",)])
        p.op("vector", lambda e: e.tensor_tensor(
            rowsE[0:1, 0:LG], rowsE[0:1, 0:LG], rowsE[0:1, LG:2 * LG], op=ALU.mult),
            reads=[("rowsE",)], writes=[("rowsE",)])
        for (n0, n1) in LSL:
            p.op("tensor", lambda e, n0=n0, n1=n1: e.matmul(
                psE[:, n0:n1], ones_row_sb[:], rowsE[0:1, LG + n0:LG + n1],
                start=True, stop=True), reads=[("rowsE",), KC], writes=[KE])
            p.op("vector", lambda e, n0=n0, n1=n1: e.tensor_copy(
                abE[:, n0:n1], psE[:, n0:n1]), reads=[KE], writes=[("abE",)])
            p.op("tensor", lambda e, n0=n0, n1=n1: e.matmul(
                psW[:, n0:n1], ones_row_sb[:], rowsE[0:1, n0:n1],
                start=True, stop=True), reads=[("rowsE",), KC], writes=[KW_])
            p.op("vector", lambda e, n0=n0, n1=n1: e.tensor_copy(
                abE[:, LG + n0:LG + n1], psW[:, n0:n1]), reads=[KW_], writes=[("abE",)])
        for dt in range(16):
            par = dt % 2
            p.op("vector", lambda e, dt=dt, par=par: e.tensor_tensor(
                sqE[par][:], x2acc[:, dt * LG:(dt + 1) * LG], abE[:, 0:LG], op=ALU.mult),
                reads=[("x2a", dt), ("abE",)], writes=[("sqE", par)])
            p.op("vector", lambda e, dt=dt, par=par: e.tensor_tensor(
                ln2[:, dt * LG:(dt + 1) * LG], sqE[par][:], abE[:, LG:2 * LG],
                op=ALU.subtract),
                reads=[("sqE", par), ("abE",)], writes=[("ln2", dt)])
        KLN2 = [("ln2", dt) for dt in range(16)]
        for st in range(LG // 128):
            for kt in range(16):
                p.op("tensor", lambda e, kt=kt, st=st: e.matmul(
                    psE[:, 0:8], ln2[:, kt * LG + st * 128:kt * LG + (st + 1) * 128],
                    gkE[:, kt * 8:(kt + 1) * 8], start=(kt == 0), stop=(kt == 15)),
                    reads=[KLN2[kt], ("gkE",)], writes=[KE])
            p.op("vector", lambda e: e.tensor_copy(g8[:], psE[:, 0:8]),
                 reads=[KE], writes=[("g8",)])
            p.op("vector", lambda e: e.max(out=m8[:], in_=g8[:]),
                 reads=[("g8",)], writes=[("m8",)])
            p.op("vector", lambda e: e.tensor_scalar(maskE[:], g8[:], m8[:, 1:2], None,
                                                     op0=ALU.is_ge),
                 reads=[("g8",), ("m8",)], writes=[("maskE",)])
            p.op("vector", lambda e: e.tensor_scalar(negm1E[:], m8[:, 0:1], -1.0, None,
                                                     op0=ALU.mult),
                 reads=[("m8",)], writes=[("negm1",)])
            p.op("scalar", lambda e: e.activation(eallE[:], g8[:], AF.Exp, bias=negm1E[:]),
                 reads=[("g8",), ("negm1",)], writes=[("eall",)])
            p.op("scalar", lambda e: e.activation(em2E[:], m8[:, 1:2], AF.Exp, bias=negm1E[:]),
                 reads=[("m8",), ("negm1",)], writes=[("em2",)])
            p.op("vector", lambda e: e.tensor_scalar(denE[:], em2E[:], 1.0, None, op0=ALU.add),
                 reads=[("em2",)], writes=[("den",)])
            p.op("vector", lambda e: e.reciprocal(denE[:], denE[:]),
                 reads=[("den",)], writes=[("den",)])
            p.op("vector", lambda e: e.scalar_tensor_tensor(
                cwE[:], eallE[:], denE[:], maskE[:], op0=ALU.mult, op1=ALU.mult),
                reads=[("eall",), ("den",), ("maskE",)], writes=[("cwE",)])
            p.op("tensor", lambda e: e.transpose(psE[0:8, 512:512 + 128], cwE[:],
                                                 identf_sb[:]),
                 reads=[("cwE",), KC], writes=[KE])
            p.op("vector", lambda e, st=st: e.tensor_copy(
                cwTE[:, st * 128:(st + 1) * 128], psE[0:8, 512:512 + 128]),
                reads=[KE], writes=[("cwT",)])
        p.dma("scalar", "e_cw", cw_d[:, t0:t0 + LG], cwTE[:],
              reads=[("cwT",)], writes=[("d_cw",)])

        for ex in range(8):
            p.dma("sync", "e_cwrow", cwrow[:], cw_d[ex:ex + 1, t0:t0 + LG],
                  reads=[("d_cw",)], writes=[("cwrow",)])
            for (n0, n1) in LSL:
                p.op("tensor", lambda e, n0=n0, n1=n1: e.matmul(
                    psE[:, n0:n1], ones_row_sb[:], cwrow[0:1, n0:n1],
                    start=True, stop=True), reads=[("cwrow",), KC], writes=[KE])
                p.op("vector", lambda e, n0=n0, n1=n1: e.tensor_copy(
                    cwb[:, n0:n1], psE[:, n0:n1]), reads=[KE], writes=[("cwb",)])
            for mt in range(16):
                par = mt % 2
                ph0, kh0 = (psH0, KH0) if par == 0 else (psW, KW_)
                ph1, kh1 = (psH1, KH1) if par == 0 else (psE, KE)
                p.dma("gpsimd", f"e_wi0_{par}", wiA[par][:],
                      wi0_r[ex, mt].rearrange("d k m -> d (k m)"), writes=[("wiA", par)])
                p.dma("gpsimd", f"e_wi1_{par}", wiB[par][:],
                      wi1_r[ex, mt].rearrange("d k m -> d (k m)"), writes=[("wiB", par)])
                for kt in range(16):
                    for (n0, n1) in LSL:
                        p.op("tensor", lambda e, par=par, kt=kt, n0=n0, n1=n1, ph0=ph0:
                             e.matmul(ph0[:, n0:n1], wiA[par][:, kt * 128:(kt + 1) * 128],
                                      ln2[:, kt * LG + n0:kt * LG + n1],
                                      start=(kt == 0), stop=(kt == 15)),
                             reads=[("wiA", par), KLN2[kt]], writes=[kh0])
                        p.op("tensor", lambda e, par=par, kt=kt, n0=n0, n1=n1, ph1=ph1:
                             e.matmul(ph1[:, n0:n1], wiB[par][:, kt * 128:(kt + 1) * 128],
                                      ln2[:, kt * LG + n0:kt * LG + n1],
                                      start=(kt == 0), stop=(kt == 15)),
                             reads=[("wiB", par), KLN2[kt]], writes=[kh1])
                usbp, kusb = (usb, ("usb", 0)) if par == 0 else (usb2, ("usb", 1))
                ssbp, kssb = (ssb, ("ssb", 0)) if par == 0 else (ssb2, ("ssb", 1))
                p.op("vector", lambda e, ph1=ph1, usbp=usbp: e.tensor_tensor(
                    usbp[:], ph1[:, 0:LG], cwb[:], op=ALU.mult),
                    reads=[kh1, ("cwb",)], writes=[kusb])
                p.op("scalar", lambda e, ph0=ph0, ssbp=ssbp: e.activation(
                    ssbp[:], ph0[:, 0:LG], AF.Silu),
                    reads=[kh0], writes=[kssb])
                p.op("vector", lambda e, mt=mt, usbp=usbp, ssbp=ssbp: e.tensor_tensor(
                    hsb[:, mt * LG:(mt + 1) * LG], usbp[:], ssbp[:], op=ALU.mult),
                    reads=[kusb, kssb], writes=[("hsb", mt)])
            for dt in range(16):
                par = dt % 2
                pw, kw2 = (psH0, KH0) if par == 0 else (psW, KW_)
                p.dma("gpsimd", f"e_wo_{par}", woE[par][:],
                      wo_r[ex, dt].rearrange("m t d -> m (t d)"), writes=[("woE", par)])
                for mt in range(16):
                    for (n0, n1) in LSL:
                        p.op("tensor", lambda e, par=par, mt=mt, n0=n0, n1=n1, pw=pw:
                             e.matmul(pw[:, n0:n1], woE[par][:, mt * 128:(mt + 1) * 128],
                                      hsb[:, mt * LG + n0:mt * LG + n1],
                                      start=(mt == 0), stop=(mt == 15)),
                             reads=[("woE", par), ("hsb", mt)], writes=[kw2])
                for (n0, n1) in LSL:
                    p.op("vector", lambda e, dt=dt, n0=n0, n1=n1, pw=pw: e.tensor_tensor(
                        x2acc[:, dt * LG + n0:dt * LG + n1],
                        x2acc[:, dt * LG + n0:dt * LG + n1], pw[:, n0:n1], op=ALU.add),
                        reads=[("x2a", dt), kw2], writes=[("x2a", dt)])
        for dt in range(16):
            p.dma("scalar", "e_out", out_d[dt][:, t0:t0 + LG],
                  x2acc[:, dt * LG:(dt + 1) * LG],
                  reads=[("x2a", dt)], writes=[("d_out",)])

    p.fence("gpsimd", [("d_out",)])
    p.build()
    return nc


def kernel(**inputs):
    T = 2048
    in_maps = host_layouts(inputs, T)
    nc = build_program(T)
    res = run_bass_kernel_spmd(nc, in_maps, core_ids=list(range(8)))
    out = np.zeros((4, 4096, 2048), np.float32)
    for core in range(8):
        b, half = core // 2, core % 2
        o = np.asarray(res.results[core]["out_fm"])
        out[b, half * T:(half + 1) * T, :] = o.reshape(2048, T).T
    return out
